# version 18
# baseline (speedup 1.0000x reference)
import numpy as np
import concourse.bass as bass
import concourse.mybir as mybir
from concourse.bass_utils import run_bass_kernel_spmd

F32 = mybir.dt.float32
BF16 = mybir.dt.bfloat16
AF = mybir.ActivationFunctionType
Alu = mybir.AluOpType
AX = mybir.AxisListType

D = 1024
SEQ = 2048
E = 8
HD = 256
KCAP = 1024
NT = 16
NDT = 8
NCH = 4
TPC = 4
NIT = 24
NSNAP = 4
NEWT0 = 8
NEWT = 2
EPS_LN = 1e-5
EPS_DEN = 1e-6


class Ctr:
    __slots__ = ("h", "v", "name")

    def __init__(self, h, name):
        self.h = h
        self.v = 0
        self.name = name

    def inc(self, n=1):
        self.v += n
        return self.v


class Plan:
    def __init__(self):
        self.ops = {"t": [], "v": [], "s": [], "g": [], "y": []}
        self.waited = {k: {} for k in self.ops}

    def q(self, e, build, waits=(), inc=None, n=1):
        todo = []
        for c, val in waits:
            if val and val > 0 and self.waited[e].get(c.name, 0) < val:
                todo.append((c.h, val))
                self.waited[e][c.name] = val

        def f(eng, todo=todo, build=build, inc=inc, n=n):
            for h, val in todo:
                eng.wait_ge(h, val)
            ins = build(eng)
            if inc is not None and ins is not None:
                ins.then_inc(inc.h, n)

        self.ops[e].append(f)
        return inc.inc(n) if inc is not None else None


def build_nc(stage=3):
    nc = bass.Bass("TRN2", target_bir_lowering=False,
                   dynamic_dma_scratch_size=2048)

    x_d = nc.dram_tensor("x", [SEQ, D], F32, kind="ExternalInput").ap()
    wqkv_d = nc.dram_tensor("wqkv", [D, 3 * D], F32, kind="ExternalInput").ap()
    bqkv_d = nc.dram_tensor("bqkv", [3 * D], F32, kind="ExternalInput").ap()
    wproj_d = nc.dram_tensor("wproj", [D, D], F32, kind="ExternalInput").ap()
    bproj_d = nc.dram_tensor("bproj", [D], F32, kind="ExternalInput").ap()
    gam_d = nc.dram_tensor("gam", [1, 1], F32, kind="ExternalInput").ap()
    wg_d = nc.dram_tensor("wg", [D, E], F32, kind="ExternalInput").ap()
    wgb_d = nc.dram_tensor("wgb", [E], F32, kind="ExternalInput").ap()
    w1_d = nc.dram_tensor("w1", [E, D, HD], F32, kind="ExternalInput").ap()
    b1_d = nc.dram_tensor("b1", [E, HD], F32, kind="ExternalInput").ap()
    w2_d = nc.dram_tensor("w2", [E, HD, D], F32, kind="ExternalInput").ap()
    b2_d = nc.dram_tensor("b2", [E, D], F32, kind="ExternalInput").ap()
    y_d = nc.dram_tensor("y", [SEQ, D], F32, kind="ExternalOutput").ap()
    if stage in (1, 2):
        xln_d = nc.dram_tensor("xln", [SEQ, D], F32, kind="ExternalOutput").ap()
    if stage == 2:
        sel_d = nc.dram_tensor("selo", [E, SEQ], F32, kind="ExternalOutput").ap()
    cc_in = nc.dram_tensor("cc_in", [128, 128], F32, kind="Internal")
    mb_d = nc.dram_tensor("mb_d", [8, 2048], F32, kind="Internal")
    cc_out = nc.dram_tensor("cc_out", [128, 128], F32, kind="Internal")

    A = nc.alloc_sbuf_tensor
    AT = nc.alloc_sbuf_tensor_at

    def addr(h):
        return nc.lookup_mloc(h).addr

    t1f = A("t1f", [128, NT * D], F32)
    kTs = A("kTs", [128, 16384], BF16)
    vNs = A("vNs", [128, 16384], BF16)
    qTs = A("qTs", [128, 16384], BF16)
    scs = A("scs", [128, 16384], BF16)
    wps = A("wps", [128, 8192], BF16)
    stg = A("stg", [128, 1024], F32)

    sc0, q0, k0, v0, wp0 = addr(scs), addr(qTs), addr(kTs), addr(vNs), addr(wps)
    stg0 = addr(stg)
    rowstg = AT("rowstg", [1, 512], F32, offset=stg0)
    xT = AT("xT", [128, 16384], BF16, offset=sc0)
    pT = AT("pT", [128, 8192], BF16, offset=sc0)
    pnat = AT("pnat", [128, 4096], BF16, offset=sc0 + 16384)
    junk1024 = AT("jk", [128, 1024], F32, offset=sc0 + 24576)
    xb16 = AT("xb16", [128, 1024], BF16, offset=sc0 + 28672)
    stgB = AT("stgB", [128, 1024], F32, offset=sc0 + 24576)
    xb16b = AT("xb16b", [128, 1024], BF16, offset=sc0 + 30720)
    selgt = AT("selgt", [8, 2048], F32, offset=sc0)
    rchb = AT("rchb", [128, 128], F32, offset=sc0 + 28672)
    sjk128 = AT("sjk128", [128, 128], F32, offset=sc0 + 29184)
    bmt8 = AT("bmt8", [8, 128], F32, offset=sc0 + 29696)
    cnt128 = AT("cnt128", [128, 1], F32, offset=sc0 + 30208)
    bmrT = AT("bmrT", [128, 8], F32, offset=sc0 + 30240)
    tau128s = AT("tau128s", [128, 1], F32, offset=sc0 + 30272)
    pm128 = AT("pm128", [128, 1], F32, offset=sc0 + 30304)
    rowm8 = AT("rowm8", [1, 8], F32, offset=sc0 + 30336)
    fct = AT("fct", [8, 64], F32, offset=sc0 + 30368)
    selba = AT("selba", [128, 4096], BF16, offset=sc0)
    hbuf = AT("hbuf", [128, 8192], BF16, offset=sc0 + 8192)
    aoT = [AT("aoT%d" % c, [128, 4096], BF16, offset=q0 + c * 8192) for c in range(NCH)]
    xlnT = AT("xlnT", [128, 16384], BF16, offset=q0)
    w1bf = AT("w1bf", [128, 16384], BF16, offset=k0)
    w2bf = AT("w2bf", [128, 16384], BF16, offset=v0)
    routeT = AT("routeT", [8, 2048], F32, offset=wp0)
    seleq = AT("seleq", [8, 2048], F32, offset=wp0 + 8192)
    selbf = AT("selbf", [8, 2048], BF16, offset=wp0)
    b2bf = AT("b2bf", [8, 1024], BF16, offset=wp0 + 4096)
    esel = AT("esel", [8, 1024], BF16, offset=wp0 + 6144)
    b2stg = AT("b2stg", [8, 512], F32, offset=wp0 + 8192)

    identbf = A("identbf", [128, 128], BF16)
    identf = A("identf", [128, 128], F32)
    decayD = A("decayD", [128, 128], F32)
    onesf = A("onesf", [1, 128], F32)
    onesbf = A("onesbf", [1, 128], BF16)
    rowbf = A("rowbf", [1, 2048], BF16)
    bq32c = A("bq32c", [128, 8], F32)
    bkc = A("bkc", [128, 8], F32)
    b1c = A("b1c", [128, 16], F32)
    wgbb = A("wgbb", [128, 8], F32)
    wgstg = A("wgstg", [128, 64], F32)
    wgbf = A("wgbf", [128, 64], BF16)
    gall = A("gall", [128, 128], F32)
    gscr = A("gscr", [128, 128], F32)
    maskd = A("maskd", [128, 128], F32)
    denom = A("denom", [128, 128], F32)
    mcol = A("mcol", [128, 1], F32)
    rall = A("rall", [128, 16], F32)
    rq = A("rq", [128, 8], F32)
    invr = A("invr", [128, 16], F32)
    mu1 = A("mu1", [128, 16], F32)
    s21 = A("s21", [128, 16], F32)
    inv1 = A("inv1", [128, 16], F32)
    mu2 = A("mu2", [128, 16], F32)
    s22 = A("s22", [128, 16], F32)
    inv2 = A("inv2", [128, 16], F32)
    lntmp = A("lntmp", [128, 16], F32)
    lntmp2 = A("lntmp2", [128, 16], F32)
    nb1 = A("nb1", [128, 16], F32)
    gsm = A("gsm", [1, 8], F32)
    glc = A("glc", [128, 4], F32)
    st8 = A("st8", [8, 24], F32)
    m1c = A("m1c", [128, 16], F32)
    m2c = A("m2c", [128, 16], F32)
    r8c = A("r8c", [128, 16], F32)
    stm = A("stm", [8, 2], mybir.dt.uint32)
    qmx = A("qmx", [128, 4], F32)
    assert nc.sbuf_bytes_remaining >= 0

    ps_s = nc.alloc_psum_tensor("ps_s", [128, 2048], F32)
    ps_ba = nc.alloc_psum_tensor("ps_ba", [128, 1024], BF16)
    ps_bb = nc.alloc_psum_tensor("ps_bb", [128, 1024], BF16)
    pr_a = nc.alloc_psum_tensor("pr_a", [128, 512], F32)
    pr_b = nc.alloc_psum_tensor("pr_b", [128, 512], F32)
    assert nc.psum_banks_remaining == 0

    dY = Ctr(nc.alloc_semaphore("dY"), "dY")
    dC = Ctr(nc.alloc_semaphore("dC"), "dC")
    dX = Ctr(nc.alloc_semaphore("dX"), "dX")
    dB = Ctr(nc.alloc_semaphore("dB"), "dB")
    dG = Ctr(nc.alloc_semaphore("dG"), "dG")
    dS = Ctr(nc.alloc_semaphore("dS"), "dS")
    sT = Ctr(nc.alloc_semaphore("sT"), "sT")
    sV = Ctr(nc.alloc_semaphore("sV"), "sV")
    sS = Ctr(nc.alloc_semaphore("sS"), "sS")
    sG = Ctr(nc.alloc_semaphore("sG"), "sG")
    sC = Ctr(nc.alloc_semaphore("sC"), "sC")

    P = Plan()
    bankV = {}
    bankS = {}

    def bw(key):
        return [(sV, bankV.get(key, 0)), (sS, bankS.get(key, 0))]

    def dmaw(qe, ctr, out, in_, waits=()):
        return P.q(qe, lambda e, o=out, i=in_: e.dma_start(out=o, in_=i),
                   waits=waits, inc=ctr, n=16)

    P.q("g", lambda e: e.memset(identbf[:, :], 0.0), inc=sG)
    P.q("g", lambda e: e.affine_select(out=identbf[:, :], in_=identbf[:, :],
                                       pattern=[[-1, 128]], compare_op=Alu.not_equal,
                                       fill=1.0, base=0, channel_multiplier=1),
        waits=[(sG, sG.v)], inc=sG)
    P.q("g", lambda e: e.memset(identf[:, :], 0.0), inc=sG)
    P.q("g", lambda e: e.affine_select(out=identf[:, :], in_=identf[:, :],
                                       pattern=[[-1, 128]], compare_op=Alu.not_equal,
                                       fill=1.0, base=0, channel_multiplier=1),
        waits=[(sG, sG.v)], inc=sG)
    P.q("g", lambda e: e.iota(decayD[:, :], pattern=[[-1, 128]], base=0,
                              channel_multiplier=1,
                              allow_small_or_imprecise_dtypes=True), inc=sG)
    vg_iota = sG.v
    P.q("g", lambda e: e.memset(onesf[:, :], 1.0), inc=sG)
    P.q("g", lambda e: e.memset(onesbf[:, :], 1.0), inc=sG)
    vg_ones = sG.v

    v_gam = dmaw("y", dC, gsm[0:1, 0:1], gam_d[:, :], waits=[(dC, dC.v)])
    v_abs = P.q("s", lambda e: e.activation(gsm[0:1, 1:2], gsm[0:1, 0:1], AF.Abs),
                waits=[(dC, v_gam)], inc=sS)
    v_cl = P.q("v", lambda e: e.tensor_scalar_max(gsm[0:1, 1:2], gsm[0:1, 1:2], 1e-30),
               waits=[(sS, v_abs)], inc=sV)
    P.q("s", lambda e: e.activation(gsm[0:1, 2:3], gsm[0:1, 1:2], AF.Ln),
        waits=[(sV, v_cl)], inc=sS)
    v_lng = P.q("s", lambda e: e.activation(gsm[0:1, 3:4], gsm[0:1, 0:1], AF.Sign),
                waits=[(sV, v_cl)], inc=sS)
    v_glmm = P.q("t", lambda e: e.matmul(pr_a[:, 0:2], onesf[0:1, :], gsm[0:1, 2:4],
                                         start=True, stop=True),
                 waits=[(sS, v_lng), (sG, vg_ones)], inc=sT)
    v_glc = P.q("v", lambda e: e.tensor_copy(glc[:, 0:2], pr_a[:, 0:2]),
                waits=[(sT, v_glmm)], inc=sV)
    bankV["pa"] = v_glc
    P.q("v", lambda e: e.tensor_scalar(gscr[:, :], decayD[:, :], glc[:, 0:1], 88.0,
                                       Alu.mult, Alu.min),
        waits=[(sV, v_glc), (sG, vg_iota)], inc=sV)
    P.q("v", lambda e: e.memset(rowstg[0:1, 0:128], 1.0), waits=[(sV, sV.v)], inc=sV)
    v_rrow = P.q("v", lambda e: e.memset(
        rowstg[0:1, 0:128].rearrange("p (a b) -> p a b", b=2)[:, :, 1:2], -1.0),
        waits=[(sV, sV.v)], inc=sV)
    v_pmm = P.q("t", lambda e: e.matmul(pr_a[:, 0:128], rowstg[0:1, 0:128],
                                        rowstg[0:1, 0:128], start=True, stop=True),
                waits=[(sV, v_rrow)] + bw("pa"), inc=sT)
    last_row_use = (sT, v_pmm)
    v_gev = P.q("v", lambda e: e.tensor_copy(maskd[:, :], pr_a[:, 0:128]),
                waits=[(sT, v_pmm)], inc=sV)
    bankV["pa"] = v_gev
    P.q("v", lambda e: e.tensor_scalar(glc[:, 2:3], glc[:, 1:2], 0.5, 0.5, Alu.mult, Alu.add),
        waits=[(sV, v_glc)], inc=sV)
    P.q("v", lambda e: e.tensor_scalar(glc[:, 3:4], glc[:, 1:2], -0.5, 0.5, Alu.mult, Alu.add),
        waits=[(sV, sV.v)], inc=sV)
    v_fac = P.q("v", lambda e: e.tensor_scalar(maskd[:, :], maskd[:, :], glc[:, 3:4],
                                               glc[:, 2:3], Alu.mult, Alu.add),
                waits=[(sV, sV.v), (sV, v_gev)], inc=sV)
    v_dexp = P.q("s", lambda e: e.activation(decayD[:, :], gscr[:, :], AF.Exp),
                 waits=[(sV, v_fac)], inc=sS)
    v_dfac = P.q("v", lambda e: e.tensor_mul(decayD[:, :], decayD[:, :], maskd[:, :]),
                 waits=[(sS, v_dexp)], inc=sV)
    vg_decay = P.q("g", lambda e: e.affine_select(out=decayD[:, :], in_=decayD[:, :],
                                                  pattern=[[-1, 128]], compare_op=Alu.is_ge,
                                                  fill=0.0, base=0, channel_multiplier=1),
                   waits=[(sV, v_dfac)], inc=sG)


    def row_load(src_ap):
        nonlocal last_row_use
        c, v = last_row_use
        return dmaw("y", dC, rowstg[0:1, 0:src_ap.shape[-1]], src_ap,
                    waits=[(c, v), (dC, dC.v)])

    def col_block(dst, base_col, v_row, scale=None):
        nonlocal last_row_use
        for j in range(4):
            P.q("t", lambda e, j_=j: e.matmul(
                pr_b[:, base_col + j_:base_col + j_ + 1],
                rowstg[0:1, j_ * 128:(j_ + 1) * 128], onesf[0:1, 0:1],
                start=True, stop=True),
                waits=[(dC, v_row), (sG, vg_ones)] + bw("pb"),
                inc=(sT if j == 3 else None))
        v_mm = sT.v
        last_row_use = (sT, v_mm)
        if scale is None:
            v_ev = P.q("v", lambda e: e.tensor_copy(dst[:, base_col:base_col + 4],
                                                    pr_b[:, base_col:base_col + 4]),
                       waits=[(sT, v_mm)], inc=sV)
        else:
            v_ev = P.q("v", lambda e, s_=scale: e.tensor_scalar(
                dst[:, base_col:base_col + 4], pr_b[:, base_col:base_col + 4],
                s_, None, Alu.mult),
                waits=[(sT, v_mm)], inc=sV)
        bankV["pb"] = v_ev
        return v_ev

    for half in range(2):
        v_r = row_load(bqkv_d[half * 512:(half + 1) * 512][None, :])
        col_block(bq32c, half * 4, v_r, scale=1.0 / 32.0)
    for half in range(2):
        v_r = row_load(bqkv_d[1024 + half * 512:1024 + (half + 1) * 512][None, :])
        col_block(bkc, half * 4, v_r)
    for half in range(2):
        v_r = row_load(bqkv_d[2048 + half * 512:2048 + (half + 1) * 512][None, :])
        v_c = P.q("v", lambda e, h=half: e.tensor_copy(
            rowbf[0:1, h * 512:(h + 1) * 512], rowstg[0:1, :]),
            waits=[(dC, v_r)], inc=sV)
        last_row_use = (sV, v_c)
    for half in range(2):
        v_r = row_load(bproj_d[half * 512:(half + 1) * 512][None, :])
        v_c = P.q("v", lambda e, h=half: e.tensor_copy(
            rowbf[0:1, 1024 + h * 512:1024 + (h + 1) * 512], rowstg[0:1, :]),
            waits=[(dC, v_r)], inc=sV)
        last_row_use = (sV, v_c)
    for quarter in range(4):
        v_r = row_load(b1_d.rearrange("e h -> (e h)")[quarter * 512:(quarter + 1) * 512][None, :])
        col_block(b1c, quarter * 4, v_r)
    v_r = row_load(wgb_d[:][None, :])
    v_mm = P.q("t", lambda e: e.matmul(pr_b[:, 16:24], onesf[0:1, :], rowstg[0:1, 0:8],
                                       start=True, stop=True),
               waits=[(dC, v_r)] + bw("pb"), inc=sT)
    last_row_use = (sT, v_mm)
    v_ev = P.q("v", lambda e: e.tensor_copy(wgbb[:, :], pr_b[:, 16:24]),
               waits=[(sT, v_mm)], inc=sV)
    bankV["pb"] = v_ev
    v_wg = dmaw("y", dC, wgstg[:, :].rearrange("p (t e) -> p t e", e=8),
                wg_d.rearrange("(t p) e -> p t e", p=128),
                waits=[(dC, dC.v)])
    v_consts = P.q("v", lambda e: e.tensor_copy(wgbf[:, :], wgstg[:, :]),
                   waits=[(dC, v_wg)], inc=sV)

    x_re = x_d.rearrange("(t p) d -> p t d", p=128)
    t1_v = t1f[:, :].rearrange("p (t d) -> p t d", d=D)
    xT_v = xT[:, :].rearrange("p (dt s) -> p dt s", s=SEQ)
    wp_v = wps[:, :].rearrange("p (t f) -> p t f", f=1024)
    v_xc = []
    for c in range(NCH):
        v_xc.append(dmaw("y", dX, t1_v[:, 4 * c:4 * c + 4, :],
                         x_re[:, 4 * c:4 * c + 4, :], waits=[(dX, dX.v)]))

    v_xTc = [0] * NCH
    for tt in range(NT):
        for grp in range(2):
            key, bank = (("pa", pr_a) if (tt * 2 + grp) % 2 == 0 else ("pb", pr_b))
            for k in range(4):
                ds = grp * 4 + k
                P.q("t", lambda e, b=bank, k_=k, t_=tt, d_=ds: e.transpose(
                    b[:, k_ * 128:(k_ + 1) * 128],
                    t1f[:, t_ * D + d_ * 128: t_ * D + (d_ + 1) * 128],
                    identf[:, :]),
                    waits=[(dX, v_xc[tt // 4]), (sG, vg_decay)] + bw(key),
                    inc=(sT if k == 3 else None))
            v_tp = sT.v
            v_ev = P.q("v", lambda e, b=bank, g_=grp, t_=tt: e.tensor_copy(
                xT_v[:, g_ * 4:(g_ + 1) * 4, t_ * 128:(t_ + 1) * 128],
                b[:, :].rearrange("p (k m) -> p k m", m=128)),
                waits=[(sT, v_tp)], inc=sV)
            bankV[key] = v_ev
        if tt % 4 == 3:
            v_xTc[tt // 4] = sV.v
    v_xT_done = sV.v

    stg_half = [(sV, 0)]
    w_re = wqkv_d.rearrange("(t p) f -> p t f", p=128)
    wproj_re = wproj_d.rearrange("(t p) f -> p t f", p=128)
    stg3 = stg[:, :].rearrange("p (t f) -> p t f", f=128)

    def load_w_piece(col0, j, extra_waits=(), cast_waits=()):
        c, v = stg_half[0]
        if col0 >= 10000:
            srcap = wproj_re[:, :, j * 128:(j + 1) * 128]
        else:
            srcap = w_re[:, :, col0 + j * 128:col0 + (j + 1) * 128]
        v_l = dmaw("y", dY, stg3[:, :, :], srcap,
                   waits=[(c, v)] + list(extra_waits))
        v_c = P.q("v", lambda e, j_=j: e.tensor_copy(
            wp_v[:, :, j_ * 128:(j_ + 1) * 128], stg3[:, :, :]),
            waits=[(dY, v_l)] + list(cast_waits), inc=sV)
        stg_half[0] = (sV, v_c)
        return v_c

    qT_v = qTs[:, :].rearrange("p (c ft s) -> p c ft s", ft=8, s=512)
    v_qmm_ft = []
    v_wq = [load_w_piece(0, 0, [last_row_use])]
    for ft in range(8):
        if ft < 7:
            v_wq.append(load_w_piece(0, ft + 1))
        for t in range(NDT):
            for c in range(NCH):
                P.q("t", lambda e, f_=ft, t_=t, c_=c: e.matmul(
                    ps_s[:, c_ * 512:(c_ + 1) * 512],
                    wp_v[:, t_, f_ * 128:(f_ + 1) * 128],
                    xT_v[:, t_, c_ * 512:(c_ + 1) * 512],
                    start=(t_ == 0), stop=(t_ == NDT - 1)),
                    waits=[(sV, max(v_wq[ft], v_xTc[c]))] + bw("s%d" % c),
                    inc=(sT if (t == NDT - 1 and c == NCH - 1) else None))
        v_mm = sT.v
        v_qmm_ft.append(v_mm)
        for c in range(NCH):
            v_e = P.q("s", lambda e, f_=ft, c_=c: e.activation(
                qT_v[:, c_, f_, :], ps_s[:, c_ * 512:(c_ + 1) * 512], AF.Identity,
                bias=bq32c[:, f_:f_ + 1], scale=1.0 / 32.0),
                waits=[(sT, v_mm), (sV, v_consts)], inc=sS)
            bankS["s%d" % c] = v_e
    v_qT_done = sS.v

    kT_v = kTs[:, :].rearrange("p (ft s) -> p ft s", s=SEQ)
    v_kmm_ft = []
    v_wk = [load_w_piece(1024, 0, cast_waits=[(sT, v_qmm_ft[0])])]
    for ft in range(8):
        if ft < 7:
            v_wk.append(load_w_piece(1024, ft + 1, cast_waits=[(sT, v_qmm_ft[ft + 1])]))
        for t in range(NDT):
            for c in range(NCH):
                P.q("t", lambda e, f_=ft, t_=t, c_=c: e.matmul(
                    ps_s[:, c_ * 512:(c_ + 1) * 512],
                    wp_v[:, t_, f_ * 128:(f_ + 1) * 128],
                    xT_v[:, t_, c_ * 512:(c_ + 1) * 512],
                    start=(t_ == 0), stop=(t_ == NDT - 1)),
                    waits=[(sV, v_wk[ft])] + bw("s%d" % c),
                    inc=(sT if (t == NDT - 1 and c == NCH - 1) else None))
        v_mm = sT.v
        v_kmm_ft.append(v_mm)
        for c in range(NCH):
            v_e = P.q("s", lambda e, f_=ft, c_=c: e.activation(
                kT_v[:, f_, c_ * 512:(c_ + 1) * 512],
                ps_s[:, c_ * 512:(c_ + 1) * 512], AF.Identity,
                bias=bkc[:, f_:f_ + 1], scale=1.0),
                waits=[(sT, v_mm)], inc=sS)
            bankS["s%d" % c] = v_e
    v_kT_done = sS.v

    vN_v = vNs[:, :].rearrange("p (kt f) -> p kt f", f=1024)
    v_vmm_h = []
    v_wv = [load_w_piece(2048, 0, cast_waits=[(sT, v_kmm_ft[0])])]
    for j in range(1, 4):
        v_wv.append(load_w_piece(2048, j, cast_waits=[(sT, v_kmm_ft[j])]))
    for h in range(2):
        vh = v_wv[4 * h + 3]
        for tt in range(NT):
            if h == 0 and tt < 4:
                v_wv.append(load_w_piece(2048, 4 + tt, cast_waits=[(sT, v_kmm_ft[4 + tt])]))
            lo = (2 * tt + h) % 4
            for t in range(NDT):
                P.q("t", lambda e, t_=t, tt_=tt, h_=h, l_=lo, v_=vh: e.matmul(
                    ps_s[:, l_ * 512:(l_ + 1) * 512],
                    xT_v[:, t_, tt_ * 128:(tt_ + 1) * 128],
                    wp_v[:, t_, h_ * 512:(h_ + 1) * 512],
                    start=(t_ == 0), stop=False),
                    waits=[(sV, max(vh, v_xT_done, v_consts))] + bw("s%d" % lo),
                    inc=None)
            P.q("t", lambda e, h_=h, l_=lo: e.matmul(
                ps_s[:, l_ * 512:(l_ + 1) * 512],
                onesbf[0:1, :], rowbf[0:1, h_ * 512:(h_ + 1) * 512],
                start=False, stop=True), inc=sT)
            v_mm = sT.v
            v_ev = P.q("v", lambda e, tt_=tt, h_=h, l_=lo: e.tensor_copy(
                vN_v[:, tt_, h_ * 512:(h_ + 1) * 512],
                ps_s[:, l_ * 512:(l_ + 1) * 512]),
                waits=[(sT, v_mm)], inc=sV)
            bankV["s%d" % lo] = v_ev
        v_vmm_h.append(sT.v)
    v_v_done = sV.v
    v_vmm_done = sT.v

    v_wproj = 0
    for j in range(8):
        v_wproj = load_w_piece(10000, j, cast_waits=[(sT, v_vmm_h[j // 4])])

    bgV = []

    def drain_bg(n):
        k = 0
        while bgV and k < n:
            bgV.pop(0)()
            k += 1

    xlncm = xlnT[:, :].rearrange("p (c dt s) -> p c dt s", dt=8, s=512)
    wgbf_v = wgbf[:, :].rearrange("p (dt e) -> p dt e", e=8)
    v_xlncm = [0] * NCH
    v_masked_c = [0] * NCH

    def emit_gate(c):
        sl = slice(4 * c, 4 * c + 4)
        P.q("v", lambda e: e.tensor_scalar(mu1[:, sl], mu1[:, sl], 1.0 / D, None, Alu.mult),
            waits=[(sS, s_sums[c]), (sV, sV.v)], inc=sV)

        def op(f):
            P.q("v", f, waits=[(sV, sV.v)], inc=sV)

        op(lambda e: e.tensor_scalar(lntmp[:, sl], s21[:, sl], 1.0 / D, None, Alu.mult))
        op(lambda e: e.tensor_mul(lntmp2[:, sl], mu1[:, sl], mu1[:, sl]))
        op(lambda e: e.tensor_sub(lntmp[:, sl], lntmp[:, sl], lntmp2[:, sl]))
        op(lambda e: e.tensor_scalar_add(lntmp[:, sl], lntmp[:, sl], EPS_LN))
        op(lambda e: e.reciprocal(lntmp[:, sl], lntmp[:, sl]))
        op(lambda e: e.tensor_scalar(inv1[:, sl], lntmp[:, sl], 0.5, 0.5, Alu.mult, Alu.add))
        for _ in range(8):
            op(lambda e: e.reciprocal(lntmp2[:, sl], inv1[:, sl]))
            op(lambda e: e.tensor_mul(lntmp2[:, sl], lntmp[:, sl], lntmp2[:, sl]))
            op(lambda e: e.tensor_add(inv1[:, sl], inv1[:, sl], lntmp2[:, sl]))
            op(lambda e: e.tensor_scalar(inv1[:, sl], inv1[:, sl], 0.5, None, Alu.mult))
        v_fin = P.q("v", lambda e: e.scalar_tensor_tensor(
            nb1[:, sl], mu1[:, sl], -1.0, inv1[:, sl], Alu.mult, Alu.mult),
            waits=[(sV, sV.v)], inc=sV)
        for j in range(TPC):
            i = 4 * c + j
            xb = xb16 if j % 2 == 0 else xb16b
            s_app = P.q("s", lambda e, i_=i: e.activation(
                t1_v[:, i_, :], t1_v[:, i_, :], AF.Identity,
                bias=nb1[:, i_:i_ + 1], scale=inv1[:, i_:i_ + 1]),
                waits=[(sV, v_fin), (sS, sS.v)], inc=sS)
            v_cast = P.q("s", lambda e, i_=i, x_=xb: e.activation(x_[:, :], t1_v[:, i_, :], AF.Copy),
                         waits=[(sS, s_app), (sV, bankV.get("xb%d" % (j % 2), 0))], inc=sS)
            for k in range(8):
                P.q("t", lambda e, k_=k, x_=xb: e.transpose(
                    ps_ba[:, k_ * 128:(k_ + 1) * 128],
                    x_[:, k_ * 128:(k_ + 1) * 128],
                    identbf[:, :]),
                    waits=[(sS, v_cast)] + bw("ba"),
                    inc=(sT if k == 7 else None))
            v_tp = sT.v
            v_ev = P.q("v", lambda e, c_=c, j_=j: e.tensor_copy(
                xlncm[:, c_, 0:8, j_ * 128:(j_ + 1) * 128],
                ps_ba[:, :].rearrange("p (k m) -> p k m", m=128)),
                waits=[(sT, v_tp)], inc=sV)
            bankV["ba"] = v_ev
            bankV["xb%d" % (j % 2)] = v_ev
            v_xlncm_t = v_ev
            key, bank = (("pa", pr_a) if j % 2 == 0 else ("pb", pr_b))
            for dt in range(NDT):
                P.q("t", lambda e, b=bank, c_=c, j_=j, dt_=dt: e.matmul(
                    b[:, 0:8],
                    xlncm[:, c_, dt_, j_ * 128:(j_ + 1) * 128],
                    wgbf_v[:, dt_, :],
                    start=(dt_ == 0), stop=(dt_ == NDT - 1)),
                    waits=[(sV, max(v_xlncm_t, v_consts))] + bw(key),
                    inc=(sT if dt == NDT - 1 else None))
            v_mm = sT.v
            v_ev = P.q("v", lambda e, b=bank, i_=i: e.tensor_add(
                gall[:, i_ * 8:(i_ + 1) * 8], b[:, 0:8], wgbb[:, :]),
                waits=[(sT, v_mm)], inc=sV)
            bankV[key] = v_ev
            sl8 = slice(i * 8, (i + 1) * 8)

            def opb(f):
                bgV.append(lambda f=f: P.q("v", f, waits=[(sV, sV.v)], inc=sV))

            opb(lambda e, s_=sl8, i_=i: e.reduce_max(m1c[:, i_:i_ + 1], gall[:, s_], axis=AX.X))
            opb(lambda e, s_=sl8, i_=i: e.tensor_scalar(
                gscr[:, s_], gall[:, s_], m1c[:, i_:i_ + 1], None, Alu.subtract))

            def gexp(s_=sl8):
                v_gs = sV.v
                v_ge = P.q("s", lambda e, s2=s_: e.activation(gscr[:, s2], gscr[:, s2], AF.Exp),
                           waits=[(sV, v_gs)], inc=sS)
                P.q("v", lambda e, s2=s_: e.tensor_copy(gscr[:, s2], gscr[:, s2]),
                    waits=[(sS, v_ge)], inc=sV)

            bgV.append(gexp)
            opb(lambda e, s_=sl8, i_=i: e.reduce_sum(r8c[:, i_:i_ + 1], gscr[:, s_], axis=AX.X))
            opb(lambda e, i_=i: e.reciprocal(r8c[:, i_:i_ + 1], r8c[:, i_:i_ + 1]))
            opb(lambda e, s_=sl8, i_=i: e.tensor_scalar(
                gscr[:, s_], gscr[:, s_], r8c[:, i_:i_ + 1], None, Alu.mult))
            opb(lambda e, s_=sl8, i_=i: e.tensor_scalar(
                maskd[:, s_], gall[:, s_], m1c[:, i_:i_ + 1], -1e30, Alu.is_equal, Alu.mult))
            opb(lambda e, s_=sl8: e.tensor_add(maskd[:, s_], maskd[:, s_], gall[:, s_]))
            opb(lambda e, s_=sl8, i_=i: e.reduce_max(m2c[:, i_:i_ + 1], maskd[:, s_], axis=AX.X))
            opb(lambda e, s_=sl8, i_=i: e.tensor_scalar(
                maskd[:, s_], gall[:, s_], m2c[:, i_:i_ + 1], None, Alu.is_ge))

            def lastm(s_=sl8, c_=c):
                v_masked_c[c_] = P.q("v", lambda e, s2=s_: e.tensor_mul(
                    maskd[:, s2], maskd[:, s2], gscr[:, s2]),
                    waits=[(sV, sV.v)], inc=sV)

            bgV.append(lastm)

        v_xlncm[c] = sV.v

    s_sums = [0] * NCH
    pT_v = pT[:, :].rearrange("p (kt s) -> p kt s", s=512)
    v_exp = [0] * NT
    v_exp_p = [[0] * 4 for _ in range(NT)]
    v_rall = [0] * NT
    v_ptev = [0] * NT
    last_exp = 0

    def emit_ptp(i, pb):
        j = i % TPC
        for grp in range(2):
            key, bank = (("ba", ps_ba) if grp == 0 else ("bb", ps_bb))
            for k in range(8):
                kt = grp * 8 + k
                P.q("t", lambda e, b=bank, k_=k, kt_=kt, pb_=pb: e.transpose(
                    b[:, k_ * 128:(k_ + 1) * 128],
                    pnat[:, pb_ * 2048 + kt_ * 128: pb_ * 2048 + (kt_ + 1) * 128],
                    identbf[:, :]),
                    waits=[(sS, v_exp_p[i][2 * grp + 1])] + bw(key),
                    inc=(sT if k == 7 else None))
            v_tp = sT.v
            v_ev = P.q("s", lambda e, b=bank, g_=grp, j_=j: e.activation(
                pT_v[:, g_ * 8:(g_ + 1) * 8, j_ * 128:(j_ + 1) * 128],
                b[:, :].rearrange("p (k m) -> p k m", m=128), AF.Copy),
                waits=[(sT, v_tp)], inc=sS)
            bankS[key] = v_ev
        v_ptev[i] = sS.v
        drain_bg(24)

    for c in range(NCH):
        pend = None
        for j in range(TPC):
            i = c * TPC + j
            kc_d, off_d = divmod(i * 128, 512)
            for kc in range(4):
                for ft in range(8):
                    P.q("t", lambda e, kc_=kc, f_=ft, c_=c, j_=j: e.matmul(
                        ps_s[:, kc_ * 512:(kc_ + 1) * 512],
                        qT_v[:, c_, f_, j_ * 128:(j_ + 1) * 128],
                        kT_v[:, f_, kc_ * 512:(kc_ + 1) * 512],
                        start=(f_ == 0), stop=(f_ == 7)),
                        waits=[(sS, max(v_qT_done, v_kT_done))] + bw("s%d" % kc),
                        inc=(sT if ft == 7 else None))
                v_qmm = sT.v
                if kc == kc_d:
                    P.q("v", lambda e, kc_=kc_d, o_=off_d: e.tensor_mul(
                        ps_s[:, kc_ * 512 + o_:kc_ * 512 + o_ + 128],
                        ps_s[:, kc_ * 512 + o_:kc_ * 512 + o_ + 128],
                        decayD[:, :]),
                        waits=[(sT, v_qmm), (sG, vg_decay)], inc=sV)
                P.q("v", lambda e, kc_=kc: e.reduce_max(
                    qmx[:, kc_:kc_ + 1], ps_s[:, kc_ * 512:(kc_ + 1) * 512], axis=AX.X),
                    waits=[(sT, v_qmm), (sV, sV.v)], inc=sV)
            v_smm = sT.v
            v_rmax = P.q("v", lambda e: e.reduce_max(mcol[:, :], qmx[:, :],
                                                     axis=AX.X, negate=True),
                         waits=[(sV, sV.v)], inc=sV)
            pb = j % 2
            rb = (i % 2) * 4
            for kc in range(4):
                vp = P.q("s", lambda e, pb_=pb, kc_=kc, b_=rb: e.activation(
                    pnat[:, pb_ * 2048 + kc_ * 512:pb_ * 2048 + (kc_ + 1) * 512],
                    ps_s[:, kc_ * 512:(kc_ + 1) * 512], AF.Exp,
                    bias=mcol[:, :], scale=1.0,
                    accum_out=(None if kc_ == 0 else rq[:, b_ + kc_:b_ + kc_ + 1])),
                    waits=[(sV, v_rmax)], inc=sS)
                bankS["s%d" % kc] = vp
                v_exp_p[i][kc] = vp
            v_exp[i] = v_exp_p[i][3]
            last_exp = v_exp[i]
            P.q("v", lambda e, b_=rb, pb_=pb: e.reduce_sum(
                rq[:, b_:b_ + 1], pnat[:, pb_ * 2048:pb_ * 2048 + 512], axis=AX.X),
                waits=[(sS, v_exp_p[i][0]), (sV, sV.v)], inc=sV)
            P.q("v", lambda e, b_=rb: e.tensor_add(
                rq[:, b_:b_ + 1], rq[:, b_:b_ + 1], rq[:, b_ + 1:b_ + 2]),
                waits=[(sS, v_exp_p[i][1]), (sV, sV.v)], inc=sV)
            P.q("v", lambda e, b_=rb: e.tensor_add(
                rq[:, b_ + 2:b_ + 3], rq[:, b_ + 2:b_ + 3], rq[:, b_ + 3:b_ + 4]),
                waits=[(sS, v_exp_p[i][3])], inc=sV)
            v_rall[i] = P.q("v", lambda e, i_=i, b_=rb: e.tensor_add(
                rall[:, i_:i_ + 1], rq[:, b_:b_ + 1], rq[:, b_ + 2:b_ + 3]),
                waits=[(sV, sV.v)], inc=sV)
            P.q("v", lambda e, i_=i: e.reciprocal(invr[:, i_:i_ + 1], rall[:, i_:i_ + 1]),
                waits=[(sV, v_rall[i])], inc=sV)
            if pend is not None:
                emit_ptp(*pend)
            pend = (i, pb)
        emit_ptp(*pend)
        v_pt_all = v_ptev[c * TPC + TPC - 1]

        for ds in range(8):
            lo = ds % 2
            for kt in range(NT):
                P.q("t", lambda e, ds_=ds, kt_=kt, l_=lo: e.matmul(
                    ps_s[:, l_ * 512:(l_ + 1) * 512],
                    vN_v[:, kt_, ds_ * 128:(ds_ + 1) * 128],
                    pT_v[:, kt_, :],
                    start=(kt_ == 0), stop=(kt_ == NT - 1)),
                    waits=[(sS, max(last_exp, v_pt_all)), (sV, v_v_done)] + bw("s%d" % lo),
                    inc=(sT if kt == NT - 1 else None))
            v_mm = sT.v
            v_ev = P.q("v", lambda e, ds_=ds, c_=c, l_=lo: e.tensor_copy(
                aoT[c_][:, :].rearrange("p (ds s) -> p ds s", s=512)[:, ds_, :],
                ps_s[:, l_ * 512:(l_ + 1) * 512]),
                waits=[(sT, v_mm)], inc=sV)
            bankV["s%d" % lo] = v_ev
        v_ao_done = sV.v
        v_pv_last = sT.v

        for j in range(TPC):
            P.q("t", lambda e, i_=c * TPC + j, j_=j: e.matmul(
                pr_a[0:1, j_ * 128:(j_ + 1) * 128], rall[:, i_:i_ + 1], identf[:, :],
                start=True, stop=True),
                waits=[(sV, v_rall[c * TPC + j])] + bw("pa"),
                inc=(sT if j == TPC - 1 else None))
        v_rr = sT.v
        v_rrbf = P.q("v", lambda e: e.tensor_copy(rowbf[0:1, 0:512], pr_a[0:1, 0:512]),
                     waits=[(sT, v_rr)], inc=sV)
        bankV["pa"] = v_rrbf
        for j in range(TPC):
            i = c * TPC + j
            for h in range(2):
                key, bank = (("pa", pr_a) if h == 0 else ("pb", pr_b))
                for ds in range(8):
                    P.q("t", lambda e, b=bank, ds_=ds, c_=c, j_=j, h_=h: e.matmul(
                        b[:, :],
                        aoT[c_][:, :].rearrange("p (ds s) -> p ds s", s=512)[:, ds_, j_ * 128:(j_ + 1) * 128],
                        wp_v[:, ds_, h_ * 512:(h_ + 1) * 512],
                        start=(ds_ == 0), stop=False),
                        waits=[(sV, max(v_ao_done, v_wproj, v_rrbf))] + bw(key),
                        inc=None)
                P.q("t", lambda e, b=bank, h_=h, j_=j: e.matmul(
                    b[:, :], rowbf[0:1, j_ * 128:(j_ + 1) * 128],
                    rowbf[0:1, 1024 + h_ * 512:1024 + (h_ + 1) * 512],
                    start=False, stop=True), inc=sT)
                v_mm = sT.v
                v_ev = P.q("v", lambda e, b=bank, i_=i, h_=h: e.scalar_tensor_tensor(
                    t1_v[:, i_, h_ * 512:(h_ + 1) * 512], b[:, :], invr[:, i_:i_ + 1],
                    t1_v[:, i_, h_ * 512:(h_ + 1) * 512], Alu.mult, Alu.add),
                    waits=[(sT, v_mm), (dX, v_xc[c])], inc=sV)
                bankV[key] = v_ev
            v_sum_s = P.q("s", lambda e, i_=i: e.activation(
                junk1024[:, :], t1_v[:, i_, :], AF.Identity, bias=0.0, scale=1.0,
                accum_out=mu1[:, i_:i_ + 1]),
                waits=[(sV, sV.v), (sS, sS.v)], inc=sS)
            s_sums[c] = P.q("s", lambda e, i_=i: e.activation(
                junk1024[:, :], t1_v[:, i_, :], AF.Square,
                accum_out=s21[:, i_:i_ + 1]),
                waits=[(sS, v_sum_s)], inc=sS)
        if c >= 1:
            drain_bg(1000000)
            emit_gate(c - 1)
    drain_bg(1000000)
    emit_gate(NCH - 1)
    drain_bg(1000000)
    v_ln1_sums = sV.v
    v_masked = v_masked_c[NCH - 1]

    if stage == 1:
        v_out = 0
        for c in range(NCH):
            v_out = dmaw("s", dS, xln_d.rearrange("(t p) d -> p t d", p=128)[:, 4 * c:4 * c + 4, :],
                         t1_v[:, 4 * c:4 * c + 4, :], waits=[(sV, v_masked)])
        return finalize(nc, P, [(dS, v_out)])

    v_ccw = dmaw("g", dG, cc_in.ap(), maskd[:, :], waits=[(sV, v_masked)])
    v_cc = P.q("g", lambda e: e.collective_compute(
        "AllReduce", Alu.add, replica_groups=[list(range(8))],
        ins=[cc_in.ap().opt()], outs=[cc_out.ap().opt()]),
        waits=[(dG, v_ccw)], inc=sC)
    w1_re = w1_d.rearrange("e (t p) h -> p e t h", p=128)
    w1bf_v = w1bf[:, :].rearrange("p (e t h) -> p e t h", t=8, h=256)
    stg_ab = [stg_half[0], (sS, s_sums[NCH - 1])]
    bufs = [stg[:, :], stgB[:, :]]
    pidx = 0
    w1cast = {}
    w2_re = w2_d.rearrange("e (t p) f -> p e t f", p=128)
    w2bf_v = w2bf[:, :].rearrange("p (e t f) -> p e t f", t=2, f=1024)
    for ei in range(E):
        for t in range(2):
            ab = pidx % 2
            dma_ctr = dY if ab == 0 else dB
            c, v = stg_ab[ab]
            v_l = dmaw("y", dma_ctr, bufs[ab], w2_re[:, ei, t, :],
                       waits=[(c, v)])
            v_c = P.q("s", lambda e, e_=ei, t_=t, a_=ab: e.activation(
                w2bf_v[:, e_, t_, :], bufs[a_], AF.Copy),
                waits=[(dma_ctr, v_l), (sT, v_pv_last)], inc=sS)
            stg_ab[ab] = (sS, v_c)
            pidx += 1
    v_w2_done = sS.v
    for ei in range(E):
        for hh in range(2):
            ab = pidx % 2
            dma_ctr = dY if ab == 0 else dB
            c, v = stg_ab[ab]
            v_l = dmaw("y", dma_ctr, bufs[ab].rearrange("p (t k) -> p t k", k=256),
                       w1_re[:, ei, hh * 4:(hh + 1) * 4, :],
                       waits=[(c, v)])
            v_c = P.q("s", lambda e, e_=ei, hh_=hh, a_=ab: e.activation(
                w1bf_v[:, e_, hh_ * 4:(hh_ + 1) * 4, :].rearrange("p t k -> p (t k)"),
                bufs[a_], AF.Copy),
                waits=[(dma_ctr, v_l), (sS, sS.v)], inc=sS)
            stg_ab[ab] = (sS, v_c)
            w1cast[ei] = v_c
            pidx += 1
    v_w1_done = sS.v

    v_ccr = dmaw("g", dG, denom[:, :], cc_out.ap(), waits=[(sC, v_cc)])
    P.q("v", lambda e: e.tensor_scalar_add(denom[:, :], denom[:, :], EPS_DEN),
        waits=[(dG, v_ccr)], inc=sV)
    P.q("v", lambda e: e.reciprocal(denom[:, :], denom[:, :]),
        waits=[(sV, sV.v)], inc=sV)
    v_route = P.q("v", lambda e: e.tensor_mul(gall[:, :], maskd[:, :], denom[:, :]),
                  waits=[(sV, sV.v)], inc=sV)

    selba_v = selba[:, :].rearrange("p (e s) -> p e s", s=512)
    hbuf_v = hbuf[:, :].rearrange("p (e t s) -> p e t s", t=2, s=512)
    w2state = {"prev_mm": 0}
    gelus = {}

    def emit_w1(cc, hook=None):
        for ei in range(E):
            for ht in range(2):
                lo = (ei * 2 + ht) % 4
                for dt in range(NDT):
                    P.q("t", lambda e, l_=lo, e_=ei, t_=dt, h_=ht, c_=cc: e.matmul(
                        ps_s[:, l_ * 512:(l_ + 1) * 512],
                        w1bf_v[:, e_, t_, h_ * 128:(h_ + 1) * 128],
                        xlncm[:, c_, t_, :],
                        start=(t_ == 0), stop=(t_ == NDT - 1)),
                        waits=[(sS, w1cast[ei]),
                               (sV, v_xlncm[cc])] + bw("s%d" % lo),
                        inc=(sT if dt == NDT - 1 else None))
                v_mm = sT.v
                v_gelu = P.q("s", lambda e, l_=lo, e_=ei, h_=ht: e.activation(
                    hbuf_v[:, e_, h_, :], ps_s[:, l_ * 512:(l_ + 1) * 512],
                    AF.Gelu, bias=b1c[:, e_ * 2 + h_:e_ * 2 + h_ + 1]),
                    waits=[(sT, v_mm), (sT, w2state["prev_mm"]),
                           (sV, bankV.get("hb%d%d" % (ei, ht), 0))], inc=sS)
                bankS["s%d" % lo] = v_gelu
                gelus[(ei, ht)] = v_gelu
                if hook is not None:
                    hook(ei * 2 + ht)

    def emit_rowT(src, wait_v):
        for grp in range(4):
            key, bank = (("pa", pr_a) if grp % 2 == 0 else ("pb", pr_b))
            for k in range(4):
                i = grp * 4 + k
                P.q("t", lambda e, b=bank, k_=k, i_=i, s_=src: e.transpose(
                    b[0:8, k_ * 128:(k_ + 1) * 128],
                    s_[:, i_ * 8:(i_ + 1) * 8],
                    identf[:, :]),
                    waits=[(sV, wait_v)] + bw(key),
                    inc=(sT if k == 3 else None))
            v_tp = sT.v
            v_ev = P.q("v", lambda e, b=bank, g_=grp: e.tensor_copy(
                routeT[0:8, :].rearrange("p (t m) -> p t m", m=128)[:, g_ * 4:(g_ + 1) * 4, :],
                b[0:8, :].rearrange("p (k m) -> p k m", m=128)),
                waits=[(sT, v_tp)], inc=sV)
            bankV[key] = v_ev
        return sV.v

    Cc = lambda k: st8[0:8, k:k + 1]

    def vq(build, waits=()):
        return P.q("v", build, waits=waits, inc=sV)

    v_maskT = emit_rowT(maskd, v_masked)
    v_mb_w = dmaw("s", dC, mb_d.ap(), routeT[0:8, :],
                  waits=[(sV, v_maskT), (dC, dC.v)])
    v_mb = dmaw("s", dC, rchb[:, :],
                mb_d.ap().rearrange("e (s f) -> (e s) f", f=128),
                waits=[(dC, v_mb_w)])
    vg_b0 = P.q("g", lambda e: e.memset(bmt8[:, :], 0.0),
                waits=[(sG, sG.v), (sS, sS.v)], inc=sG)
    vg_bm = P.q("g", lambda e: e.affine_select(
        out=bmt8[:, :], in_=bmt8[:, :], pattern=[[-1, 8], [0, 16]],
        compare_op=Alu.not_equal, fill=1.0, base=0, channel_multiplier=1),
        waits=[(sG, vg_b0)], inc=sG)
    v_bmm = P.q("t", lambda e: e.transpose(pr_a[:, 0:8], bmt8[:, :], identf[0:8, 0:8]),
                waits=[(sG, vg_bm)] + bw("pa"), inc=sT)
    v_bmr = P.q("v", lambda e: e.tensor_copy(bmrT[:, :], pr_a[:, 0:8]),
                waits=[(sT, v_bmm)], inc=sV)
    bankV["pa"] = v_bmr

    def pcol_reduce(dstcol, rop):
        v_tr = P.q("t", lambda e: e.matmul(pr_a[0:1, 0:128], pm128[:, :], identf[:, :],
                                           start=True, stop=True),
                   waits=[(sV, sV.v)] + bw("pa"), inc=sT)
        v_r8 = vq(lambda e, r_=rop: e.tensor_reduce(
            rowm8[0:1, :], pr_a[0:1, 0:128].rearrange("p (g s) -> p g s", s=16),
            AX.X, r_), waits=[(sT, v_tr)])
        bankV["pa"] = v_r8
        v_cm = P.q("t", lambda e: e.matmul(pr_b[0:8, 0:1], rowm8[0:1, :], onesf[0:1, 0:1],
                                           start=True, stop=True),
                   waits=[(sV, v_r8)] + bw("pb"), inc=sT)
        v_m8 = vq(lambda e, d_=dstcol: e.tensor_copy(Cc(d_), pr_b[0:8, 0:1]),
                  waits=[(sT, v_cm)])
        bankV["pb"] = v_m8

    vq(lambda e: e.tensor_scalar(sjk128[:, :], rchb[:, :], 0.0, 1e30,
                                 Alu.is_equal, Alu.mult),
       waits=[(dC, v_mb), (sV, sV.v)])
    vq(lambda e: e.tensor_add(sjk128[:, :], sjk128[:, :], rchb[:, :]), waits=[(sV, sV.v)])
    vq(lambda e: e.tensor_reduce(pm128[:, :], sjk128[:, :], AX.X, Alu.min), waits=[(sV, sV.v)])
    pcol_reduce(9, Alu.min)
    vq(lambda e: e.reduce_max(pm128[:, :], rchb[:, :], axis=AX.X), waits=[(sV, sV.v)])
    pcol_reduce(8, Alu.max)
    vq(lambda e: e.tensor_scalar(sjk128[:, :], rchb[:, :], 0.0, None, Alu.is_gt, Alu.add,
                                 accum_out=cnt128[:, :]), waits=[(sV, sV.v)])
    v_rd = P.q("t", lambda e: e.matmul(pr_a[0:8, 0:1], bmrT[:, :], cnt128[:, :],
                                       start=True, stop=True),
               waits=[(sV, sV.v)] + bw("pa"), inc=sT)
    v_c8 = vq(lambda e: e.tensor_copy(Cc(4), pr_a[0:8, 0:1]), waits=[(sT, v_rd)])
    bankV["pa"] = v_c8
    vq(lambda e: e.tensor_scalar(Cc(0), Cc(9), 0.25, None, Alu.mult), waits=[(sV, sV.v)])
    vq(lambda e: e.tensor_scalar(stm[0:8, 0:1], Cc(0), 0.0, None, Alu.is_equal), waits=[(sV, sV.v)])
    vq(lambda e: e.copy_predicated(Cc(0), stm[0:8, 0:1], Cc(9)), waits=[(sV, sV.v)])
    vq(lambda e: e.tensor_scalar(Cc(1), Cc(8), 1.001e6, 1e-38, Alu.mult, Alu.max), waits=[(sV, sV.v)])
    v_rchb_free = sV.v

    Fc = lambda j: fct[0:8, j:j + 1]

    def SQCH(dst_ap, src_ap):
        sq = P.q("s", lambda e: e.activation(dst_ap, src_ap, AF.Sqrt),
                 waits=[(sV, sV.v)], inc=sS)
        vq(lambda e: e.reciprocal(Cc(11), dst_ap), waits=[(sS, sq)])
        vq(lambda e: e.tensor_mul(Cc(11), src_ap, Cc(11)), waits=[(sV, sV.v)])
        vq(lambda e: e.tensor_add(dst_ap, dst_ap, Cc(11)), waits=[(sV, sV.v)])
        vq(lambda e: e.tensor_scalar(dst_ap, dst_ap, 0.5, None, Alu.mult), waits=[(sV, sV.v)])

    SQCH(Fc(62), Cc(0))
    SQCH(Fc(63), Cc(1))
    vq(lambda e: e.reciprocal(Fc(61), Fc(62)), waits=[(sV, sV.v)])
    vq(lambda e: e.tensor_mul(Fc(1), Fc(63), Fc(61)), waits=[(sV, sV.v)])
    JS = 14
    for j in range(2, JS + 1):
        SQCH(Fc(j), Fc(j - 1))
    vq(lambda e: e.tensor_scalar_add(Fc(60), Fc(JS), -1.0), waits=[(sV, sV.v)])
    for j in range(JS + 1, NIT + 1):
        vq(lambda e: e.tensor_mul(Cc(11), Fc(60), Fc(60)), waits=[(sV, sV.v)])
        vq(lambda e: e.tensor_scalar(Fc(60), Fc(60), 0.5, None, Alu.mult), waits=[(sV, sV.v)])
        vq(lambda e: e.tensor_scalar(Cc(11), Cc(11), -0.125, None, Alu.mult), waits=[(sV, sV.v)])
        vq(lambda e: e.tensor_add(Fc(60), Fc(60), Cc(11)), waits=[(sV, sV.v)])
        vq(lambda e, j_=j: e.tensor_scalar_add(Fc(j_), Fc(60), 1.0), waits=[(sV, sV.v)])
    for j in range(1, NIT + 1):
        vq(lambda e, j_=j: e.reciprocal(Fc(32 + j_), Fc(j_)), waits=[(sV, sV.v)])
    v_t1 = vq(lambda e: e.tensor_mul(Cc(2), Cc(0), Fc(1)), waits=[(sV, sV.v)])

    v_routeT = emit_rowT(gall, v_route)
    v_rb_w = dmaw("s", dC, cc_in.ap().rearrange("(e s) f -> e (s f)", s=16),
                  routeT[0:8, :], waits=[(sV, v_routeT), (sC, v_cc), (dC, dC.v)])
    v_rb = dmaw("s", dC, rchb[:, :], cc_in.ap(),
                waits=[(dC, v_rb_w), (sV, v_rchb_free)])

    def count_at(col):
        v_bc = P.q("t", lambda e, c_=col: e.matmul(
            pr_b[:, 0:1], bmt8[:, :], Cc(c_), start=True, stop=True),
            waits=[(sV, sV.v)] + bw("pb"), inc=sT)
        v_cn = P.q("v", lambda e: e.tensor_scalar(
            sjk128[:, :], rchb[:, :], pr_b[:, 0:1], None, Alu.is_ge, Alu.add,
            accum_out=cnt128[:, :]),
            waits=[(sT, v_bc), (dC, v_rb)], inc=sV)
        bankV["pb"] = v_cn
        v_rd = P.q("t", lambda e: e.matmul(
            pr_a[0:8, 0:1], bmrT[:, :], cnt128[:, :], start=True, stop=True),
            waits=[(sV, v_cn)] + bw("pa"), inc=sT)
        v_c8 = vq(lambda e: e.tensor_copy(Cc(3), pr_a[0:8, 0:1]), waits=[(sT, v_rd)])
        bankV["pa"] = v_c8

    def emit_iter(it):
        count_at(2)
        vq(lambda e: e.tensor_scalar(stm[0:8, 0:1], Cc(3), float(KCAP), None, Alu.is_ge), waits=[(sV, sV.v)])
        vq(lambda e: e.copy_predicated(Cc(0), stm[0:8, 0:1], Cc(2)), waits=[(sV, sV.v)])
        vq(lambda e: e.tensor_scalar(stm[0:8, 1:2], Cc(3), float(KCAP), None, Alu.is_lt), waits=[(sV, sV.v)])
        vq(lambda e: e.copy_predicated(Cc(1), stm[0:8, 1:2], Cc(2)), waits=[(sV, sV.v)])
        if it < NIT - 1:
            vq(lambda e, j_=it + 2: e.tensor_copy(Cc(6), Fc(32 + j_)), waits=[(sV, sV.v)])
            vq(lambda e, j_=it + 2: e.copy_predicated(Cc(6), stm[0:8, 0:1], Fc(j_)), waits=[(sV, sV.v)])
            vq(lambda e: e.tensor_mul(Cc(2), Cc(2), Cc(6)), waits=[(sV, sV.v)])

    it_state = {"next": 0}

    def w1hook(g):
        tgt = (g + 1) * NIT // 16
        while it_state["next"] < tgt:
            emit_iter(it_state["next"])
            it_state["next"] += 1

    emit_w1(0, hook=w1hook)
    while it_state["next"] < NIT:
        emit_iter(it_state["next"])
        it_state["next"] += 1
    for _sn in range(NSNAP):
        v_bc = P.q("t", lambda e: e.matmul(pr_b[:, 0:1], bmt8[:, :], Cc(1),
                                           start=True, stop=True),
                   waits=[(sV, sV.v)] + bw("pb"), inc=sT)
        v_hc = vq(lambda e: e.tensor_copy(tau128s[:, :], pr_b[:, 0:1]), waits=[(sT, v_bc)])
        bankV["pb"] = v_hc
        vq(lambda e: e.scalar_tensor_tensor(sjk128[:, :], rchb[:, :], tau128s[:, :],
                                            rchb[:, :], Alu.is_lt, Alu.mult),
           waits=[(sV, sV.v)])
        vq(lambda e: e.reduce_max(pm128[:, :], sjk128[:, :], axis=AX.X), waits=[(sV, sV.v)])
        v_tr = P.q("t", lambda e: e.matmul(pr_a[0:1, 0:128], pm128[:, :], identf[:, :],
                                           start=True, stop=True),
                   waits=[(sV, sV.v)] + bw("pa"), inc=sT)
        v_r8 = vq(lambda e: e.tensor_reduce(
            rowm8[0:1, :], pr_a[0:1, 0:128].rearrange("p (g s) -> p g s", s=16),
            AX.X, Alu.max), waits=[(sT, v_tr)])
        bankV["pa"] = v_r8
        v_cm = P.q("t", lambda e: e.matmul(pr_b[0:8, 0:1], rowm8[0:1, :], onesf[0:1, 0:1],
                                           start=True, stop=True),
                   waits=[(sV, v_r8)] + bw("pb"), inc=sT)
        v_m8 = vq(lambda e: e.tensor_copy(Cc(12), pr_b[0:8, 0:1]), waits=[(sT, v_cm)])
        bankV["pb"] = v_m8
        count_at(12)
        vq(lambda e: e.tensor_scalar(stm[0:8, 0:1], Cc(3), float(KCAP), None, Alu.is_lt), waits=[(sV, sV.v)])
        vq(lambda e: e.copy_predicated(Cc(1), stm[0:8, 0:1], Cc(12)), waits=[(sV, sV.v)])
        vq(lambda e: e.tensor_scalar(stm[0:8, 1:2], Cc(3), float(KCAP), None, Alu.is_ge), waits=[(sV, sV.v)])
        vq(lambda e: e.copy_predicated(Cc(0), stm[0:8, 1:2], Cc(12)), waits=[(sV, sV.v)])
    vq(lambda e: e.tensor_scalar(stm[0:8, 0:1], Cc(4), float(KCAP), None, Alu.is_ge), waits=[(sV, sV.v)])
    vq(lambda e: e.memset(Cc(13), 0.0), waits=[(sV, sV.v)])
    vq(lambda e: e.copy_predicated(Cc(13), stm[0:8, 0:1], Cc(0)), waits=[(sV, sV.v)])
    vq(lambda e: e.tensor_scalar(selgt[0:8, :], routeT[0:8, :], Cc(13), None,
                                 Alu.is_gt, Alu.add, accum_out=Cc(14)), waits=[(sV, sV.v)])
    vq(lambda e: e.tensor_scalar(seleq[0:8, :], routeT[0:8, :], Cc(13), None, Alu.is_equal),
       waits=[(sV, sV.v)])
    vq(lambda e: e.memset(routeT[0:8, :], 0.0), waits=[(sV, sV.v)])
    vq(lambda e: e.tensor_tensor_scan(routeT[0:8, :], seleq[0:8, :], routeT[0:8, :],
                                      0.0, Alu.add, Alu.add), waits=[(sV, sV.v)])
    vq(lambda e: e.tensor_scalar(Cc(5), Cc(14), -1.0, float(KCAP), Alu.mult, Alu.add),
       waits=[(sV, sV.v)])
    vq(lambda e: e.tensor_scalar(routeT[0:8, :], routeT[0:8, :], Cc(5), None, Alu.is_le),
       waits=[(sV, sV.v)])
    vq(lambda e: e.tensor_mul(seleq[0:8, :], seleq[0:8, :], routeT[0:8, :]), waits=[(sV, sV.v)])
    v_self32 = vq(lambda e: e.tensor_add(selgt[0:8, :], selgt[0:8, :], seleq[0:8, :]),
                  waits=[(sV, sV.v)])

    if stage == 2:
        v_o1 = dmaw("s", dS, sel_d[:, :], selgt[0:8, :], waits=[(sV, v_self32)])
        v_out = v_o1
        for c in range(NCH):
            v_out = dmaw("s", dS, xln_d.rearrange("(t p) d -> p t d", p=128)[:, 4 * c:4 * c + 4, :],
                         t1_v[:, 4 * c:4 * c + 4, :], waits=[(sV, v_self32)])
        return finalize(nc, P, [(dS, v_out)])

    v_selbf = vq(lambda e: e.tensor_copy(selbf[0:8, :], selgt[0:8, :]), waits=[(sV, v_self32)])
    v_b2 = 0
    for half in range(2):
        v_l = dmaw("y", dC, b2stg[0:8, :], b2_d[:, half * 512:(half + 1) * 512],
                   waits=[(sV, v_selbf), (dC, dC.v), (sS, v_b2)])
        v_b2 = P.q("s", lambda e, h=half: e.activation(
            b2bf[0:8, h * 512:(h + 1) * 512], b2stg[0:8, :], AF.Copy),
            waits=[(dC, v_l)], inc=sS)
    v_esel0 = P.q("g", lambda e: e.memset(esel[0:8, :], 0.0),
                  waits=[(sV, v_selbf)], inc=sG)
    v_esel = P.q("g", lambda e: e.affine_select(
        out=esel[0:8, :], in_=esel[0:8, :],
        pattern=[[-1, 8], [0, 128]], compare_op=Alu.not_equal,
        fill=1.0, base=0, channel_multiplier=1),
        waits=[(sG, v_esel0)], inc=sG)

    v_ln2_sums = 0
    for cc in range(NCH):
        if cc > 0:
            emit_w1(cc)
        v_selba = 0
        for ei in range(E):
            key, bank = (("pa", pr_a) if ei % 2 == 0 else ("pb", pr_b))
            v_mm = P.q("t", lambda e, b=bank, e_=ei, c_=cc: e.matmul(
                b[:, :], esel[0:8, e_ * 128:(e_ + 1) * 128],
                selbf[0:8, c_ * 512:(c_ + 1) * 512],
                start=True, stop=True),
                waits=[(sV, v_selbf), (sG, v_esel)] + bw(key), inc=sT)
            v_ev = P.q("v", lambda e, b=bank, e_=ei: e.tensor_copy(
                selba_v[:, e_, :], b[:, :]),
                waits=[(sT, v_mm)], inc=sV)
            bankV[key] = v_ev
            v_selba = v_ev
        for ei in range(E):
            for ht in range(2):
                v_hmask = P.q("v", lambda e, e_=ei, h_=ht: e.tensor_mul(
                    hbuf_v[:, e_, h_, :], hbuf_v[:, e_, h_, :], selba_v[:, e_, :]),
                    waits=[(sS, gelus[(ei, ht)]), (sV, v_selba)], inc=sV)
                bankV["hb%d%d" % (ei, ht)] = v_hmask
        v_h_done = sV.v
        for jt in range(TPC):
            i = cc * TPC + jt
            for h in range(2):
                key, bank = (("pa", pr_a) if h == 0 else ("pb", pr_b))
                for ei in range(E):
                    for ht in range(2):
                        P.q("t", lambda e, b=bank, e_=ei, h_=h, ht_=ht, j_=jt: e.matmul(
                            b[:, :],
                            hbuf_v[:, e_, ht_, j_ * 128:(j_ + 1) * 128],
                            w2bf_v[:, e_, ht_, h_ * 512:(h_ + 1) * 512],
                            start=(e_ == 0 and ht_ == 0), stop=False),
                            waits=[(sV, v_h_done), (sS, max(v_w2_done, v_b2))] + bw(key),
                            inc=None)
                P.q("t", lambda e, b=bank, i_=i, h_=h: e.matmul(
                    b[:, :], selbf[0:8, i_ * 128:(i_ + 1) * 128],
                    b2bf[0:8, h_ * 512:(h_ + 1) * 512],
                    start=False, stop=True), inc=sT)
                v_mm = sT.v
                w2state["prev_mm"] = v_mm
                v_ev = P.q("v", lambda e, b=bank, i_=i, h_=h: e.tensor_add(
                    t1_v[:, i_, h_ * 512:(h_ + 1) * 512], b[:, :],
                    t1_v[:, i_, h_ * 512:(h_ + 1) * 512]),
                    waits=[(sT, v_mm)], inc=sV)
                bankV[key] = v_ev
            P.q("v", lambda e, i_=i: e.reduce_sum(mu2[:, i_:i_ + 1], t1_v[:, i_, :], axis=AX.X),
                waits=[(sV, sV.v)], inc=sV)
            P.q("v", lambda e, i_=i: e.scalar_tensor_tensor(
                junk1024[:, :], t1_v[:, i_, :], 1.0, t1_v[:, i_, :],
                Alu.mult, Alu.mult, accum_out=s22[:, i_:i_ + 1]),
                waits=[(sV, sV.v)], inc=sV)
    v_ln2_sums = sV.v

    P.q("v", lambda e: e.tensor_scalar(mu2[:, :], mu2[:, :], 1.0 / D, None, Alu.mult),
        waits=[(sV, v_ln2_sums)], inc=sV)
    P.q("v", lambda e: e.tensor_scalar(lntmp[:, :], s22[:, :], 1.0 / D, None, Alu.mult),
        waits=[(sV, sV.v)], inc=sV)
    P.q("v", lambda e: e.tensor_mul(lntmp2[:, :], mu2[:, :], mu2[:, :]),
        waits=[(sV, sV.v)], inc=sV)
    P.q("v", lambda e: e.tensor_sub(lntmp[:, :], lntmp[:, :], lntmp2[:, :]),
        waits=[(sV, sV.v)], inc=sV)
    P.q("v", lambda e: e.tensor_scalar_add(lntmp[:, :], lntmp[:, :], EPS_LN),
        waits=[(sV, sV.v)], inc=sV)
    v_rc2 = P.q("v", lambda e: e.reciprocal(lntmp2[:, :], lntmp[:, :]),
                waits=[(sV, sV.v)], inc=sV)
    v_inv2 = P.q("s", lambda e: e.activation(inv2[:, :], lntmp2[:, :], AF.Sqrt),
                 waits=[(sV, v_rc2)], inc=sS)
    v_final = 0
    for i in range(NT):
        v_final = P.q("v", lambda e, i_=i: e.tensor_scalar(
            t1_v[:, i_, :], t1_v[:, i_, :], mu2[:, i_:i_ + 1], inv2[:, i_:i_ + 1],
            Alu.subtract, Alu.mult),
            waits=[(sS, v_inv2)], inc=sV)
    y_re = y_d.rearrange("(t p) d -> p t d", p=128)
    v_out = 0
    for c in range(NCH):
        v_out = dmaw("s", dS, y_re[:, 4 * c:4 * c + 4, :], t1_v[:, 4 * c:4 * c + 4, :],
                     waits=[(sV, v_final)])
    return finalize(nc, P, [(dS, v_out)])


def finalize(nc, P, final_waits):
    def fw(eng):
        for c, val in final_waits:
            eng.wait_ge(c.h, val)
    P.ops["y"].append(fw)
    with nc.Block() as block:
        @block.sync
        def _(eng):
            for f in P.ops["y"]:
                f(eng)

        @block.tensor
        def _(eng):
            for f in P.ops["t"]:
                f(eng)

        @block.vector
        def _(eng):
            for f in P.ops["v"]:
                f(eng)

        @block.scalar
        def _(eng):
            for f in P.ops["s"]:
                f(eng)

        @block.gpsimd
        def _(eng):
            for f in P.ops["g"]:
                f(eng)
    return nc


def make_in_maps(L0_attn_x, L0_attn_Wqkv, L0_attn_bqkv, L0_attn_Wproj, L0_attn_bproj,
                 L0_attn_gamma, L0_ffn_w_gate_w, L0_ffn_w_gate_b, L0_ffn_W1, L0_ffn_b1,
                 L0_ffn_W2, L0_ffn_b2):
    x = np.ascontiguousarray(np.asarray(L0_attn_x, dtype=np.float32))
    com = {
        "wqkv": np.ascontiguousarray(np.asarray(L0_attn_Wqkv, dtype=np.float32)),
        "bqkv": np.ascontiguousarray(np.asarray(L0_attn_bqkv, dtype=np.float32)),
        "wproj": np.ascontiguousarray(np.asarray(L0_attn_Wproj, dtype=np.float32)),
        "bproj": np.ascontiguousarray(np.asarray(L0_attn_bproj, dtype=np.float32)),
        "gam": np.asarray(L0_attn_gamma, dtype=np.float32).reshape(1, 1).copy(),
        "wg": np.ascontiguousarray(np.asarray(L0_ffn_w_gate_w, dtype=np.float32)),
        "wgb": np.ascontiguousarray(np.asarray(L0_ffn_w_gate_b, dtype=np.float32)),
        "w1": np.ascontiguousarray(np.asarray(L0_ffn_W1, dtype=np.float32)),
        "b1": np.ascontiguousarray(np.asarray(L0_ffn_b1, dtype=np.float32)),
        "w2": np.ascontiguousarray(np.asarray(L0_ffn_W2, dtype=np.float32)),
        "b2": np.ascontiguousarray(np.asarray(L0_ffn_b2, dtype=np.float32)),
    }
    return [dict(com, x=np.ascontiguousarray(x[b])) for b in range(8)]


def kernel(**inputs):
    nc = build_nc(stage=3)
    in_maps = make_in_maps(**inputs)
    res = run_bass_kernel_spmd(nc, in_maps, core_ids=list(range(8)))
    return np.stack([res.results[b]["y"] for b in range(8)], axis=0)


# version 19
# speedup vs baseline: 1.0882x; 1.0882x over previous
import numpy as np
import concourse.bass as bass
import concourse.mybir as mybir
from concourse.bass_utils import run_bass_kernel_spmd

F32 = mybir.dt.float32
BF16 = mybir.dt.bfloat16
AF = mybir.ActivationFunctionType
Alu = mybir.AluOpType
AX = mybir.AxisListType

D = 1024
SEQ = 2048
E = 8
HD = 256
KCAP = 1024
NT = 16
NDT = 8
NCH = 4
TPC = 4
NIT = 24
NSNAP = 4
NEWT0 = 8
NEWT = 2
EPS_LN = 1e-5
EPS_DEN = 1e-6


class Ctr:
    __slots__ = ("h", "v", "name")

    def __init__(self, h, name):
        self.h = h
        self.v = 0
        self.name = name

    def inc(self, n=1):
        self.v += n
        return self.v


class Plan:
    def __init__(self):
        self.ops = {"t": [], "v": [], "s": [], "g": [], "y": []}
        self.waited = {k: {} for k in self.ops}

    def q(self, e, build, waits=(), inc=None, n=1):
        todo = []
        for c, val in waits:
            if val and val > 0 and self.waited[e].get(c.name, 0) < val:
                todo.append((c.h, val))
                self.waited[e][c.name] = val

        def f(eng, todo=todo, build=build, inc=inc, n=n):
            for h, val in todo:
                eng.wait_ge(h, val)
            ins = build(eng)
            if inc is not None and ins is not None:
                ins.then_inc(inc.h, n)

        self.ops[e].append(f)
        return inc.inc(n) if inc is not None else None


def build_nc(stage=3):
    nc = bass.Bass("TRN2", target_bir_lowering=False,
                   dynamic_dma_scratch_size=2048)

    x_d = nc.dram_tensor("x", [SEQ, D], F32, kind="ExternalInput").ap()
    wqkv_d = nc.dram_tensor("wqkv", [D, 3 * D], F32, kind="ExternalInput").ap()
    bqkv_d = nc.dram_tensor("bqkv", [3 * D], F32, kind="ExternalInput").ap()
    wproj_d = nc.dram_tensor("wproj", [D, D], F32, kind="ExternalInput").ap()
    bproj_d = nc.dram_tensor("bproj", [D], F32, kind="ExternalInput").ap()
    gam_d = nc.dram_tensor("gam", [1, 1], F32, kind="ExternalInput").ap()
    wg_d = nc.dram_tensor("wg", [D, E], F32, kind="ExternalInput").ap()
    wgb_d = nc.dram_tensor("wgb", [E], F32, kind="ExternalInput").ap()
    w1_d = nc.dram_tensor("w1", [E, D, HD], F32, kind="ExternalInput").ap()
    b1_d = nc.dram_tensor("b1", [E, HD], F32, kind="ExternalInput").ap()
    w2_d = nc.dram_tensor("w2", [E, HD, D], F32, kind="ExternalInput").ap()
    b2_d = nc.dram_tensor("b2", [E, D], F32, kind="ExternalInput").ap()
    y_d = nc.dram_tensor("y", [SEQ, D], F32, kind="ExternalOutput").ap()
    if stage in (1, 2):
        xln_d = nc.dram_tensor("xln", [SEQ, D], F32, kind="ExternalOutput").ap()
    if stage == 2:
        sel_d = nc.dram_tensor("selo", [E, SEQ], F32, kind="ExternalOutput").ap()
    cc_in = nc.dram_tensor("cc_in", [128, 128], F32, kind="Internal")
    cc_out = nc.dram_tensor("cc_out", [128, 128], F32, kind="Internal")

    A = nc.alloc_sbuf_tensor
    AT = nc.alloc_sbuf_tensor_at

    def addr(h):
        return nc.lookup_mloc(h).addr

    t1f = A("t1f", [128, NT * D], F32)
    kTs = A("kTs", [128, 16384], BF16)
    vNs = A("vNs", [128, 16384], BF16)
    qTs = A("qTs", [128, 16384], BF16)
    scs = A("scs", [128, 16384], BF16)
    wps = A("wps", [128, 8192], BF16)
    stg = A("stg", [128, 1024], F32)

    sc0, q0, k0, v0, wp0 = addr(scs), addr(qTs), addr(kTs), addr(vNs), addr(wps)
    stg0 = addr(stg)
    rowstg = AT("rowstg", [1, 512], F32, offset=stg0)
    xT = AT("xT", [128, 16384], BF16, offset=sc0)
    pT = AT("pT", [128, 8192], BF16, offset=sc0)
    pnat = AT("pnat", [128, 4096], BF16, offset=sc0 + 16384)
    junk1024 = AT("jk", [128, 1024], F32, offset=sc0 + 24576)
    xb16 = AT("xb16", [128, 1024], BF16, offset=sc0 + 28672)
    stgB = AT("stgB", [128, 1024], F32, offset=sc0 + 24576)
    xb16b = AT("xb16b", [128, 1024], BF16, offset=sc0 + 30720)
    selgt = AT("selgt", [8, 2048], F32, offset=sc0)
    selba = AT("selba", [128, 4096], BF16, offset=sc0)
    hbuf = AT("hbuf", [128, 8192], BF16, offset=sc0 + 8192)
    aoT = [AT("aoT%d" % c, [128, 4096], BF16, offset=q0 + c * 8192) for c in range(NCH)]
    xlnT = AT("xlnT", [128, 16384], BF16, offset=q0)
    w1bf = AT("w1bf", [128, 16384], BF16, offset=k0)
    w2bf = AT("w2bf", [128, 16384], BF16, offset=v0)
    routeT = AT("routeT", [8, 2048], F32, offset=wp0)
    seleq = AT("seleq", [8, 2048], F32, offset=wp0 + 8192)
    selbf = AT("selbf", [8, 2048], BF16, offset=wp0)
    b2bf = AT("b2bf", [8, 1024], BF16, offset=wp0 + 4096)
    esel = AT("esel", [8, 1024], BF16, offset=wp0 + 6144)
    b2stg = AT("b2stg", [8, 512], F32, offset=wp0 + 8192)

    identbf = A("identbf", [128, 128], BF16)
    identf = A("identf", [128, 128], F32)
    decayD = A("decayD", [128, 128], F32)
    onesf = A("onesf", [1, 128], F32)
    onesbf = A("onesbf", [1, 128], BF16)
    rowbf = A("rowbf", [1, 2048], BF16)
    bq32c = A("bq32c", [128, 8], F32)
    bkc = A("bkc", [128, 8], F32)
    b1c = A("b1c", [128, 16], F32)
    wgbb = A("wgbb", [128, 8], F32)
    wgstg = A("wgstg", [128, 64], F32)
    wgbf = A("wgbf", [128, 64], BF16)
    gall = A("gall", [128, 128], F32)
    gscr = A("gscr", [128, 128], F32)
    maskd = A("maskd", [128, 128], F32)
    denom = A("denom", [128, 128], F32)
    mcol = A("mcol", [128, 1], F32)
    rall = A("rall", [128, 16], F32)
    rq = A("rq", [128, 8], F32)
    invr = A("invr", [128, 16], F32)
    mu1 = A("mu1", [128, 16], F32)
    s21 = A("s21", [128, 16], F32)
    inv1 = A("inv1", [128, 16], F32)
    mu2 = A("mu2", [128, 16], F32)
    s22 = A("s22", [128, 16], F32)
    inv2 = A("inv2", [128, 16], F32)
    lntmp = A("lntmp", [128, 16], F32)
    lntmp2 = A("lntmp2", [128, 16], F32)
    nb1 = A("nb1", [128, 16], F32)
    gsm = A("gsm", [1, 8], F32)
    glc = A("glc", [128, 4], F32)
    st8 = A("st8", [8, 24], F32)
    m1c = A("m1c", [128, 16], F32)
    m2c = A("m2c", [128, 16], F32)
    r8c = A("r8c", [128, 16], F32)
    stm = A("stm", [8, 2], mybir.dt.uint32)
    qmx = A("qmx", [128, 4], F32)
    assert nc.sbuf_bytes_remaining >= 0

    ps_s = nc.alloc_psum_tensor("ps_s", [128, 2048], F32)
    ps_ba = nc.alloc_psum_tensor("ps_ba", [128, 1024], BF16)
    ps_bb = nc.alloc_psum_tensor("ps_bb", [128, 1024], BF16)
    pr_a = nc.alloc_psum_tensor("pr_a", [128, 512], F32)
    pr_b = nc.alloc_psum_tensor("pr_b", [128, 512], F32)
    assert nc.psum_banks_remaining == 0

    dY = Ctr(nc.alloc_semaphore("dY"), "dY")
    dC = Ctr(nc.alloc_semaphore("dC"), "dC")
    dX = Ctr(nc.alloc_semaphore("dX"), "dX")
    dB = Ctr(nc.alloc_semaphore("dB"), "dB")
    dG = Ctr(nc.alloc_semaphore("dG"), "dG")
    dS = Ctr(nc.alloc_semaphore("dS"), "dS")
    sT = Ctr(nc.alloc_semaphore("sT"), "sT")
    sV = Ctr(nc.alloc_semaphore("sV"), "sV")
    sS = Ctr(nc.alloc_semaphore("sS"), "sS")
    sG = Ctr(nc.alloc_semaphore("sG"), "sG")
    sC = Ctr(nc.alloc_semaphore("sC"), "sC")

    P = Plan()
    bankV = {}
    bankS = {}

    def bw(key):
        return [(sV, bankV.get(key, 0)), (sS, bankS.get(key, 0))]

    def dmaw(qe, ctr, out, in_, waits=()):
        return P.q(qe, lambda e, o=out, i=in_: e.dma_start(out=o, in_=i),
                   waits=waits, inc=ctr, n=16)

    P.q("g", lambda e: e.memset(identbf[:, :], 0.0), inc=sG)
    P.q("g", lambda e: e.affine_select(out=identbf[:, :], in_=identbf[:, :],
                                       pattern=[[-1, 128]], compare_op=Alu.not_equal,
                                       fill=1.0, base=0, channel_multiplier=1),
        waits=[(sG, sG.v)], inc=sG)
    P.q("g", lambda e: e.memset(identf[:, :], 0.0), inc=sG)
    P.q("g", lambda e: e.affine_select(out=identf[:, :], in_=identf[:, :],
                                       pattern=[[-1, 128]], compare_op=Alu.not_equal,
                                       fill=1.0, base=0, channel_multiplier=1),
        waits=[(sG, sG.v)], inc=sG)
    P.q("g", lambda e: e.iota(decayD[:, :], pattern=[[-1, 128]], base=0,
                              channel_multiplier=1,
                              allow_small_or_imprecise_dtypes=True), inc=sG)
    vg_iota = sG.v
    P.q("g", lambda e: e.memset(onesf[:, :], 1.0), inc=sG)
    P.q("g", lambda e: e.memset(onesbf[:, :], 1.0), inc=sG)
    vg_ones = sG.v

    v_gam = dmaw("y", dC, gsm[0:1, 0:1], gam_d[:, :], waits=[(dC, dC.v)])
    v_abs = P.q("s", lambda e: e.activation(gsm[0:1, 1:2], gsm[0:1, 0:1], AF.Abs),
                waits=[(dC, v_gam)], inc=sS)
    v_cl = P.q("v", lambda e: e.tensor_scalar_max(gsm[0:1, 1:2], gsm[0:1, 1:2], 1e-30),
               waits=[(sS, v_abs)], inc=sV)
    P.q("s", lambda e: e.activation(gsm[0:1, 2:3], gsm[0:1, 1:2], AF.Ln),
        waits=[(sV, v_cl)], inc=sS)
    v_lng = P.q("s", lambda e: e.activation(gsm[0:1, 3:4], gsm[0:1, 0:1], AF.Sign),
                waits=[(sV, v_cl)], inc=sS)
    v_glmm = P.q("t", lambda e: e.matmul(pr_a[:, 0:2], onesf[0:1, :], gsm[0:1, 2:4],
                                         start=True, stop=True),
                 waits=[(sS, v_lng), (sG, vg_ones)], inc=sT)
    v_glc = P.q("v", lambda e: e.tensor_copy(glc[:, 0:2], pr_a[:, 0:2]),
                waits=[(sT, v_glmm)], inc=sV)
    bankV["pa"] = v_glc
    P.q("v", lambda e: e.tensor_scalar(gscr[:, :], decayD[:, :], glc[:, 0:1], 88.0,
                                       Alu.mult, Alu.min),
        waits=[(sV, v_glc), (sG, vg_iota)], inc=sV)
    P.q("v", lambda e: e.memset(rowstg[0:1, 0:128], 1.0), waits=[(sV, sV.v)], inc=sV)
    v_rrow = P.q("v", lambda e: e.memset(
        rowstg[0:1, 0:128].rearrange("p (a b) -> p a b", b=2)[:, :, 1:2], -1.0),
        waits=[(sV, sV.v)], inc=sV)
    v_pmm = P.q("t", lambda e: e.matmul(pr_a[:, 0:128], rowstg[0:1, 0:128],
                                        rowstg[0:1, 0:128], start=True, stop=True),
                waits=[(sV, v_rrow)] + bw("pa"), inc=sT)
    last_row_use = (sT, v_pmm)
    v_gev = P.q("v", lambda e: e.tensor_copy(maskd[:, :], pr_a[:, 0:128]),
                waits=[(sT, v_pmm)], inc=sV)
    bankV["pa"] = v_gev
    P.q("v", lambda e: e.tensor_scalar(glc[:, 2:3], glc[:, 1:2], 0.5, 0.5, Alu.mult, Alu.add),
        waits=[(sV, v_glc)], inc=sV)
    P.q("v", lambda e: e.tensor_scalar(glc[:, 3:4], glc[:, 1:2], -0.5, 0.5, Alu.mult, Alu.add),
        waits=[(sV, sV.v)], inc=sV)
    v_fac = P.q("v", lambda e: e.tensor_scalar(maskd[:, :], maskd[:, :], glc[:, 3:4],
                                               glc[:, 2:3], Alu.mult, Alu.add),
                waits=[(sV, sV.v), (sV, v_gev)], inc=sV)
    v_dexp = P.q("s", lambda e: e.activation(decayD[:, :], gscr[:, :], AF.Exp),
                 waits=[(sV, v_fac)], inc=sS)
    v_dfac = P.q("v", lambda e: e.tensor_mul(decayD[:, :], decayD[:, :], maskd[:, :]),
                 waits=[(sS, v_dexp)], inc=sV)
    vg_decay = P.q("g", lambda e: e.affine_select(out=decayD[:, :], in_=decayD[:, :],
                                                  pattern=[[-1, 128]], compare_op=Alu.is_ge,
                                                  fill=0.0, base=0, channel_multiplier=1),
                   waits=[(sV, v_dfac)], inc=sG)


    def row_load(src_ap):
        nonlocal last_row_use
        c, v = last_row_use
        return dmaw("y", dC, rowstg[0:1, 0:src_ap.shape[-1]], src_ap,
                    waits=[(c, v), (dC, dC.v)])

    def col_block(dst, base_col, v_row, scale=None):
        nonlocal last_row_use
        for j in range(4):
            P.q("t", lambda e, j_=j: e.matmul(
                pr_b[:, base_col + j_:base_col + j_ + 1],
                rowstg[0:1, j_ * 128:(j_ + 1) * 128], onesf[0:1, 0:1],
                start=True, stop=True),
                waits=[(dC, v_row), (sG, vg_ones)] + bw("pb"),
                inc=(sT if j == 3 else None))
        v_mm = sT.v
        last_row_use = (sT, v_mm)
        if scale is None:
            v_ev = P.q("v", lambda e: e.tensor_copy(dst[:, base_col:base_col + 4],
                                                    pr_b[:, base_col:base_col + 4]),
                       waits=[(sT, v_mm)], inc=sV)
        else:
            v_ev = P.q("v", lambda e, s_=scale: e.tensor_scalar(
                dst[:, base_col:base_col + 4], pr_b[:, base_col:base_col + 4],
                s_, None, Alu.mult),
                waits=[(sT, v_mm)], inc=sV)
        bankV["pb"] = v_ev
        return v_ev

    for half in range(2):
        v_r = row_load(bqkv_d[half * 512:(half + 1) * 512][None, :])
        col_block(bq32c, half * 4, v_r, scale=1.0 / 32.0)
    for half in range(2):
        v_r = row_load(bqkv_d[1024 + half * 512:1024 + (half + 1) * 512][None, :])
        col_block(bkc, half * 4, v_r)
    for half in range(2):
        v_r = row_load(bqkv_d[2048 + half * 512:2048 + (half + 1) * 512][None, :])
        v_c = P.q("v", lambda e, h=half: e.tensor_copy(
            rowbf[0:1, h * 512:(h + 1) * 512], rowstg[0:1, :]),
            waits=[(dC, v_r)], inc=sV)
        last_row_use = (sV, v_c)
    for half in range(2):
        v_r = row_load(bproj_d[half * 512:(half + 1) * 512][None, :])
        v_c = P.q("v", lambda e, h=half: e.tensor_copy(
            rowbf[0:1, 1024 + h * 512:1024 + (h + 1) * 512], rowstg[0:1, :]),
            waits=[(dC, v_r)], inc=sV)
        last_row_use = (sV, v_c)
    for quarter in range(4):
        v_r = row_load(b1_d.rearrange("e h -> (e h)")[quarter * 512:(quarter + 1) * 512][None, :])
        col_block(b1c, quarter * 4, v_r)
    v_r = row_load(wgb_d[:][None, :])
    v_mm = P.q("t", lambda e: e.matmul(pr_b[:, 16:24], onesf[0:1, :], rowstg[0:1, 0:8],
                                       start=True, stop=True),
               waits=[(dC, v_r)] + bw("pb"), inc=sT)
    last_row_use = (sT, v_mm)
    v_ev = P.q("v", lambda e: e.tensor_copy(wgbb[:, :], pr_b[:, 16:24]),
               waits=[(sT, v_mm)], inc=sV)
    bankV["pb"] = v_ev
    v_wg = dmaw("y", dC, wgstg[:, :].rearrange("p (t e) -> p t e", e=8),
                wg_d.rearrange("(t p) e -> p t e", p=128),
                waits=[(dC, dC.v)])
    v_consts = P.q("v", lambda e: e.tensor_copy(wgbf[:, :], wgstg[:, :]),
                   waits=[(dC, v_wg)], inc=sV)

    x_re = x_d.rearrange("(t p) d -> p t d", p=128)
    t1_v = t1f[:, :].rearrange("p (t d) -> p t d", d=D)
    xT_v = xT[:, :].rearrange("p (dt s) -> p dt s", s=SEQ)
    wp_v = wps[:, :].rearrange("p (t f) -> p t f", f=1024)
    v_xc = []
    for c in range(NCH):
        v_xc.append(dmaw("y", dX, t1_v[:, 4 * c:4 * c + 4, :],
                         x_re[:, 4 * c:4 * c + 4, :], waits=[(dX, dX.v)]))

    v_xTc = [0] * NCH
    for tt in range(NT):
        for grp in range(2):
            key, bank = (("pa", pr_a) if (tt * 2 + grp) % 2 == 0 else ("pb", pr_b))
            for k in range(4):
                ds = grp * 4 + k
                P.q("t", lambda e, b=bank, k_=k, t_=tt, d_=ds: e.transpose(
                    b[:, k_ * 128:(k_ + 1) * 128],
                    t1f[:, t_ * D + d_ * 128: t_ * D + (d_ + 1) * 128],
                    identf[:, :]),
                    waits=[(dX, v_xc[tt // 4]), (sG, vg_decay)] + bw(key),
                    inc=(sT if k == 3 else None))
            v_tp = sT.v
            v_ev = P.q("v", lambda e, b=bank, g_=grp, t_=tt: e.tensor_copy(
                xT_v[:, g_ * 4:(g_ + 1) * 4, t_ * 128:(t_ + 1) * 128],
                b[:, :].rearrange("p (k m) -> p k m", m=128)),
                waits=[(sT, v_tp)], inc=sV)
            bankV[key] = v_ev
        if tt % 4 == 3:
            v_xTc[tt // 4] = sV.v
    v_xT_done = sV.v

    stg_half = [(sV, 0)]
    w_re = wqkv_d.rearrange("(t p) f -> p t f", p=128)
    wproj_re = wproj_d.rearrange("(t p) f -> p t f", p=128)
    stg3 = stg[:, :].rearrange("p (t f) -> p t f", f=128)

    def load_w_piece(col0, j, extra_waits=(), cast_waits=()):
        c, v = stg_half[0]
        if col0 >= 10000:
            srcap = wproj_re[:, :, j * 128:(j + 1) * 128]
        else:
            srcap = w_re[:, :, col0 + j * 128:col0 + (j + 1) * 128]
        v_l = dmaw("y", dY, stg3[:, :, :], srcap,
                   waits=[(c, v)] + list(extra_waits))
        v_c = P.q("v", lambda e, j_=j: e.tensor_copy(
            wp_v[:, :, j_ * 128:(j_ + 1) * 128], stg3[:, :, :]),
            waits=[(dY, v_l)] + list(cast_waits), inc=sV)
        stg_half[0] = (sV, v_c)
        return v_c

    qT_v = qTs[:, :].rearrange("p (c ft s) -> p c ft s", ft=8, s=512)
    v_qmm_ft = []
    v_wq = [load_w_piece(0, 0, [last_row_use])]
    for ft in range(8):
        if ft < 7:
            v_wq.append(load_w_piece(0, ft + 1))
        for t in range(NDT):
            for c in range(NCH):
                P.q("t", lambda e, f_=ft, t_=t, c_=c: e.matmul(
                    ps_s[:, c_ * 512:(c_ + 1) * 512],
                    wp_v[:, t_, f_ * 128:(f_ + 1) * 128],
                    xT_v[:, t_, c_ * 512:(c_ + 1) * 512],
                    start=(t_ == 0), stop=(t_ == NDT - 1)),
                    waits=[(sV, max(v_wq[ft], v_xTc[c]))] + bw("s%d" % c),
                    inc=(sT if (t == NDT - 1 and c == NCH - 1) else None))
        v_mm = sT.v
        v_qmm_ft.append(v_mm)
        for c in range(NCH):
            v_e = P.q("s", lambda e, f_=ft, c_=c: e.activation(
                qT_v[:, c_, f_, :], ps_s[:, c_ * 512:(c_ + 1) * 512], AF.Identity,
                bias=bq32c[:, f_:f_ + 1], scale=1.0 / 32.0),
                waits=[(sT, v_mm), (sV, v_consts)], inc=sS)
            bankS["s%d" % c] = v_e
    v_qT_done = sS.v

    kT_v = kTs[:, :].rearrange("p (ft s) -> p ft s", s=SEQ)
    v_kmm_ft = []
    v_wk = [load_w_piece(1024, 0, cast_waits=[(sT, v_qmm_ft[0])])]
    for ft in range(8):
        if ft < 7:
            v_wk.append(load_w_piece(1024, ft + 1, cast_waits=[(sT, v_qmm_ft[ft + 1])]))
        for t in range(NDT):
            for c in range(NCH):
                P.q("t", lambda e, f_=ft, t_=t, c_=c: e.matmul(
                    ps_s[:, c_ * 512:(c_ + 1) * 512],
                    wp_v[:, t_, f_ * 128:(f_ + 1) * 128],
                    xT_v[:, t_, c_ * 512:(c_ + 1) * 512],
                    start=(t_ == 0), stop=(t_ == NDT - 1)),
                    waits=[(sV, v_wk[ft])] + bw("s%d" % c),
                    inc=(sT if (t == NDT - 1 and c == NCH - 1) else None))
        v_mm = sT.v
        v_kmm_ft.append(v_mm)
        for c in range(NCH):
            v_e = P.q("s", lambda e, f_=ft, c_=c: e.activation(
                kT_v[:, f_, c_ * 512:(c_ + 1) * 512],
                ps_s[:, c_ * 512:(c_ + 1) * 512], AF.Identity,
                bias=bkc[:, f_:f_ + 1], scale=1.0),
                waits=[(sT, v_mm)], inc=sS)
            bankS["s%d" % c] = v_e
    v_kT_done = sS.v

    vN_v = vNs[:, :].rearrange("p (kt f) -> p kt f", f=1024)
    v_vmm_h = []
    v_wv = [load_w_piece(2048, 0, cast_waits=[(sT, v_kmm_ft[0])])]
    for j in range(1, 4):
        v_wv.append(load_w_piece(2048, j, cast_waits=[(sT, v_kmm_ft[j])]))
    for h in range(2):
        vh = v_wv[4 * h + 3]
        for tt in range(NT):
            if h == 0 and tt < 4:
                v_wv.append(load_w_piece(2048, 4 + tt, cast_waits=[(sT, v_kmm_ft[4 + tt])]))
            lo = (2 * tt + h) % 4
            for t in range(NDT):
                P.q("t", lambda e, t_=t, tt_=tt, h_=h, l_=lo, v_=vh: e.matmul(
                    ps_s[:, l_ * 512:(l_ + 1) * 512],
                    xT_v[:, t_, tt_ * 128:(tt_ + 1) * 128],
                    wp_v[:, t_, h_ * 512:(h_ + 1) * 512],
                    start=(t_ == 0), stop=False),
                    waits=[(sV, max(vh, v_xT_done, v_consts))] + bw("s%d" % lo),
                    inc=None)
            P.q("t", lambda e, h_=h, l_=lo: e.matmul(
                ps_s[:, l_ * 512:(l_ + 1) * 512],
                onesbf[0:1, :], rowbf[0:1, h_ * 512:(h_ + 1) * 512],
                start=False, stop=True), inc=sT)
            v_mm = sT.v
            v_ev = P.q("v", lambda e, tt_=tt, h_=h, l_=lo: e.tensor_copy(
                vN_v[:, tt_, h_ * 512:(h_ + 1) * 512],
                ps_s[:, l_ * 512:(l_ + 1) * 512]),
                waits=[(sT, v_mm)], inc=sV)
            bankV["s%d" % lo] = v_ev
        v_vmm_h.append(sT.v)
    v_v_done = sV.v
    v_vmm_done = sT.v

    v_wproj = 0
    for j in range(8):
        v_wproj = load_w_piece(10000, j, cast_waits=[(sT, v_vmm_h[j // 4])])

    bgV = []

    def drain_bg(n):
        k = 0
        while bgV and k < n:
            bgV.pop(0)()
            k += 1

    xlncm = xlnT[:, :].rearrange("p (c dt s) -> p c dt s", dt=8, s=512)
    wgbf_v = wgbf[:, :].rearrange("p (dt e) -> p dt e", e=8)
    v_xlncm = [0] * NCH
    v_masked_c = [0] * NCH

    def emit_gate(c):
        sl = slice(4 * c, 4 * c + 4)
        P.q("v", lambda e: e.tensor_scalar(mu1[:, sl], mu1[:, sl], 1.0 / D, None, Alu.mult),
            waits=[(sS, s_sums[c]), (sV, sV.v)], inc=sV)

        def op(f):
            P.q("v", f, waits=[(sV, sV.v)], inc=sV)

        op(lambda e: e.tensor_scalar(lntmp[:, sl], s21[:, sl], 1.0 / D, None, Alu.mult))
        op(lambda e: e.tensor_mul(lntmp2[:, sl], mu1[:, sl], mu1[:, sl]))
        op(lambda e: e.tensor_sub(lntmp[:, sl], lntmp[:, sl], lntmp2[:, sl]))
        op(lambda e: e.tensor_scalar_add(lntmp[:, sl], lntmp[:, sl], EPS_LN))
        op(lambda e: e.reciprocal(lntmp[:, sl], lntmp[:, sl]))
        op(lambda e: e.tensor_scalar(inv1[:, sl], lntmp[:, sl], 0.5, 0.5, Alu.mult, Alu.add))
        for _ in range(8):
            op(lambda e: e.reciprocal(lntmp2[:, sl], inv1[:, sl]))
            op(lambda e: e.tensor_mul(lntmp2[:, sl], lntmp[:, sl], lntmp2[:, sl]))
            op(lambda e: e.tensor_add(inv1[:, sl], inv1[:, sl], lntmp2[:, sl]))
            op(lambda e: e.tensor_scalar(inv1[:, sl], inv1[:, sl], 0.5, None, Alu.mult))
        v_fin = P.q("v", lambda e: e.scalar_tensor_tensor(
            nb1[:, sl], mu1[:, sl], -1.0, inv1[:, sl], Alu.mult, Alu.mult),
            waits=[(sV, sV.v)], inc=sV)
        for j in range(TPC):
            i = 4 * c + j
            xb = xb16 if j % 2 == 0 else xb16b
            s_app = P.q("s", lambda e, i_=i: e.activation(
                t1_v[:, i_, :], t1_v[:, i_, :], AF.Identity,
                bias=nb1[:, i_:i_ + 1], scale=inv1[:, i_:i_ + 1]),
                waits=[(sV, v_fin), (sS, sS.v)], inc=sS)
            v_cast = P.q("s", lambda e, i_=i, x_=xb: e.activation(x_[:, :], t1_v[:, i_, :], AF.Copy),
                         waits=[(sS, s_app), (sV, bankV.get("xb%d" % (j % 2), 0))], inc=sS)
            for k in range(8):
                P.q("t", lambda e, k_=k, x_=xb: e.transpose(
                    ps_ba[:, k_ * 128:(k_ + 1) * 128],
                    x_[:, k_ * 128:(k_ + 1) * 128],
                    identbf[:, :]),
                    waits=[(sS, v_cast)] + bw("ba"),
                    inc=(sT if k == 7 else None))
            v_tp = sT.v
            v_ev = P.q("v", lambda e, c_=c, j_=j: e.tensor_copy(
                xlncm[:, c_, 0:8, j_ * 128:(j_ + 1) * 128],
                ps_ba[:, :].rearrange("p (k m) -> p k m", m=128)),
                waits=[(sT, v_tp)], inc=sV)
            bankV["ba"] = v_ev
            bankV["xb%d" % (j % 2)] = v_ev
            v_xlncm_t = v_ev
            key, bank = (("pa", pr_a) if j % 2 == 0 else ("pb", pr_b))
            for dt in range(NDT):
                P.q("t", lambda e, b=bank, c_=c, j_=j, dt_=dt: e.matmul(
                    b[:, 0:8],
                    xlncm[:, c_, dt_, j_ * 128:(j_ + 1) * 128],
                    wgbf_v[:, dt_, :],
                    start=(dt_ == 0), stop=(dt_ == NDT - 1)),
                    waits=[(sV, max(v_xlncm_t, v_consts))] + bw(key),
                    inc=(sT if dt == NDT - 1 else None))
            v_mm = sT.v
            v_ev = P.q("v", lambda e, b=bank, i_=i: e.tensor_add(
                gall[:, i_ * 8:(i_ + 1) * 8], b[:, 0:8], wgbb[:, :]),
                waits=[(sT, v_mm)], inc=sV)
            bankV[key] = v_ev
            sl8 = slice(i * 8, (i + 1) * 8)

            def opb(f):
                bgV.append(lambda f=f: P.q("v", f, waits=[(sV, sV.v)], inc=sV))

            opb(lambda e, s_=sl8, i_=i: e.reduce_max(m1c[:, i_:i_ + 1], gall[:, s_], axis=AX.X))
            opb(lambda e, s_=sl8, i_=i: e.tensor_scalar(
                gscr[:, s_], gall[:, s_], m1c[:, i_:i_ + 1], None, Alu.subtract))

            def gexp(s_=sl8):
                v_gs = sV.v
                v_ge = P.q("s", lambda e, s2=s_: e.activation(gscr[:, s2], gscr[:, s2], AF.Exp),
                           waits=[(sV, v_gs)], inc=sS)
                P.q("v", lambda e, s2=s_: e.tensor_copy(gscr[:, s2], gscr[:, s2]),
                    waits=[(sS, v_ge)], inc=sV)

            bgV.append(gexp)
            opb(lambda e, s_=sl8, i_=i: e.reduce_sum(r8c[:, i_:i_ + 1], gscr[:, s_], axis=AX.X))
            opb(lambda e, i_=i: e.reciprocal(r8c[:, i_:i_ + 1], r8c[:, i_:i_ + 1]))
            opb(lambda e, s_=sl8, i_=i: e.tensor_scalar(
                gscr[:, s_], gscr[:, s_], r8c[:, i_:i_ + 1], None, Alu.mult))
            opb(lambda e, s_=sl8, i_=i: e.tensor_scalar(
                maskd[:, s_], gall[:, s_], m1c[:, i_:i_ + 1], -1e30, Alu.is_equal, Alu.mult))
            opb(lambda e, s_=sl8: e.tensor_add(maskd[:, s_], maskd[:, s_], gall[:, s_]))
            opb(lambda e, s_=sl8, i_=i: e.reduce_max(m2c[:, i_:i_ + 1], maskd[:, s_], axis=AX.X))
            opb(lambda e, s_=sl8, i_=i: e.tensor_scalar(
                maskd[:, s_], gall[:, s_], m2c[:, i_:i_ + 1], None, Alu.is_ge))

            def lastm(s_=sl8, c_=c):
                v_masked_c[c_] = P.q("v", lambda e, s2=s_: e.tensor_mul(
                    maskd[:, s2], maskd[:, s2], gscr[:, s2]),
                    waits=[(sV, sV.v)], inc=sV)

            bgV.append(lastm)

        v_xlncm[c] = sV.v

    s_sums = [0] * NCH
    pT_v = pT[:, :].rearrange("p (kt s) -> p kt s", s=512)
    v_exp = [0] * NT
    v_exp_p = [[0] * 4 for _ in range(NT)]
    v_rall = [0] * NT
    v_ptev = [0] * NT
    last_exp = 0

    def emit_ptp(i, pb):
        j = i % TPC
        for grp in range(2):
            key, bank = (("ba", ps_ba) if grp == 0 else ("bb", ps_bb))
            for k in range(8):
                kt = grp * 8 + k
                P.q("t", lambda e, b=bank, k_=k, kt_=kt, pb_=pb: e.transpose(
                    b[:, k_ * 128:(k_ + 1) * 128],
                    pnat[:, pb_ * 2048 + kt_ * 128: pb_ * 2048 + (kt_ + 1) * 128],
                    identbf[:, :]),
                    waits=[(sS, v_exp_p[i][2 * grp + 1])] + bw(key),
                    inc=(sT if k == 7 else None))
            v_tp = sT.v
            v_ev = P.q("s", lambda e, b=bank, g_=grp, j_=j: e.activation(
                pT_v[:, g_ * 8:(g_ + 1) * 8, j_ * 128:(j_ + 1) * 128],
                b[:, :].rearrange("p (k m) -> p k m", m=128), AF.Copy),
                waits=[(sT, v_tp)], inc=sS)
            bankS[key] = v_ev
        v_ptev[i] = sS.v
        drain_bg(24)

    for c in range(NCH):
        pend = None
        for j in range(TPC):
            i = c * TPC + j
            kc_d, off_d = divmod(i * 128, 512)
            for kc in range(4):
                for ft in range(8):
                    P.q("t", lambda e, kc_=kc, f_=ft, c_=c, j_=j: e.matmul(
                        ps_s[:, kc_ * 512:(kc_ + 1) * 512],
                        qT_v[:, c_, f_, j_ * 128:(j_ + 1) * 128],
                        kT_v[:, f_, kc_ * 512:(kc_ + 1) * 512],
                        start=(f_ == 0), stop=(f_ == 7)),
                        waits=[(sS, max(v_qT_done, v_kT_done))] + bw("s%d" % kc),
                        inc=(sT if ft == 7 else None))
                v_qmm = sT.v
                if kc == kc_d:
                    P.q("v", lambda e, kc_=kc_d, o_=off_d: e.tensor_mul(
                        ps_s[:, kc_ * 512 + o_:kc_ * 512 + o_ + 128],
                        ps_s[:, kc_ * 512 + o_:kc_ * 512 + o_ + 128],
                        decayD[:, :]),
                        waits=[(sT, v_qmm), (sG, vg_decay)], inc=sV)
                P.q("v", lambda e, kc_=kc: e.reduce_max(
                    qmx[:, kc_:kc_ + 1], ps_s[:, kc_ * 512:(kc_ + 1) * 512], axis=AX.X),
                    waits=[(sT, v_qmm), (sV, sV.v)], inc=sV)
            v_smm = sT.v
            v_rmax = P.q("v", lambda e: e.reduce_max(mcol[:, :], qmx[:, :],
                                                     axis=AX.X, negate=True),
                         waits=[(sV, sV.v)], inc=sV)
            pb = j % 2
            rb = (i % 2) * 4
            for kc in range(4):
                vp = P.q("s", lambda e, pb_=pb, kc_=kc, b_=rb: e.activation(
                    pnat[:, pb_ * 2048 + kc_ * 512:pb_ * 2048 + (kc_ + 1) * 512],
                    ps_s[:, kc_ * 512:(kc_ + 1) * 512], AF.Exp,
                    bias=mcol[:, :], scale=1.0,
                    accum_out=(None if kc_ == 0 else rq[:, b_ + kc_:b_ + kc_ + 1])),
                    waits=[(sV, v_rmax)], inc=sS)
                bankS["s%d" % kc] = vp
                v_exp_p[i][kc] = vp
            v_exp[i] = v_exp_p[i][3]
            last_exp = v_exp[i]
            P.q("v", lambda e, b_=rb, pb_=pb: e.reduce_sum(
                rq[:, b_:b_ + 1], pnat[:, pb_ * 2048:pb_ * 2048 + 512], axis=AX.X),
                waits=[(sS, v_exp_p[i][0]), (sV, sV.v)], inc=sV)
            P.q("v", lambda e, b_=rb: e.tensor_add(
                rq[:, b_:b_ + 1], rq[:, b_:b_ + 1], rq[:, b_ + 1:b_ + 2]),
                waits=[(sS, v_exp_p[i][1])], inc=sV)
            P.q("v", lambda e, b_=rb: e.tensor_add(
                rq[:, b_ + 2:b_ + 3], rq[:, b_ + 2:b_ + 3], rq[:, b_ + 3:b_ + 4]),
                waits=[(sS, v_exp_p[i][3])], inc=sV)
            v_rall[i] = P.q("v", lambda e, i_=i, b_=rb: e.tensor_add(
                rall[:, i_:i_ + 1], rq[:, b_:b_ + 1], rq[:, b_ + 2:b_ + 3]),
                waits=[(sV, sV.v)], inc=sV)
            P.q("v", lambda e, i_=i: e.reciprocal(invr[:, i_:i_ + 1], rall[:, i_:i_ + 1]),
                waits=[(sV, v_rall[i])], inc=sV)
            if pend is not None:
                emit_ptp(*pend)
            pend = (i, pb)
        emit_ptp(*pend)
        v_pt_all = v_ptev[c * TPC + TPC - 1]

        for ds in range(8):
            lo = ds % 2
            for kt in range(NT):
                P.q("t", lambda e, ds_=ds, kt_=kt, l_=lo: e.matmul(
                    ps_s[:, l_ * 512:(l_ + 1) * 512],
                    vN_v[:, kt_, ds_ * 128:(ds_ + 1) * 128],
                    pT_v[:, kt_, :],
                    start=(kt_ == 0), stop=(kt_ == NT - 1)),
                    waits=[(sS, max(last_exp, v_pt_all)), (sV, v_v_done)] + bw("s%d" % lo),
                    inc=(sT if kt == NT - 1 else None))
            v_mm = sT.v
            v_ev = P.q("v", lambda e, ds_=ds, c_=c, l_=lo: e.tensor_copy(
                aoT[c_][:, :].rearrange("p (ds s) -> p ds s", s=512)[:, ds_, :],
                ps_s[:, l_ * 512:(l_ + 1) * 512]),
                waits=[(sT, v_mm)], inc=sV)
            bankV["s%d" % lo] = v_ev
        v_ao_done = sV.v
        v_pv_last = sT.v

        for j in range(TPC):
            P.q("t", lambda e, i_=c * TPC + j, j_=j: e.matmul(
                pr_a[0:1, j_ * 128:(j_ + 1) * 128], rall[:, i_:i_ + 1], identf[:, :],
                start=True, stop=True),
                waits=[(sV, v_rall[c * TPC + j])] + bw("pa"),
                inc=(sT if j == TPC - 1 else None))
        v_rr = sT.v
        v_rrbf = P.q("v", lambda e: e.tensor_copy(rowbf[0:1, 0:512], pr_a[0:1, 0:512]),
                     waits=[(sT, v_rr)], inc=sV)
        bankV["pa"] = v_rrbf
        for j in range(TPC):
            i = c * TPC + j
            for h in range(2):
                key, bank = (("pa", pr_a) if h == 0 else ("pb", pr_b))
                for ds in range(8):
                    P.q("t", lambda e, b=bank, ds_=ds, c_=c, j_=j, h_=h: e.matmul(
                        b[:, :],
                        aoT[c_][:, :].rearrange("p (ds s) -> p ds s", s=512)[:, ds_, j_ * 128:(j_ + 1) * 128],
                        wp_v[:, ds_, h_ * 512:(h_ + 1) * 512],
                        start=(ds_ == 0), stop=False),
                        waits=[(sV, max(v_ao_done, v_wproj, v_rrbf))] + bw(key),
                        inc=None)
                P.q("t", lambda e, b=bank, h_=h, j_=j: e.matmul(
                    b[:, :], rowbf[0:1, j_ * 128:(j_ + 1) * 128],
                    rowbf[0:1, 1024 + h_ * 512:1024 + (h_ + 1) * 512],
                    start=False, stop=True), inc=sT)
                v_mm = sT.v
                v_ev = P.q("v", lambda e, b=bank, i_=i, h_=h: e.scalar_tensor_tensor(
                    t1_v[:, i_, h_ * 512:(h_ + 1) * 512], b[:, :], invr[:, i_:i_ + 1],
                    t1_v[:, i_, h_ * 512:(h_ + 1) * 512], Alu.mult, Alu.add),
                    waits=[(sT, v_mm), (dX, v_xc[c])], inc=sV)
                bankV[key] = v_ev
            v_sum_s = P.q("s", lambda e, i_=i: e.activation(
                junk1024[:, :], t1_v[:, i_, :], AF.Identity, bias=0.0, scale=1.0,
                accum_out=mu1[:, i_:i_ + 1]),
                waits=[(sV, sV.v), (sS, sS.v)], inc=sS)
            s_sums[c] = P.q("s", lambda e, i_=i: e.activation(
                junk1024[:, :], t1_v[:, i_, :], AF.Square,
                accum_out=s21[:, i_:i_ + 1]),
                waits=[(sS, v_sum_s)], inc=sS)
        if c >= 1:
            drain_bg(1000000)
            emit_gate(c - 1)
    drain_bg(1000000)
    emit_gate(NCH - 1)
    drain_bg(1000000)
    v_ln1_sums = sV.v
    v_masked = v_masked_c[NCH - 1]

    if stage == 1:
        v_out = 0
        for c in range(NCH):
            v_out = dmaw("s", dS, xln_d.rearrange("(t p) d -> p t d", p=128)[:, 4 * c:4 * c + 4, :],
                         t1_v[:, 4 * c:4 * c + 4, :], waits=[(sV, v_masked)])
        return finalize(nc, P, [(dS, v_out)])

    v_ccw = dmaw("g", dG, cc_in.ap(), maskd[:, :], waits=[(sV, v_masked)])
    v_cc = P.q("g", lambda e: e.collective_compute(
        "AllReduce", Alu.add, replica_groups=[list(range(8))],
        ins=[cc_in.ap().opt()], outs=[cc_out.ap().opt()]),
        waits=[(dG, v_ccw)], inc=sC)
    w1_re = w1_d.rearrange("e (t p) h -> p e t h", p=128)
    w1bf_v = w1bf[:, :].rearrange("p (e t h) -> p e t h", t=8, h=256)
    stg_ab = [stg_half[0], (sS, s_sums[NCH - 1])]
    bufs = [stg[:, :], stgB[:, :]]
    pidx = 0
    w1cast = {}
    w2_re = w2_d.rearrange("e (t p) f -> p e t f", p=128)
    w2bf_v = w2bf[:, :].rearrange("p (e t f) -> p e t f", t=2, f=1024)
    for ei in range(E):
        for t in range(2):
            ab = pidx % 2
            dma_ctr = dY if ab == 0 else dB
            c, v = stg_ab[ab]
            v_l = dmaw("y", dma_ctr, bufs[ab], w2_re[:, ei, t, :],
                       waits=[(c, v)])
            v_c = P.q("v", lambda e, e_=ei, t_=t, a_=ab: e.tensor_copy(
                w2bf_v[:, e_, t_, :], bufs[a_]),
                waits=[(dma_ctr, v_l), (sT, v_pv_last)], inc=sV)
            stg_ab[ab] = (sV, v_c)
            pidx += 1
    v_w2_done = sV.v
    for ei in range(E):
        for hh in range(2):
            ab = pidx % 2
            dma_ctr = dY if ab == 0 else dB
            c, v = stg_ab[ab]
            v_l = dmaw("y", dma_ctr, bufs[ab].rearrange("p (t k) -> p t k", k=256),
                       w1_re[:, ei, hh * 4:(hh + 1) * 4, :],
                       waits=[(c, v)])
            v_c = P.q("v", lambda e, e_=ei, hh_=hh, a_=ab: e.tensor_copy(
                w1bf_v[:, e_, hh_ * 4:(hh_ + 1) * 4, :].rearrange("p t k -> p (t k)"),
                bufs[a_]),
                waits=[(dma_ctr, v_l), (sS, last_exp)], inc=sV)
            stg_ab[ab] = (sV, v_c)
            w1cast[ei] = v_c
            pidx += 1
    v_w1_done = sV.v

    v_ccr = dmaw("g", dG, denom[:, :], cc_out.ap(), waits=[(sC, v_cc)])
    P.q("v", lambda e: e.tensor_scalar_add(denom[:, :], denom[:, :], EPS_DEN),
        waits=[(dG, v_ccr)], inc=sV)
    P.q("v", lambda e: e.reciprocal(denom[:, :], denom[:, :]),
        waits=[(sV, sV.v)], inc=sV)
    v_route = P.q("v", lambda e: e.tensor_mul(gall[:, :], maskd[:, :], denom[:, :]),
                  waits=[(sV, sV.v)], inc=sV)

    selba_v = selba[:, :].rearrange("p (e s) -> p e s", s=512)
    hbuf_v = hbuf[:, :].rearrange("p (e t s) -> p e t s", t=2, s=512)
    w2state = {"prev_mm": 0}
    gelus = {}

    def emit_w1(cc):
        for ei in range(E):
            for ht in range(2):
                lo = (ei * 2 + ht) % 4
                for dt in range(NDT):
                    P.q("t", lambda e, l_=lo, e_=ei, t_=dt, h_=ht, c_=cc: e.matmul(
                        ps_s[:, l_ * 512:(l_ + 1) * 512],
                        w1bf_v[:, e_, t_, h_ * 128:(h_ + 1) * 128],
                        xlncm[:, c_, t_, :],
                        start=(t_ == 0), stop=(t_ == NDT - 1)),
                        waits=[(sV, w1cast[ei]),
                               (sV, v_xlncm[cc])] + bw("s%d" % lo),
                        inc=(sT if dt == NDT - 1 else None))
                v_mm = sT.v
                v_gelu = P.q("s", lambda e, l_=lo, e_=ei, h_=ht: e.activation(
                    hbuf_v[:, e_, h_, :], ps_s[:, l_ * 512:(l_ + 1) * 512],
                    AF.Gelu, bias=b1c[:, e_ * 2 + h_:e_ * 2 + h_ + 1]),
                    waits=[(sT, v_mm), (sT, w2state["prev_mm"]),
                           (sV, bankV.get("hb%d%d" % (ei, ht), 0))], inc=sS)
                bankS["s%d" % lo] = v_gelu
                gelus[(ei, ht)] = v_gelu

    emit_w1(0)

    for grp in range(4):
        key, bank = (("pa", pr_a) if grp % 2 == 0 else ("pb", pr_b))
        for k in range(4):
            i = grp * 4 + k
            P.q("t", lambda e, b=bank, k_=k, i_=i: e.transpose(
                b[0:8, k_ * 128:(k_ + 1) * 128],
                gall[:, i_ * 8:(i_ + 1) * 8],
                identf[:, :]),
                waits=[(sV, v_route)] + bw(key),
                inc=(sT if k == 3 else None))
        v_tp = sT.v
        v_ev = P.q("v", lambda e, b=bank, g_=grp: e.tensor_copy(
            routeT[0:8, :].rearrange("p (t m) -> p t m", m=128)[:, g_ * 4:(g_ + 1) * 4, :],
            b[0:8, :].rearrange("p (k m) -> p k m", m=128)),
            waits=[(sT, v_tp)], inc=sV)
        bankV[key] = v_ev
    v_routeT = sV.v

    Cc = lambda k: st8[0:8, k:k + 1]

    def vq(build, waits=()):
        return P.q("v", build, waits=waits, inc=sV)

    vq(lambda e: e.reduce_max(Cc(8), routeT[0:8, :], axis=AX.X), waits=[(sV, v_routeT)])
    vq(lambda e: e.tensor_scalar(seleq[0:8, :], routeT[0:8, :], 0.0, 1e30, Alu.is_equal, Alu.mult),
       waits=[(sV, sV.v)])
    vq(lambda e: e.tensor_add(seleq[0:8, :], seleq[0:8, :], routeT[0:8, :]), waits=[(sV, sV.v)])
    vq(lambda e: e.tensor_reduce(Cc(9), seleq[0:8, :], AX.X, Alu.min), waits=[(sV, sV.v)])
    vq(lambda e: e.tensor_scalar(selgt[0:8, :], routeT[0:8, :], 0.0, None, Alu.is_gt, Alu.add,
                                 accum_out=Cc(4)), waits=[(sV, sV.v)])
    vq(lambda e: e.tensor_copy(Cc(17), Cc(9)), waits=[(sV, sV.v)])
    vq(lambda e: e.tensor_scalar(Cc(18), Cc(8), 1.001, 1e-38, Alu.mult, Alu.max), waits=[(sV, sV.v)])

    def SQ(dst, srccol, polish=True):
        sq = P.q("s", lambda e: e.activation(Cc(dst), Cc(srccol), AF.Sqrt),
                 waits=[(sV, sV.v)], inc=sS)
        if not polish:
            vq(lambda e: e.tensor_copy(Cc(dst), Cc(dst)), waits=[(sS, sq)])
            return
        vq(lambda e: e.reciprocal(Cc(11), Cc(dst)), waits=[(sS, sq)])
        vq(lambda e: e.tensor_mul(Cc(12), Cc(srccol), Cc(11)), waits=[(sV, sV.v)])
        vq(lambda e: e.tensor_add(Cc(dst), Cc(dst), Cc(12)), waits=[(sV, sV.v)])
        vq(lambda e: e.tensor_scalar(Cc(dst), Cc(dst), 0.5, None, Alu.mult), waits=[(sV, sV.v)])

    SQ(0, 17)
    SQ(1, 18)
    vq(lambda e: e.tensor_mul(Cc(10), Cc(0), Cc(1)), waits=[(sV, sV.v)])
    SQ(2, 10)
    for it in range(NIT):
        vq(lambda e: e.tensor_mul(Cc(16), Cc(2), Cc(2)), waits=[(sV, sV.v)])
        vq(lambda e: e.tensor_scalar(selgt[0:8, :], routeT[0:8, :], Cc(16), None,
                                     Alu.is_ge, Alu.add, accum_out=Cc(3)), waits=[(sV, sV.v)])
        vq(lambda e: e.tensor_scalar(stm[0:8, 0:1], Cc(3), float(KCAP), None, Alu.is_ge), waits=[(sV, sV.v)])
        vq(lambda e: e.copy_predicated(Cc(0), stm[0:8, 0:1], Cc(2)), waits=[(sV, sV.v)])
        vq(lambda e: e.tensor_scalar(stm[0:8, 1:2], Cc(3), float(KCAP), None, Alu.is_lt), waits=[(sV, sV.v)])
        vq(lambda e: e.copy_predicated(Cc(1), stm[0:8, 1:2], Cc(2)), waits=[(sV, sV.v)])
        vq(lambda e: e.tensor_mul(Cc(10), Cc(0), Cc(1)), waits=[(sV, sV.v)])
        SQ(2, 10, polish=(it >= NIT - 8))
        if it >= 4:
            P.q("t", lambda e: e.matmul(ps_s[:, 0:128], identbf[:, :], identbf[:, :],
                                        start=True, stop=True),
                waits=[(sV, sV.v), (sS, gelus[(E - 1, 1)])], inc=None)
    vq(lambda e: e.tensor_mul(Cc(0), Cc(0), Cc(0)), waits=[(sV, sV.v)])
    vq(lambda e: e.tensor_mul(Cc(1), Cc(1), Cc(1)), waits=[(sV, sV.v)])
    for _sn in range(NSNAP):
        P.q("t", lambda e: e.matmul(ps_s[:, 0:128], identbf[:, :], identbf[:, :],
                                    start=True, stop=True),
            waits=[(sV, sV.v)], inc=None)
        vq(lambda e: e.scalar_tensor_tensor(seleq[0:8, :], routeT[0:8, :], Cc(1),
                                            routeT[0:8, :], Alu.is_lt, Alu.mult), waits=[(sV, sV.v)])
        vq(lambda e: e.reduce_max(Cc(12), seleq[0:8, :], axis=AX.X), waits=[(sV, sV.v)])
        vq(lambda e: e.tensor_scalar(selgt[0:8, :], routeT[0:8, :], Cc(12), None,
                                     Alu.is_ge, Alu.add, accum_out=Cc(3)), waits=[(sV, sV.v)])
        vq(lambda e: e.tensor_scalar(stm[0:8, 0:1], Cc(3), float(KCAP), None, Alu.is_lt), waits=[(sV, sV.v)])
        vq(lambda e: e.copy_predicated(Cc(1), stm[0:8, 0:1], Cc(12)), waits=[(sV, sV.v)])
        vq(lambda e: e.tensor_scalar(stm[0:8, 1:2], Cc(3), float(KCAP), None, Alu.is_ge), waits=[(sV, sV.v)])
        vq(lambda e: e.copy_predicated(Cc(0), stm[0:8, 1:2], Cc(12)), waits=[(sV, sV.v)])
    vq(lambda e: e.tensor_scalar(stm[0:8, 0:1], Cc(4), float(KCAP), None, Alu.is_ge), waits=[(sV, sV.v)])
    vq(lambda e: e.memset(Cc(13), 0.0), waits=[(sV, sV.v)])
    vq(lambda e: e.copy_predicated(Cc(13), stm[0:8, 0:1], Cc(0)), waits=[(sV, sV.v)])
    vq(lambda e: e.tensor_scalar(selgt[0:8, :], routeT[0:8, :], Cc(13), None,
                                 Alu.is_gt, Alu.add, accum_out=Cc(14)), waits=[(sV, sV.v)])
    vq(lambda e: e.tensor_scalar(seleq[0:8, :], routeT[0:8, :], Cc(13), None, Alu.is_equal),
       waits=[(sV, sV.v)])
    vq(lambda e: e.memset(routeT[0:8, :], 0.0), waits=[(sV, sV.v)])
    vq(lambda e: e.tensor_tensor_scan(routeT[0:8, :], seleq[0:8, :], routeT[0:8, :],
                                      0.0, Alu.add, Alu.add), waits=[(sV, sV.v)])
    vq(lambda e: e.tensor_scalar(Cc(5), Cc(14), -1.0, float(KCAP), Alu.mult, Alu.add),
       waits=[(sV, sV.v)])
    vq(lambda e: e.tensor_scalar(routeT[0:8, :], routeT[0:8, :], Cc(5), None, Alu.is_le),
       waits=[(sV, sV.v)])
    vq(lambda e: e.tensor_mul(seleq[0:8, :], seleq[0:8, :], routeT[0:8, :]), waits=[(sV, sV.v)])
    v_self32 = vq(lambda e: e.tensor_add(selgt[0:8, :], selgt[0:8, :], seleq[0:8, :]),
                  waits=[(sV, sV.v)])

    if stage == 2:
        v_o1 = dmaw("s", dS, sel_d[:, :], selgt[0:8, :], waits=[(sV, v_self32)])
        v_out = v_o1
        for c in range(NCH):
            v_out = dmaw("s", dS, xln_d.rearrange("(t p) d -> p t d", p=128)[:, 4 * c:4 * c + 4, :],
                         t1_v[:, 4 * c:4 * c + 4, :], waits=[(sV, v_self32)])
        return finalize(nc, P, [(dS, v_out)])

    v_selbf = vq(lambda e: e.tensor_copy(selbf[0:8, :], selgt[0:8, :]), waits=[(sV, v_self32)])
    v_b2 = 0
    for half in range(2):
        v_l = dmaw("y", dC, b2stg[0:8, :], b2_d[:, half * 512:(half + 1) * 512],
                   waits=[(sV, v_selbf), (dC, dC.v), (sS, v_b2)])
        v_b2 = P.q("s", lambda e, h=half: e.activation(
            b2bf[0:8, h * 512:(h + 1) * 512], b2stg[0:8, :], AF.Copy),
            waits=[(dC, v_l)], inc=sS)
    v_esel0 = P.q("g", lambda e: e.memset(esel[0:8, :], 0.0),
                  waits=[(sV, v_selbf)], inc=sG)
    v_esel = P.q("g", lambda e: e.affine_select(
        out=esel[0:8, :], in_=esel[0:8, :],
        pattern=[[-1, 8], [0, 128]], compare_op=Alu.not_equal,
        fill=1.0, base=0, channel_multiplier=1),
        waits=[(sG, v_esel0)], inc=sG)

    v_ln2_sums = 0
    for cc in range(NCH):
        if cc > 0:
            emit_w1(cc)
        v_selba = 0
        for ei in range(E):
            key, bank = (("pa", pr_a) if ei % 2 == 0 else ("pb", pr_b))
            v_mm = P.q("t", lambda e, b=bank, e_=ei, c_=cc: e.matmul(
                b[:, :], esel[0:8, e_ * 128:(e_ + 1) * 128],
                selbf[0:8, c_ * 512:(c_ + 1) * 512],
                start=True, stop=True),
                waits=[(sV, v_selbf), (sG, v_esel)] + bw(key), inc=sT)
            v_ev = P.q("v", lambda e, b=bank, e_=ei: e.tensor_copy(
                selba_v[:, e_, :], b[:, :]),
                waits=[(sT, v_mm)], inc=sV)
            bankV[key] = v_ev
            v_selba = v_ev
        for ei in range(E):
            for ht in range(2):
                v_hmask = P.q("v", lambda e, e_=ei, h_=ht: e.tensor_mul(
                    hbuf_v[:, e_, h_, :], hbuf_v[:, e_, h_, :], selba_v[:, e_, :]),
                    waits=[(sS, gelus[(ei, ht)]), (sV, v_selba)], inc=sV)
                bankV["hb%d%d" % (ei, ht)] = v_hmask
        v_h_done = sV.v
        for jt in range(TPC):
            i = cc * TPC + jt
            for h in range(2):
                key, bank = (("pa", pr_a) if h == 0 else ("pb", pr_b))
                for ei in range(E):
                    for ht in range(2):
                        P.q("t", lambda e, b=bank, e_=ei, h_=h, ht_=ht, j_=jt: e.matmul(
                            b[:, :],
                            hbuf_v[:, e_, ht_, j_ * 128:(j_ + 1) * 128],
                            w2bf_v[:, e_, ht_, h_ * 512:(h_ + 1) * 512],
                            start=(e_ == 0 and ht_ == 0), stop=False),
                            waits=[(sV, max(v_h_done, v_w2_done)), (sS, v_b2)] + bw(key),
                            inc=None)
                P.q("t", lambda e, b=bank, i_=i, h_=h: e.matmul(
                    b[:, :], selbf[0:8, i_ * 128:(i_ + 1) * 128],
                    b2bf[0:8, h_ * 512:(h_ + 1) * 512],
                    start=False, stop=True), inc=sT)
                v_mm = sT.v
                w2state["prev_mm"] = v_mm
                v_ev = P.q("v", lambda e, b=bank, i_=i, h_=h: e.tensor_add(
                    t1_v[:, i_, h_ * 512:(h_ + 1) * 512], b[:, :],
                    t1_v[:, i_, h_ * 512:(h_ + 1) * 512]),
                    waits=[(sT, v_mm)], inc=sV)
                bankV[key] = v_ev
            P.q("v", lambda e, i_=i: e.reduce_sum(mu2[:, i_:i_ + 1], t1_v[:, i_, :], axis=AX.X),
                waits=[(sV, sV.v)], inc=sV)
            P.q("v", lambda e, i_=i: e.scalar_tensor_tensor(
                junk1024[:, :], t1_v[:, i_, :], 1.0, t1_v[:, i_, :],
                Alu.mult, Alu.mult, accum_out=s22[:, i_:i_ + 1]),
                waits=[(sV, sV.v)], inc=sV)
    v_ln2_sums = sV.v

    P.q("v", lambda e: e.tensor_scalar(mu2[:, :], mu2[:, :], 1.0 / D, None, Alu.mult),
        waits=[(sV, v_ln2_sums)], inc=sV)
    P.q("v", lambda e: e.tensor_scalar(lntmp[:, :], s22[:, :], 1.0 / D, None, Alu.mult),
        waits=[(sV, sV.v)], inc=sV)
    P.q("v", lambda e: e.tensor_mul(lntmp2[:, :], mu2[:, :], mu2[:, :]),
        waits=[(sV, sV.v)], inc=sV)
    P.q("v", lambda e: e.tensor_sub(lntmp[:, :], lntmp[:, :], lntmp2[:, :]),
        waits=[(sV, sV.v)], inc=sV)
    P.q("v", lambda e: e.tensor_scalar_add(lntmp[:, :], lntmp[:, :], EPS_LN),
        waits=[(sV, sV.v)], inc=sV)
    v_rc2 = P.q("v", lambda e: e.reciprocal(lntmp2[:, :], lntmp[:, :]),
                waits=[(sV, sV.v)], inc=sV)
    v_inv2 = P.q("s", lambda e: e.activation(inv2[:, :], lntmp2[:, :], AF.Sqrt),
                 waits=[(sV, v_rc2)], inc=sS)
    v_final = 0
    for i in range(NT):
        v_final = P.q("v", lambda e, i_=i: e.tensor_scalar(
            t1_v[:, i_, :], t1_v[:, i_, :], mu2[:, i_:i_ + 1], inv2[:, i_:i_ + 1],
            Alu.subtract, Alu.mult),
            waits=[(sS, v_inv2)], inc=sV)
    y_re = y_d.rearrange("(t p) d -> p t d", p=128)
    v_out = 0
    for c in range(NCH):
        v_out = dmaw("s", dS, y_re[:, 4 * c:4 * c + 4, :], t1_v[:, 4 * c:4 * c + 4, :],
                     waits=[(sV, v_final)])
    return finalize(nc, P, [(dS, v_out)])


def finalize(nc, P, final_waits):
    def fw(eng):
        for c, val in final_waits:
            eng.wait_ge(c.h, val)
    P.ops["y"].append(fw)
    with nc.Block() as block:
        @block.sync
        def _(eng):
            for f in P.ops["y"]:
                f(eng)

        @block.tensor
        def _(eng):
            for f in P.ops["t"]:
                f(eng)

        @block.vector
        def _(eng):
            for f in P.ops["v"]:
                f(eng)

        @block.scalar
        def _(eng):
            for f in P.ops["s"]:
                f(eng)

        @block.gpsimd
        def _(eng):
            for f in P.ops["g"]:
                f(eng)
    return nc


def make_in_maps(L0_attn_x, L0_attn_Wqkv, L0_attn_bqkv, L0_attn_Wproj, L0_attn_bproj,
                 L0_attn_gamma, L0_ffn_w_gate_w, L0_ffn_w_gate_b, L0_ffn_W1, L0_ffn_b1,
                 L0_ffn_W2, L0_ffn_b2):
    x = np.ascontiguousarray(np.asarray(L0_attn_x, dtype=np.float32))
    com = {
        "wqkv": np.ascontiguousarray(np.asarray(L0_attn_Wqkv, dtype=np.float32)),
        "bqkv": np.ascontiguousarray(np.asarray(L0_attn_bqkv, dtype=np.float32)),
        "wproj": np.ascontiguousarray(np.asarray(L0_attn_Wproj, dtype=np.float32)),
        "bproj": np.ascontiguousarray(np.asarray(L0_attn_bproj, dtype=np.float32)),
        "gam": np.asarray(L0_attn_gamma, dtype=np.float32).reshape(1, 1).copy(),
        "wg": np.ascontiguousarray(np.asarray(L0_ffn_w_gate_w, dtype=np.float32)),
        "wgb": np.ascontiguousarray(np.asarray(L0_ffn_w_gate_b, dtype=np.float32)),
        "w1": np.ascontiguousarray(np.asarray(L0_ffn_W1, dtype=np.float32)),
        "b1": np.ascontiguousarray(np.asarray(L0_ffn_b1, dtype=np.float32)),
        "w2": np.ascontiguousarray(np.asarray(L0_ffn_W2, dtype=np.float32)),
        "b2": np.ascontiguousarray(np.asarray(L0_ffn_b2, dtype=np.float32)),
    }
    return [dict(com, x=np.ascontiguousarray(x[b])) for b in range(8)]


def kernel(**inputs):
    nc = build_nc(stage=3)
    in_maps = make_in_maps(**inputs)
    res = run_bass_kernel_spmd(nc, in_maps, core_ids=list(range(8)))
    return np.stack([res.results[b]["y"] for b in range(8)], axis=0)


# version 21 (final)
# speedup vs baseline: 1.1295x; 1.0380x over previous
import numpy as np
import concourse.bass as bass
import concourse.mybir as mybir
from concourse.bass_utils import run_bass_kernel_spmd

F32 = mybir.dt.float32
BF16 = mybir.dt.bfloat16
AF = mybir.ActivationFunctionType
Alu = mybir.AluOpType
AX = mybir.AxisListType

D = 1024
SEQ = 2048
E = 8
HD = 256
KCAP = 1024
NT = 16
NDT = 8
NCH = 4
TPC = 4
NIT = 24
NSNAP = 4
NEWT0 = 8
NEWT = 2
EPS_LN = 1e-5
EPS_DEN = 1e-6


class Ctr:
    __slots__ = ("h", "v", "name")

    def __init__(self, h, name):
        self.h = h
        self.v = 0
        self.name = name

    def inc(self, n=1):
        self.v += n
        return self.v


class Plan:
    def __init__(self):
        self.ops = {"t": [], "v": [], "s": [], "g": [], "y": []}
        self.waited = {k: {} for k in self.ops}

    def q(self, e, build, waits=(), inc=None, n=1):
        todo = []
        for c, val in waits:
            if val and val > 0 and self.waited[e].get(c.name, 0) < val:
                todo.append((c.h, val))
                self.waited[e][c.name] = val

        def f(eng, todo=todo, build=build, inc=inc, n=n):
            for h, val in todo:
                eng.wait_ge(h, val)
            ins = build(eng)
            if inc is not None and ins is not None:
                ins.then_inc(inc.h, n)

        self.ops[e].append(f)
        return inc.inc(n) if inc is not None else None


def build_nc(stage=3):
    nc = bass.Bass("TRN2", target_bir_lowering=False,
                   dynamic_dma_scratch_size=2048)

    x_d = nc.dram_tensor("x", [SEQ, D], F32, kind="ExternalInput").ap()
    wqkv_d = nc.dram_tensor("wqkv", [D, 3 * D], F32, kind="ExternalInput").ap()
    bqkv_d = nc.dram_tensor("bqkv", [3 * D], F32, kind="ExternalInput").ap()
    wproj_d = nc.dram_tensor("wproj", [D, D], F32, kind="ExternalInput").ap()
    bproj_d = nc.dram_tensor("bproj", [D], F32, kind="ExternalInput").ap()
    gam_d = nc.dram_tensor("gam", [1, 1], F32, kind="ExternalInput").ap()
    wg_d = nc.dram_tensor("wg", [D, E], F32, kind="ExternalInput").ap()
    wgb_d = nc.dram_tensor("wgb", [E], F32, kind="ExternalInput").ap()
    w1_d = nc.dram_tensor("w1", [E, D, HD], F32, kind="ExternalInput").ap()
    b1_d = nc.dram_tensor("b1", [E, HD], F32, kind="ExternalInput").ap()
    w2_d = nc.dram_tensor("w2", [E, HD, D], F32, kind="ExternalInput").ap()
    b2_d = nc.dram_tensor("b2", [E, D], F32, kind="ExternalInput").ap()
    y_d = nc.dram_tensor("y", [SEQ, D], F32, kind="ExternalOutput").ap()
    if stage in (1, 2):
        xln_d = nc.dram_tensor("xln", [SEQ, D], F32, kind="ExternalOutput").ap()
    if stage == 2:
        sel_d = nc.dram_tensor("selo", [E, SEQ], F32, kind="ExternalOutput").ap()
    cc_in = nc.dram_tensor("cc_in", [128, 128], F32, kind="Internal")
    cc_out = nc.dram_tensor("cc_out", [128, 128], F32, kind="Internal")

    A = nc.alloc_sbuf_tensor
    AT = nc.alloc_sbuf_tensor_at

    def addr(h):
        return nc.lookup_mloc(h).addr

    t1f = A("t1f", [128, NT * D], F32)
    kTs = A("kTs", [128, 16384], BF16)
    vNs = A("vNs", [128, 16384], BF16)
    qTs = A("qTs", [128, 16384], BF16)
    scs = A("scs", [128, 16384], BF16)
    wps = A("wps", [128, 8192], BF16)
    stg = A("stg", [128, 1024], F32)

    sc0, q0, k0, v0, wp0 = addr(scs), addr(qTs), addr(kTs), addr(vNs), addr(wps)
    stg0 = addr(stg)
    rowstg = AT("rowstg", [1, 512], F32, offset=stg0)
    xT = AT("xT", [128, 16384], BF16, offset=sc0)
    pT = AT("pT", [128, 8192], BF16, offset=sc0)
    pnat = AT("pnat", [128, 4096], BF16, offset=sc0 + 16384)
    junk1024 = AT("jk", [128, 1024], F32, offset=sc0 + 24576)
    xb16 = AT("xb16", [128, 1024], BF16, offset=sc0 + 28672)
    stgB = AT("stgB", [128, 1024], F32, offset=sc0 + 24576)
    xb16b = AT("xb16b", [128, 1024], BF16, offset=sc0 + 30720)
    selgt = AT("selgt", [8, 2048], F32, offset=sc0)
    selba = AT("selba", [128, 4096], BF16, offset=sc0)
    hbuf = AT("hbuf", [128, 8192], BF16, offset=sc0 + 8192)
    aoT = [AT("aoT%d" % c, [128, 4096], BF16, offset=q0 + c * 8192) for c in range(NCH)]
    xlnT = AT("xlnT", [128, 16384], BF16, offset=q0)
    w1bf = AT("w1bf", [128, 16384], BF16, offset=k0)
    w2bf = AT("w2bf", [128, 16384], BF16, offset=v0)
    routeT = AT("routeT", [8, 2048], F32, offset=wp0)
    seleq = AT("seleq", [8, 2048], F32, offset=wp0 + 8192)
    selbf = AT("selbf", [8, 2048], BF16, offset=wp0)
    b2bf = AT("b2bf", [8, 1024], BF16, offset=wp0 + 4096)
    esel = AT("esel", [8, 1024], BF16, offset=wp0 + 6144)
    b2stg = AT("b2stg", [8, 512], F32, offset=wp0 + 8192)

    identbf = A("identbf", [128, 128], BF16)
    identf = A("identf", [128, 128], F32)
    decayD = A("decayD", [128, 128], F32)
    onesf = A("onesf", [1, 128], F32)
    onesbf = A("onesbf", [1, 128], BF16)
    rowbf = A("rowbf", [1, 2048], BF16)
    bq32c = A("bq32c", [128, 8], F32)
    bkc = A("bkc", [128, 8], F32)
    b1c = A("b1c", [128, 16], F32)
    wgbb = A("wgbb", [128, 8], F32)
    wgstg = A("wgstg", [128, 64], F32)
    wgbf = A("wgbf", [128, 64], BF16)
    gall = A("gall", [128, 128], F32)
    gscr = A("gscr", [128, 128], F32)
    maskd = A("maskd", [128, 128], F32)
    denom = A("denom", [128, 128], F32)
    mcol = A("mcol", [128, 1], F32)
    rall = A("rall", [128, 16], F32)
    rq = A("rq", [128, 8], F32)
    invr = A("invr", [128, 16], F32)
    mu1 = A("mu1", [128, 16], F32)
    s21 = A("s21", [128, 16], F32)
    inv1 = A("inv1", [128, 16], F32)
    mu2 = A("mu2", [128, 16], F32)
    s22 = A("s22", [128, 16], F32)
    inv2 = A("inv2", [128, 16], F32)
    lntmp = A("lntmp", [128, 16], F32)
    lntmp2 = A("lntmp2", [128, 16], F32)
    nb1 = A("nb1", [128, 16], F32)
    gsm = A("gsm", [1, 8], F32)
    glc = A("glc", [128, 4], F32)
    st8 = A("st8", [8, 24], F32)
    m1c = A("m1c", [128, 16], F32)
    m2c = A("m2c", [128, 16], F32)
    r8c = A("r8c", [128, 16], F32)
    stm = A("stm", [8, 2], mybir.dt.uint32)
    qmx = A("qmx", [128, 4], F32)
    rowstg2 = AT("rowstg2", [1, 512], F32, offset=addr(gall))
    assert nc.sbuf_bytes_remaining >= 0

    ps_s = nc.alloc_psum_tensor("ps_s", [128, 2048], F32)
    ps_ba = nc.alloc_psum_tensor("ps_ba", [128, 1024], BF16)
    ps_bb = nc.alloc_psum_tensor("ps_bb", [128, 1024], BF16)
    pr_a = nc.alloc_psum_tensor("pr_a", [128, 512], F32)
    pr_b = nc.alloc_psum_tensor("pr_b", [128, 512], F32)
    assert nc.psum_banks_remaining == 0

    dY = Ctr(nc.alloc_semaphore("dY"), "dY")
    dC = Ctr(nc.alloc_semaphore("dC"), "dC")
    dX = Ctr(nc.alloc_semaphore("dX"), "dX")
    dB = Ctr(nc.alloc_semaphore("dB"), "dB")
    dG = Ctr(nc.alloc_semaphore("dG"), "dG")
    dS = Ctr(nc.alloc_semaphore("dS"), "dS")
    sT = Ctr(nc.alloc_semaphore("sT"), "sT")
    sV = Ctr(nc.alloc_semaphore("sV"), "sV")
    sS = Ctr(nc.alloc_semaphore("sS"), "sS")
    sG = Ctr(nc.alloc_semaphore("sG"), "sG")
    sC = Ctr(nc.alloc_semaphore("sC"), "sC")

    P = Plan()
    bankV = {}
    bankS = {}

    def bw(key):
        return [(sV, bankV.get(key, 0)), (sS, bankS.get(key, 0))]

    def dmaw(qe, ctr, out, in_, waits=()):
        return P.q(qe, lambda e, o=out, i=in_: e.dma_start(out=o, in_=i),
                   waits=waits, inc=ctr, n=16)

    P.q("g", lambda e: e.memset(identbf[:, :], 0.0), inc=sG)
    P.q("g", lambda e: e.affine_select(out=identbf[:, :], in_=identbf[:, :],
                                       pattern=[[-1, 128]], compare_op=Alu.not_equal,
                                       fill=1.0, base=0, channel_multiplier=1),
        waits=[(sG, sG.v)], inc=sG)
    P.q("g", lambda e: e.memset(identf[:, :], 0.0), inc=sG)
    P.q("g", lambda e: e.affine_select(out=identf[:, :], in_=identf[:, :],
                                       pattern=[[-1, 128]], compare_op=Alu.not_equal,
                                       fill=1.0, base=0, channel_multiplier=1),
        waits=[(sG, sG.v)], inc=sG)
    P.q("g", lambda e: e.iota(decayD[:, :], pattern=[[-1, 128]], base=0,
                              channel_multiplier=1,
                              allow_small_or_imprecise_dtypes=True), inc=sG)
    vg_iota = sG.v
    P.q("g", lambda e: e.memset(onesf[:, :], 1.0), inc=sG)
    P.q("g", lambda e: e.memset(onesbf[:, :], 1.0), inc=sG)
    vg_ones = sG.v

    v_gam = dmaw("y", dC, gsm[0:1, 0:1], gam_d[:, :], waits=[(dC, dC.v)])
    v_abs = P.q("s", lambda e: e.activation(gsm[0:1, 1:2], gsm[0:1, 0:1], AF.Abs),
                waits=[(dC, v_gam)], inc=sS)
    v_cl = P.q("v", lambda e: e.tensor_scalar_max(gsm[0:1, 1:2], gsm[0:1, 1:2], 1e-30),
               waits=[(sS, v_abs)], inc=sV)
    P.q("s", lambda e: e.activation(gsm[0:1, 2:3], gsm[0:1, 1:2], AF.Ln),
        waits=[(sV, v_cl)], inc=sS)
    v_lng = P.q("s", lambda e: e.activation(gsm[0:1, 3:4], gsm[0:1, 0:1], AF.Sign),
                waits=[(sV, v_cl)], inc=sS)
    v_glmm = P.q("t", lambda e: e.matmul(pr_a[:, 0:2], onesf[0:1, :], gsm[0:1, 2:4],
                                         start=True, stop=True),
                 waits=[(sS, v_lng), (sG, vg_ones)], inc=sT)
    v_glc = P.q("v", lambda e: e.tensor_copy(glc[:, 0:2], pr_a[:, 0:2]),
                waits=[(sT, v_glmm)], inc=sV)
    bankV["pa"] = v_glc
    P.q("v", lambda e: e.tensor_scalar(gscr[:, :], decayD[:, :], glc[:, 0:1], 88.0,
                                       Alu.mult, Alu.min),
        waits=[(sV, v_glc), (sG, vg_iota)], inc=sV)
    P.q("v", lambda e: e.memset(rowstg[0:1, 0:128], 1.0), waits=[(sV, sV.v)], inc=sV)
    v_rrow = P.q("v", lambda e: e.memset(
        rowstg[0:1, 0:128].rearrange("p (a b) -> p a b", b=2)[:, :, 1:2], -1.0),
        waits=[(sV, sV.v)], inc=sV)
    v_pmm = P.q("t", lambda e: e.matmul(pr_a[:, 0:128], rowstg[0:1, 0:128],
                                        rowstg[0:1, 0:128], start=True, stop=True),
                waits=[(sV, v_rrow)] + bw("pa"), inc=sT)
    last_row_use = (sT, v_pmm)
    v_gev = P.q("v", lambda e: e.tensor_copy(maskd[:, :], pr_a[:, 0:128]),
                waits=[(sT, v_pmm)], inc=sV)
    bankV["pa"] = v_gev
    P.q("v", lambda e: e.tensor_scalar(glc[:, 2:3], glc[:, 1:2], 0.5, 0.5, Alu.mult, Alu.add),
        waits=[(sV, v_glc)], inc=sV)
    P.q("v", lambda e: e.tensor_scalar(glc[:, 3:4], glc[:, 1:2], -0.5, 0.5, Alu.mult, Alu.add),
        waits=[(sV, sV.v)], inc=sV)
    v_fac = P.q("v", lambda e: e.tensor_scalar(maskd[:, :], maskd[:, :], glc[:, 3:4],
                                               glc[:, 2:3], Alu.mult, Alu.add),
                waits=[(sV, sV.v), (sV, v_gev)], inc=sV)
    v_dexp = P.q("s", lambda e: e.activation(decayD[:, :], gscr[:, :], AF.Exp),
                 waits=[(sV, v_fac)], inc=sS)
    v_dfac = P.q("v", lambda e: e.tensor_mul(decayD[:, :], decayD[:, :], maskd[:, :]),
                 waits=[(sS, v_dexp)], inc=sV)
    vg_decay = P.q("g", lambda e: e.affine_select(out=decayD[:, :], in_=decayD[:, :],
                                                  pattern=[[-1, 128]], compare_op=Alu.is_ge,
                                                  fill=0.0, base=0, channel_multiplier=1),
                   waits=[(sV, v_dfac)], inc=sG)


    row_state = {"idx": 0}
    row_bufs = [rowstg, rowstg2]
    row_uses = [last_row_use, (sV, v_dfac)]

    def row_load(src_ap):
        bi = row_state["idx"] % 2
        row_state["idx"] += 1
        c, v = row_uses[bi]
        vl = dmaw("s", dC, row_bufs[bi][0:1, 0:src_ap.shape[-1]], src_ap,
                  waits=[(c, v), (dC, dC.v)])
        row_state["cur"] = bi
        return vl

    def col_block(dst, base_col, v_row, scale=None):
        bi = row_state["cur"]
        for j in range(4):
            P.q("t", lambda e, j_=j, b_=bi: e.matmul(
                pr_b[:, base_col + j_:base_col + j_ + 1],
                row_bufs[b_][0:1, j_ * 128:(j_ + 1) * 128], onesf[0:1, 0:1],
                start=True, stop=True),
                waits=[(dC, v_row), (sG, vg_ones)] + bw("pb"),
                inc=(sT if j == 3 else None))
        v_mm = sT.v
        row_uses[bi] = (sT, v_mm)
        if scale is None:
            v_ev = P.q("v", lambda e: e.tensor_copy(dst[:, base_col:base_col + 4],
                                                    pr_b[:, base_col:base_col + 4]),
                       waits=[(sT, v_mm)], inc=sV)
        else:
            v_ev = P.q("v", lambda e, s_=scale: e.tensor_scalar(
                dst[:, base_col:base_col + 4], pr_b[:, base_col:base_col + 4],
                s_, None, Alu.mult),
                waits=[(sT, v_mm)], inc=sV)
        bankV["pb"] = v_ev
        return v_ev

    for half in range(2):
        v_r = row_load(bqkv_d[half * 512:(half + 1) * 512][None, :])
        col_block(bq32c, half * 4, v_r, scale=1.0 / 32.0)
    for half in range(2):
        v_r = row_load(bqkv_d[1024 + half * 512:1024 + (half + 1) * 512][None, :])
        col_block(bkc, half * 4, v_r)
    for half in range(2):
        v_r = row_load(bqkv_d[2048 + half * 512:2048 + (half + 1) * 512][None, :])
        v_c = P.q("v", lambda e, h=half, b_=row_state["cur"]: e.tensor_copy(
            rowbf[0:1, h * 512:(h + 1) * 512], row_bufs[b_][0:1, :]),
            waits=[(dC, v_r)], inc=sV)
        row_uses[row_state["cur"]] = (sV, v_c)
    for half in range(2):
        v_r = row_load(bproj_d[half * 512:(half + 1) * 512][None, :])
        v_c = P.q("v", lambda e, h=half, b_=row_state["cur"]: e.tensor_copy(
            rowbf[0:1, 1024 + h * 512:1024 + (h + 1) * 512], row_bufs[b_][0:1, :]),
            waits=[(dC, v_r)], inc=sV)
        row_uses[row_state["cur"]] = (sV, v_c)
    for quarter in range(4):
        v_r = row_load(b1_d.rearrange("e h -> (e h)")[quarter * 512:(quarter + 1) * 512][None, :])
        col_block(b1c, quarter * 4, v_r)
    v_r = row_load(wgb_d[:][None, :])
    v_mm = P.q("t", lambda e, b_=row_state["cur"]: e.matmul(
        pr_b[:, 16:24], onesf[0:1, :], row_bufs[b_][0:1, 0:8],
        start=True, stop=True),
               waits=[(dC, v_r)] + bw("pb"), inc=sT)
    row_uses[row_state["cur"]] = (sT, v_mm)
    v_ev = P.q("v", lambda e: e.tensor_copy(wgbb[:, :], pr_b[:, 16:24]),
               waits=[(sT, v_mm)], inc=sV)
    bankV["pb"] = v_ev
    v_wg = dmaw("s", dC, wgstg[:, :].rearrange("p (t e) -> p t e", e=8),
                wg_d.rearrange("(t p) e -> p t e", p=128),
                waits=[(dC, dC.v)])
    v_consts = P.q("v", lambda e: e.tensor_copy(wgbf[:, :], wgstg[:, :]),
                   waits=[(dC, v_wg)], inc=sV)

    x_re = x_d.rearrange("(t p) d -> p t d", p=128)
    t1_v = t1f[:, :].rearrange("p (t d) -> p t d", d=D)
    xT_v = xT[:, :].rearrange("p (dt s) -> p dt s", s=SEQ)
    wp_v = wps[:, :].rearrange("p (t f) -> p t f", f=1024)
    v_xc = []
    for c in range(NCH):
        v_xc.append(dmaw("y", dX, t1_v[:, 4 * c:4 * c + 4, :],
                         x_re[:, 4 * c:4 * c + 4, :], waits=[(dX, dX.v)]))

    v_xTc = [0] * NCH
    for tt in range(NT):
        for grp in range(2):
            key, bank = (("pa", pr_a) if (tt * 2 + grp) % 2 == 0 else ("pb", pr_b))
            for k in range(4):
                ds = grp * 4 + k
                P.q("t", lambda e, b=bank, k_=k, t_=tt, d_=ds: e.transpose(
                    b[:, k_ * 128:(k_ + 1) * 128],
                    t1f[:, t_ * D + d_ * 128: t_ * D + (d_ + 1) * 128],
                    identf[:, :]),
                    waits=[(dX, v_xc[tt // 4]), (sG, vg_decay)] + bw(key),
                    inc=(sT if k == 3 else None))
            v_tp = sT.v
            v_ev = P.q("v", lambda e, b=bank, g_=grp, t_=tt: e.tensor_copy(
                xT_v[:, g_ * 4:(g_ + 1) * 4, t_ * 128:(t_ + 1) * 128],
                b[:, :].rearrange("p (k m) -> p k m", m=128)),
                waits=[(sT, v_tp)], inc=sV)
            bankV[key] = v_ev
        if tt % 4 == 3:
            v_xTc[tt // 4] = sV.v
    v_xT_done = sV.v

    stg_half = [row_uses[0]]
    w_re = wqkv_d.rearrange("(t p) f -> p t f", p=128)
    wproj_re = wproj_d.rearrange("(t p) f -> p t f", p=128)
    stg3 = stg[:, :].rearrange("p (t f) -> p t f", f=128)

    def load_w_piece(col0, j, extra_waits=(), cast_waits=()):
        c, v = stg_half[0]
        if col0 >= 10000:
            srcap = wproj_re[:, :, j * 128:(j + 1) * 128]
        else:
            srcap = w_re[:, :, col0 + j * 128:col0 + (j + 1) * 128]
        v_l = dmaw("y", dY, stg3[:, :, :], srcap,
                   waits=[(c, v)] + list(extra_waits))
        v_c = P.q("v", lambda e, j_=j: e.tensor_copy(
            wp_v[:, :, j_ * 128:(j_ + 1) * 128], stg3[:, :, :]),
            waits=[(dY, v_l)] + list(cast_waits), inc=sV)
        stg_half[0] = (sV, v_c)
        return v_c

    qT_v = qTs[:, :].rearrange("p (c ft s) -> p c ft s", ft=8, s=512)
    v_qmm_ft = []
    v_wq = [load_w_piece(0, 0, [last_row_use])]
    for ft in range(8):
        if ft < 7:
            v_wq.append(load_w_piece(0, ft + 1))
        for t in range(NDT):
            for c in range(NCH):
                P.q("t", lambda e, f_=ft, t_=t, c_=c: e.matmul(
                    ps_s[:, c_ * 512:(c_ + 1) * 512],
                    wp_v[:, t_, f_ * 128:(f_ + 1) * 128],
                    xT_v[:, t_, c_ * 512:(c_ + 1) * 512],
                    start=(t_ == 0), stop=(t_ == NDT - 1)),
                    waits=[(sV, max(v_wq[ft], v_xTc[c]))] + bw("s%d" % c),
                    inc=(sT if (t == NDT - 1 and c == NCH - 1) else None))
        v_mm = sT.v
        v_qmm_ft.append(v_mm)
        for c in range(NCH):
            v_e = P.q("s", lambda e, f_=ft, c_=c: e.activation(
                qT_v[:, c_, f_, :], ps_s[:, c_ * 512:(c_ + 1) * 512], AF.Identity,
                bias=bq32c[:, f_:f_ + 1], scale=1.0 / 32.0),
                waits=[(sT, v_mm), (sV, v_consts)], inc=sS)
            bankS["s%d" % c] = v_e
    v_qT_done = sS.v

    kT_v = kTs[:, :].rearrange("p (ft s) -> p ft s", s=SEQ)
    v_kmm_ft = []
    v_wk = [load_w_piece(1024, 0, cast_waits=[(sT, v_qmm_ft[0])])]
    for ft in range(8):
        if ft < 7:
            v_wk.append(load_w_piece(1024, ft + 1, cast_waits=[(sT, v_qmm_ft[ft + 1])]))
        for t in range(NDT):
            for c in range(NCH):
                P.q("t", lambda e, f_=ft, t_=t, c_=c: e.matmul(
                    ps_s[:, c_ * 512:(c_ + 1) * 512],
                    wp_v[:, t_, f_ * 128:(f_ + 1) * 128],
                    xT_v[:, t_, c_ * 512:(c_ + 1) * 512],
                    start=(t_ == 0), stop=(t_ == NDT - 1)),
                    waits=[(sV, v_wk[ft])] + bw("s%d" % c),
                    inc=(sT if (t == NDT - 1 and c == NCH - 1) else None))
        v_mm = sT.v
        v_kmm_ft.append(v_mm)
        for c in range(NCH):
            v_e = P.q("s", lambda e, f_=ft, c_=c: e.activation(
                kT_v[:, f_, c_ * 512:(c_ + 1) * 512],
                ps_s[:, c_ * 512:(c_ + 1) * 512], AF.Identity,
                bias=bkc[:, f_:f_ + 1], scale=1.0),
                waits=[(sT, v_mm)], inc=sS)
            bankS["s%d" % c] = v_e
    v_kT_done = sS.v

    vN_v = vNs[:, :].rearrange("p (kt f) -> p kt f", f=1024)
    v_vmm_h = []
    v_wv = [load_w_piece(2048, 0, cast_waits=[(sT, v_kmm_ft[0])])]
    for j in range(1, 4):
        v_wv.append(load_w_piece(2048, j, cast_waits=[(sT, v_kmm_ft[j])]))
    for h in range(2):
        vh = v_wv[4 * h + 3]
        for tt in range(NT):
            if h == 0 and tt < 4:
                v_wv.append(load_w_piece(2048, 4 + tt, cast_waits=[(sT, v_kmm_ft[4 + tt])]))
            lo = (2 * tt + h) % 4
            for t in range(NDT):
                P.q("t", lambda e, t_=t, tt_=tt, h_=h, l_=lo, v_=vh: e.matmul(
                    ps_s[:, l_ * 512:(l_ + 1) * 512],
                    xT_v[:, t_, tt_ * 128:(tt_ + 1) * 128],
                    wp_v[:, t_, h_ * 512:(h_ + 1) * 512],
                    start=(t_ == 0), stop=False),
                    waits=[(sV, max(vh, v_xT_done, v_consts))] + bw("s%d" % lo),
                    inc=None)
            P.q("t", lambda e, h_=h, l_=lo: e.matmul(
                ps_s[:, l_ * 512:(l_ + 1) * 512],
                onesbf[0:1, :], rowbf[0:1, h_ * 512:(h_ + 1) * 512],
                start=False, stop=True), inc=sT)
            v_mm = sT.v
            v_ev = P.q("v", lambda e, tt_=tt, h_=h, l_=lo: e.tensor_copy(
                vN_v[:, tt_, h_ * 512:(h_ + 1) * 512],
                ps_s[:, l_ * 512:(l_ + 1) * 512]),
                waits=[(sT, v_mm)], inc=sV)
            bankV["s%d" % lo] = v_ev
        v_vmm_h.append(sT.v)
    v_v_done = sV.v
    v_vmm_done = sT.v

    v_wproj = 0
    for j in range(8):
        v_wproj = load_w_piece(10000, j, cast_waits=[(sT, v_vmm_h[j // 4])])

    bgV = []

    def drain_bg(n):
        k = 0
        while bgV and k < n:
            bgV.pop(0)()
            k += 1

    xlncm = xlnT[:, :].rearrange("p (c dt s) -> p c dt s", dt=8, s=512)
    wgbf_v = wgbf[:, :].rearrange("p (dt e) -> p dt e", e=8)
    v_xlncm = [0] * NCH
    v_masked_c = [0] * NCH

    def emit_gate(c):
        sl = slice(4 * c, 4 * c + 4)
        P.q("v", lambda e: e.tensor_scalar(mu1[:, sl], mu1[:, sl], 1.0 / D, None, Alu.mult),
            waits=[(sS, s_sums[c]), (sV, sV.v)], inc=sV)

        def op(f):
            P.q("v", f, waits=[(sV, sV.v)], inc=sV)

        op(lambda e: e.tensor_scalar(lntmp[:, sl], s21[:, sl], 1.0 / D, None, Alu.mult))
        op(lambda e: e.tensor_mul(lntmp2[:, sl], mu1[:, sl], mu1[:, sl]))
        op(lambda e: e.tensor_sub(lntmp[:, sl], lntmp[:, sl], lntmp2[:, sl]))
        op(lambda e: e.tensor_scalar_add(lntmp[:, sl], lntmp[:, sl], EPS_LN))
        op(lambda e: e.reciprocal(lntmp[:, sl], lntmp[:, sl]))
        op(lambda e: e.tensor_scalar(inv1[:, sl], lntmp[:, sl], 0.5, 0.5, Alu.mult, Alu.add))
        for _ in range(8):
            op(lambda e: e.reciprocal(lntmp2[:, sl], inv1[:, sl]))
            op(lambda e: e.tensor_mul(lntmp2[:, sl], lntmp[:, sl], lntmp2[:, sl]))
            op(lambda e: e.tensor_add(inv1[:, sl], inv1[:, sl], lntmp2[:, sl]))
            op(lambda e: e.tensor_scalar(inv1[:, sl], inv1[:, sl], 0.5, None, Alu.mult))
        v_fin = P.q("v", lambda e: e.scalar_tensor_tensor(
            nb1[:, sl], mu1[:, sl], -1.0, inv1[:, sl], Alu.mult, Alu.mult),
            waits=[(sV, sV.v)], inc=sV)
        for j in range(TPC):
            i = 4 * c + j
            xb = xb16 if j % 2 == 0 else xb16b
            s_app = P.q("s", lambda e, i_=i: e.activation(
                t1_v[:, i_, :], t1_v[:, i_, :], AF.Identity,
                bias=nb1[:, i_:i_ + 1], scale=inv1[:, i_:i_ + 1]),
                waits=[(sV, v_fin), (sS, sS.v)], inc=sS)
            v_cast = P.q("s", lambda e, i_=i, x_=xb: e.activation(x_[:, :], t1_v[:, i_, :], AF.Copy),
                         waits=[(sS, s_app), (sV, bankV.get("xb%d" % (j % 2), 0))], inc=sS)
            for k in range(8):
                P.q("t", lambda e, k_=k, x_=xb: e.transpose(
                    ps_ba[:, k_ * 128:(k_ + 1) * 128],
                    x_[:, k_ * 128:(k_ + 1) * 128],
                    identbf[:, :]),
                    waits=[(sS, v_cast)] + bw("ba"),
                    inc=(sT if k == 7 else None))
            v_tp = sT.v
            v_ev = P.q("v", lambda e, c_=c, j_=j: e.tensor_copy(
                xlncm[:, c_, 0:8, j_ * 128:(j_ + 1) * 128],
                ps_ba[:, :].rearrange("p (k m) -> p k m", m=128)),
                waits=[(sT, v_tp)], inc=sV)
            bankV["ba"] = v_ev
            bankV["xb%d" % (j % 2)] = v_ev
            v_xlncm_t = v_ev
            key, bank = (("pa", pr_a) if j % 2 == 0 else ("pb", pr_b))
            for dt in range(NDT):
                P.q("t", lambda e, b=bank, c_=c, j_=j, dt_=dt: e.matmul(
                    b[:, 0:8],
                    xlncm[:, c_, dt_, j_ * 128:(j_ + 1) * 128],
                    wgbf_v[:, dt_, :],
                    start=(dt_ == 0), stop=(dt_ == NDT - 1)),
                    waits=[(sV, max(v_xlncm_t, v_consts))] + bw(key),
                    inc=(sT if dt == NDT - 1 else None))
            v_mm = sT.v
            v_ev = P.q("v", lambda e, b=bank, i_=i: e.tensor_add(
                gall[:, i_ * 8:(i_ + 1) * 8], b[:, 0:8], wgbb[:, :]),
                waits=[(sT, v_mm)], inc=sV)
            bankV[key] = v_ev
            sl8 = slice(i * 8, (i + 1) * 8)

            def opb(f):
                bgV.append(lambda f=f: P.q("v", f, waits=[(sV, sV.v)], inc=sV))

            opb(lambda e, s_=sl8, i_=i: e.reduce_max(m1c[:, i_:i_ + 1], gall[:, s_], axis=AX.X))
            opb(lambda e, s_=sl8, i_=i: e.tensor_scalar(
                gscr[:, s_], gall[:, s_], m1c[:, i_:i_ + 1], None, Alu.subtract))

            def gexp(s_=sl8):
                v_gs = sV.v
                v_ge = P.q("s", lambda e, s2=s_: e.activation(gscr[:, s2], gscr[:, s2], AF.Exp),
                           waits=[(sV, v_gs)], inc=sS)
                P.q("v", lambda e, s2=s_: e.tensor_copy(gscr[:, s2], gscr[:, s2]),
                    waits=[(sS, v_ge)], inc=sV)

            bgV.append(gexp)
            opb(lambda e, s_=sl8, i_=i: e.reduce_sum(r8c[:, i_:i_ + 1], gscr[:, s_], axis=AX.X))
            opb(lambda e, i_=i: e.reciprocal(r8c[:, i_:i_ + 1], r8c[:, i_:i_ + 1]))
            opb(lambda e, s_=sl8, i_=i: e.tensor_scalar(
                gscr[:, s_], gscr[:, s_], r8c[:, i_:i_ + 1], None, Alu.mult))
            opb(lambda e, s_=sl8, i_=i: e.tensor_scalar(
                maskd[:, s_], gall[:, s_], m1c[:, i_:i_ + 1], -1e30, Alu.is_equal, Alu.mult))
            opb(lambda e, s_=sl8: e.tensor_add(maskd[:, s_], maskd[:, s_], gall[:, s_]))
            opb(lambda e, s_=sl8, i_=i: e.reduce_max(m2c[:, i_:i_ + 1], maskd[:, s_], axis=AX.X))
            opb(lambda e, s_=sl8, i_=i: e.tensor_scalar(
                maskd[:, s_], gall[:, s_], m2c[:, i_:i_ + 1], None, Alu.is_ge))

            def lastm(s_=sl8, c_=c):
                v_masked_c[c_] = P.q("v", lambda e, s2=s_: e.tensor_mul(
                    maskd[:, s2], maskd[:, s2], gscr[:, s2]),
                    waits=[(sV, sV.v)], inc=sV)

            bgV.append(lastm)

        v_xlncm[c] = sV.v

    s_sums = [0] * NCH
    pT_v = pT[:, :].rearrange("p (kt s) -> p kt s", s=512)
    v_exp = [0] * NT
    v_exp_p = [[0] * 4 for _ in range(NT)]
    v_rall = [0] * NT
    v_ptev = [0] * NT
    last_exp = 0

    def emit_ptp(i, pb):
        j = i % TPC
        for grp in range(2):
            key, bank = (("ba", ps_ba) if grp == 0 else ("bb", ps_bb))
            for k in range(8):
                kt = grp * 8 + k
                P.q("t", lambda e, b=bank, k_=k, kt_=kt, pb_=pb: e.transpose(
                    b[:, k_ * 128:(k_ + 1) * 128],
                    pnat[:, pb_ * 2048 + kt_ * 128: pb_ * 2048 + (kt_ + 1) * 128],
                    identbf[:, :]),
                    waits=[(sS, v_exp_p[i][2 * grp + 1])] + bw(key),
                    inc=(sT if k == 7 else None))
            v_tp = sT.v
            v_ev = P.q("s", lambda e, b=bank, g_=grp, j_=j: e.activation(
                pT_v[:, g_ * 8:(g_ + 1) * 8, j_ * 128:(j_ + 1) * 128],
                b[:, :].rearrange("p (k m) -> p k m", m=128), AF.Copy),
                waits=[(sT, v_tp)], inc=sS)
            bankS[key] = v_ev
        v_ptev[i] = sS.v
        drain_bg(24)

    for c in range(NCH):
        pend = None
        for j in range(TPC):
            i = c * TPC + j
            kc_d, off_d = divmod(i * 128, 512)
            for kc in range(4):
                for ft in range(8):
                    P.q("t", lambda e, kc_=kc, f_=ft, c_=c, j_=j: e.matmul(
                        ps_s[:, kc_ * 512:(kc_ + 1) * 512],
                        qT_v[:, c_, f_, j_ * 128:(j_ + 1) * 128],
                        kT_v[:, f_, kc_ * 512:(kc_ + 1) * 512],
                        start=(f_ == 0), stop=(f_ == 7)),
                        waits=[(sS, max(v_qT_done, v_kT_done))] + bw("s%d" % kc),
                        inc=(sT if ft == 7 else None))
                v_qmm = sT.v
                if kc == kc_d:
                    P.q("v", lambda e, kc_=kc_d, o_=off_d: e.tensor_mul(
                        ps_s[:, kc_ * 512 + o_:kc_ * 512 + o_ + 128],
                        ps_s[:, kc_ * 512 + o_:kc_ * 512 + o_ + 128],
                        decayD[:, :]),
                        waits=[(sT, v_qmm), (sG, vg_decay)], inc=sV)
                P.q("v", lambda e, kc_=kc: e.reduce_max(
                    qmx[:, kc_:kc_ + 1], ps_s[:, kc_ * 512:(kc_ + 1) * 512], axis=AX.X),
                    waits=[(sT, v_qmm), (sV, sV.v)], inc=sV)
            v_smm = sT.v
            v_rmax = P.q("v", lambda e: e.reduce_max(mcol[:, :], qmx[:, :],
                                                     axis=AX.X, negate=True),
                         waits=[(sV, sV.v)], inc=sV)
            pb = j % 2
            rb = (i % 2) * 4
            for kc in range(4):
                vp = P.q("s", lambda e, pb_=pb, kc_=kc, b_=rb: e.activation(
                    pnat[:, pb_ * 2048 + kc_ * 512:pb_ * 2048 + (kc_ + 1) * 512],
                    ps_s[:, kc_ * 512:(kc_ + 1) * 512], AF.Exp,
                    bias=mcol[:, :], scale=1.0,
                    accum_out=(None if kc_ == 0 else rq[:, b_ + kc_:b_ + kc_ + 1])),
                    waits=[(sV, v_rmax)], inc=sS)
                bankS["s%d" % kc] = vp
                v_exp_p[i][kc] = vp
            v_exp[i] = v_exp_p[i][3]
            last_exp = v_exp[i]
            P.q("v", lambda e, b_=rb, pb_=pb: e.reduce_sum(
                rq[:, b_:b_ + 1], pnat[:, pb_ * 2048:pb_ * 2048 + 512], axis=AX.X),
                waits=[(sS, v_exp_p[i][0]), (sV, sV.v)], inc=sV)
            P.q("v", lambda e, b_=rb: e.tensor_add(
                rq[:, b_:b_ + 1], rq[:, b_:b_ + 1], rq[:, b_ + 1:b_ + 2]),
                waits=[(sS, v_exp_p[i][1]), (sV, sV.v)], inc=sV)
            P.q("v", lambda e, b_=rb: e.tensor_add(
                rq[:, b_ + 2:b_ + 3], rq[:, b_ + 2:b_ + 3], rq[:, b_ + 3:b_ + 4]),
                waits=[(sS, v_exp_p[i][3])], inc=sV)
            v_rall[i] = P.q("v", lambda e, i_=i, b_=rb: e.tensor_add(
                rall[:, i_:i_ + 1], rq[:, b_:b_ + 1], rq[:, b_ + 2:b_ + 3]),
                waits=[(sV, sV.v)], inc=sV)
            P.q("v", lambda e, i_=i: e.reciprocal(invr[:, i_:i_ + 1], rall[:, i_:i_ + 1]),
                waits=[(sV, v_rall[i])], inc=sV)
            if pend is not None:
                emit_ptp(*pend)
            pend = (i, pb)
        emit_ptp(*pend)
        v_pt_all = v_ptev[c * TPC + TPC - 1]

        for ds in range(8):
            lo = ds % 2
            for kt in range(NT):
                P.q("t", lambda e, ds_=ds, kt_=kt, l_=lo: e.matmul(
                    ps_s[:, l_ * 512:(l_ + 1) * 512],
                    vN_v[:, kt_, ds_ * 128:(ds_ + 1) * 128],
                    pT_v[:, kt_, :],
                    start=(kt_ == 0), stop=(kt_ == NT - 1)),
                    waits=[(sS, max(last_exp, v_pt_all)), (sV, v_v_done)] + bw("s%d" % lo),
                    inc=(sT if kt == NT - 1 else None))
            v_mm = sT.v
            v_ev = P.q("v", lambda e, ds_=ds, c_=c, l_=lo: e.tensor_copy(
                aoT[c_][:, :].rearrange("p (ds s) -> p ds s", s=512)[:, ds_, :],
                ps_s[:, l_ * 512:(l_ + 1) * 512]),
                waits=[(sT, v_mm)], inc=sV)
            bankV["s%d" % lo] = v_ev
        v_ao_done = sV.v
        v_pv_last = sT.v

        for j in range(TPC):
            P.q("t", lambda e, i_=c * TPC + j, j_=j: e.matmul(
                pr_a[0:1, j_ * 128:(j_ + 1) * 128], rall[:, i_:i_ + 1], identf[:, :],
                start=True, stop=True),
                waits=[(sV, v_rall[c * TPC + j])] + bw("pa"),
                inc=(sT if j == TPC - 1 else None))
        v_rr = sT.v
        v_rrbf = P.q("v", lambda e: e.tensor_copy(rowbf[0:1, 0:512], pr_a[0:1, 0:512]),
                     waits=[(sT, v_rr)], inc=sV)
        bankV["pa"] = v_rrbf
        for j in range(TPC):
            i = c * TPC + j
            for h in range(2):
                key, bank = (("pa", pr_a) if h == 0 else ("pb", pr_b))
                for ds in range(8):
                    P.q("t", lambda e, b=bank, ds_=ds, c_=c, j_=j, h_=h: e.matmul(
                        b[:, :],
                        aoT[c_][:, :].rearrange("p (ds s) -> p ds s", s=512)[:, ds_, j_ * 128:(j_ + 1) * 128],
                        wp_v[:, ds_, h_ * 512:(h_ + 1) * 512],
                        start=(ds_ == 0), stop=False),
                        waits=[(sV, max(v_ao_done, v_wproj, v_rrbf))] + bw(key),
                        inc=None)
                P.q("t", lambda e, b=bank, h_=h, j_=j: e.matmul(
                    b[:, :], rowbf[0:1, j_ * 128:(j_ + 1) * 128],
                    rowbf[0:1, 1024 + h_ * 512:1024 + (h_ + 1) * 512],
                    start=False, stop=True), inc=sT)
                v_mm = sT.v
                v_ev = P.q("v", lambda e, b=bank, i_=i, h_=h: e.scalar_tensor_tensor(
                    t1_v[:, i_, h_ * 512:(h_ + 1) * 512], b[:, :], invr[:, i_:i_ + 1],
                    t1_v[:, i_, h_ * 512:(h_ + 1) * 512], Alu.mult, Alu.add),
                    waits=[(sT, v_mm), (dX, v_xc[c])], inc=sV)
                bankV[key] = v_ev
            v_sum_s = P.q("s", lambda e, i_=i: e.activation(
                junk1024[:, :], t1_v[:, i_, :], AF.Identity, bias=0.0, scale=1.0,
                accum_out=mu1[:, i_:i_ + 1]),
                waits=[(sV, sV.v), (sS, sS.v)], inc=sS)
            s_sums[c] = P.q("s", lambda e, i_=i: e.activation(
                junk1024[:, :], t1_v[:, i_, :], AF.Square,
                accum_out=s21[:, i_:i_ + 1]),
                waits=[(sS, v_sum_s)], inc=sS)
        if c >= 1:
            drain_bg(1000000)
            emit_gate(c - 1)
    drain_bg(1000000)
    emit_gate(NCH - 1)
    drain_bg(1000000)
    v_ln1_sums = sV.v
    v_masked = v_masked_c[NCH - 1]

    if stage == 1:
        v_out = 0
        for c in range(NCH):
            v_out = dmaw("s", dS, xln_d.rearrange("(t p) d -> p t d", p=128)[:, 4 * c:4 * c + 4, :],
                         t1_v[:, 4 * c:4 * c + 4, :], waits=[(sV, v_masked)])
        return finalize(nc, P, [(dS, v_out)])

    v_ccw = dmaw("g", dG, cc_in.ap(), maskd[:, :], waits=[(sV, v_masked)])
    v_cc = P.q("g", lambda e: e.collective_compute(
        "AllReduce", Alu.add, replica_groups=[list(range(8))],
        ins=[cc_in.ap().opt()], outs=[cc_out.ap().opt()]),
        waits=[(dG, v_ccw)], inc=sC)
    w1_re = w1_d.rearrange("e (t p) h -> p e t h", p=128)
    w1bf_v = w1bf[:, :].rearrange("p (e t h) -> p e t h", t=8, h=256)
    stg_ab = [stg_half[0], (sS, s_sums[NCH - 1])]
    bufs = [stg[:, :], stgB[:, :]]
    pidx = 0
    w1cast = {}
    for ei in range(E):
        for hh in range(2):
            ab = pidx % 2
            dma_ctr = dY if ab == 0 else dB
            c, v = stg_ab[ab]
            v_l = dmaw("y", dma_ctr, bufs[ab].rearrange("p (t k) -> p t k", k=256),
                       w1_re[:, ei, hh * 4:(hh + 1) * 4, :],
                       waits=[(c, v)])
            v_c = P.q("v", lambda e, e_=ei, hh_=hh, a_=ab: e.tensor_copy(
                w1bf_v[:, e_, hh_ * 4:(hh_ + 1) * 4, :].rearrange("p t k -> p (t k)"),
                bufs[a_]),
                waits=[(dma_ctr, v_l), (sS, last_exp)], inc=sV)
            stg_ab[ab] = (sV, v_c)
            w1cast[ei] = v_c
            pidx += 1
    v_w1_done = sV.v
    w2_re = w2_d.rearrange("e (t p) f -> p e t f", p=128)
    w2bf_v = w2bf[:, :].rearrange("p (e t f) -> p e t f", t=2, f=1024)
    for ei in range(E):
        for t in range(2):
            ab = pidx % 2
            dma_ctr = dY if ab == 0 else dB
            c, v = stg_ab[ab]
            v_l = dmaw("y", dma_ctr, bufs[ab], w2_re[:, ei, t, :],
                       waits=[(c, v)])
            v_c = P.q("v", lambda e, e_=ei, t_=t, a_=ab: e.tensor_copy(
                w2bf_v[:, e_, t_, :], bufs[a_]),
                waits=[(dma_ctr, v_l), (sT, v_pv_last)], inc=sV)
            stg_ab[ab] = (sV, v_c)
            pidx += 1
    v_w2_done = sV.v

    v_ccr = dmaw("g", dG, denom[:, :], cc_out.ap(), waits=[(sC, v_cc)])
    P.q("v", lambda e: e.tensor_scalar_add(denom[:, :], denom[:, :], EPS_DEN),
        waits=[(dG, v_ccr)], inc=sV)
    P.q("v", lambda e: e.reciprocal(denom[:, :], denom[:, :]),
        waits=[(sV, sV.v)], inc=sV)
    v_route = P.q("v", lambda e: e.tensor_mul(gall[:, :], maskd[:, :], denom[:, :]),
                  waits=[(sV, sV.v)], inc=sV)

    selba_v = selba[:, :].rearrange("p (e s) -> p e s", s=512)
    hbuf_v = hbuf[:, :].rearrange("p (e t s) -> p e t s", t=2, s=512)
    w2state = {"prev_mm": 0}
    gelus = {}

    def emit_w1(cc):
        for ei in range(E):
            for ht in range(2):
                lo = (ei * 2 + ht) % 4
                for dt in range(NDT):
                    P.q("t", lambda e, l_=lo, e_=ei, t_=dt, h_=ht, c_=cc: e.matmul(
                        ps_s[:, l_ * 512:(l_ + 1) * 512],
                        w1bf_v[:, e_, t_, h_ * 128:(h_ + 1) * 128],
                        xlncm[:, c_, t_, :],
                        start=(t_ == 0), stop=(t_ == NDT - 1)),
                        waits=[(sV, w1cast[ei]),
                               (sV, v_xlncm[cc])] + bw("s%d" % lo),
                        inc=(sT if dt == NDT - 1 else None))
                v_mm = sT.v
                v_gelu = P.q("s", lambda e, l_=lo, e_=ei, h_=ht: e.activation(
                    hbuf_v[:, e_, h_, :], ps_s[:, l_ * 512:(l_ + 1) * 512],
                    AF.Gelu, bias=b1c[:, e_ * 2 + h_:e_ * 2 + h_ + 1]),
                    waits=[(sT, v_mm), (sT, w2state["prev_mm"]),
                           (sV, bankV.get("hb%d%d" % (ei, ht), 0))], inc=sS)
                bankS["s%d" % lo] = v_gelu
                gelus[(ei, ht)] = v_gelu

    emit_w1(0)

    for grp in range(4):
        key, bank = (("pa", pr_a) if grp % 2 == 0 else ("pb", pr_b))
        for k in range(4):
            i = grp * 4 + k
            P.q("t", lambda e, b=bank, k_=k, i_=i: e.transpose(
                b[0:8, k_ * 128:(k_ + 1) * 128],
                gall[:, i_ * 8:(i_ + 1) * 8],
                identf[:, :]),
                waits=[(sV, v_route)] + bw(key),
                inc=(sT if k == 3 else None))
        v_tp = sT.v
        v_ev = P.q("v", lambda e, b=bank, g_=grp: e.tensor_copy(
            routeT[0:8, :].rearrange("p (t m) -> p t m", m=128)[:, g_ * 4:(g_ + 1) * 4, :],
            b[0:8, :].rearrange("p (k m) -> p k m", m=128)),
            waits=[(sT, v_tp)], inc=sV)
        bankV[key] = v_ev
    v_routeT = sV.v

    Cc = lambda k: st8[0:8, k:k + 1]

    def vq(build, waits=()):
        return P.q("v", build, waits=waits, inc=sV)

    vq(lambda e: e.reduce_max(Cc(8), routeT[0:8, :], axis=AX.X), waits=[(sV, v_routeT)])
    vq(lambda e: e.tensor_scalar(seleq[0:8, :], routeT[0:8, :], 0.0, 1e30, Alu.is_equal, Alu.mult),
       waits=[(sV, sV.v)])
    vq(lambda e: e.tensor_add(seleq[0:8, :], seleq[0:8, :], routeT[0:8, :]), waits=[(sV, sV.v)])
    vq(lambda e: e.tensor_reduce(Cc(9), seleq[0:8, :], AX.X, Alu.min), waits=[(sV, sV.v)])
    vq(lambda e: e.tensor_scalar(selgt[0:8, :], routeT[0:8, :], 0.0, None, Alu.is_gt, Alu.add,
                                 accum_out=Cc(4)), waits=[(sV, sV.v)])
    vq(lambda e: e.tensor_copy(Cc(17), Cc(9)), waits=[(sV, sV.v)])
    vq(lambda e: e.tensor_scalar(Cc(18), Cc(8), 1.001, 1e-38, Alu.mult, Alu.max), waits=[(sV, sV.v)])

    def SQ(dst, srccol, polish=True):
        sq = P.q("s", lambda e: e.activation(Cc(dst), Cc(srccol), AF.Sqrt),
                 waits=[(sV, sV.v)], inc=sS)
        if not polish:
            vq(lambda e: e.tensor_copy(Cc(dst), Cc(dst)), waits=[(sS, sq)])
            return
        vq(lambda e: e.reciprocal(Cc(11), Cc(dst)), waits=[(sS, sq)])
        vq(lambda e: e.tensor_mul(Cc(12), Cc(srccol), Cc(11)), waits=[(sV, sV.v)])
        vq(lambda e: e.tensor_add(Cc(dst), Cc(dst), Cc(12)), waits=[(sV, sV.v)])
        vq(lambda e: e.tensor_scalar(Cc(dst), Cc(dst), 0.5, None, Alu.mult), waits=[(sV, sV.v)])

    SQ(0, 17)
    SQ(1, 18)
    vq(lambda e: e.tensor_mul(Cc(10), Cc(0), Cc(1)), waits=[(sV, sV.v)])
    SQ(2, 10)
    for it in range(NIT):
        vq(lambda e: e.tensor_mul(Cc(16), Cc(2), Cc(2)), waits=[(sV, sV.v)])
        vq(lambda e: e.tensor_scalar(selgt[0:8, :], routeT[0:8, :], Cc(16), None,
                                     Alu.is_ge, Alu.add, accum_out=Cc(3)), waits=[(sV, sV.v)])
        vq(lambda e: e.tensor_scalar(stm[0:8, 0:1], Cc(3), float(KCAP), None, Alu.is_ge), waits=[(sV, sV.v)])
        vq(lambda e: e.copy_predicated(Cc(0), stm[0:8, 0:1], Cc(2)), waits=[(sV, sV.v)])
        vq(lambda e: e.tensor_scalar(stm[0:8, 1:2], Cc(3), float(KCAP), None, Alu.is_lt), waits=[(sV, sV.v)])
        vq(lambda e: e.copy_predicated(Cc(1), stm[0:8, 1:2], Cc(2)), waits=[(sV, sV.v)])
        vq(lambda e: e.tensor_mul(Cc(10), Cc(0), Cc(1)), waits=[(sV, sV.v)])
        SQ(2, 10, polish=(it >= NIT - 8))
        if it >= 4:
            P.q("t", lambda e: e.matmul(ps_s[:, 0:128], identbf[:, :], identbf[:, :],
                                        start=True, stop=True),
                waits=[(sV, sV.v), (sS, gelus[(E - 1, 1)])], inc=None)
    vq(lambda e: e.tensor_mul(Cc(0), Cc(0), Cc(0)), waits=[(sV, sV.v)])
    vq(lambda e: e.tensor_mul(Cc(1), Cc(1), Cc(1)), waits=[(sV, sV.v)])
    for _sn in range(NSNAP):
        P.q("t", lambda e: e.matmul(ps_s[:, 0:128], identbf[:, :], identbf[:, :],
                                    start=True, stop=True),
            waits=[(sV, sV.v)], inc=None)
        vq(lambda e: e.scalar_tensor_tensor(seleq[0:8, :], routeT[0:8, :], Cc(1),
                                            routeT[0:8, :], Alu.is_lt, Alu.mult), waits=[(sV, sV.v)])
        vq(lambda e: e.reduce_max(Cc(12), seleq[0:8, :], axis=AX.X), waits=[(sV, sV.v)])
        vq(lambda e: e.tensor_scalar(selgt[0:8, :], routeT[0:8, :], Cc(12), None,
                                     Alu.is_ge, Alu.add, accum_out=Cc(3)), waits=[(sV, sV.v)])
        vq(lambda e: e.tensor_scalar(stm[0:8, 0:1], Cc(3), float(KCAP), None, Alu.is_lt), waits=[(sV, sV.v)])
        vq(lambda e: e.copy_predicated(Cc(1), stm[0:8, 0:1], Cc(12)), waits=[(sV, sV.v)])
        vq(lambda e: e.tensor_scalar(stm[0:8, 1:2], Cc(3), float(KCAP), None, Alu.is_ge), waits=[(sV, sV.v)])
        vq(lambda e: e.copy_predicated(Cc(0), stm[0:8, 1:2], Cc(12)), waits=[(sV, sV.v)])
    vq(lambda e: e.tensor_scalar(stm[0:8, 0:1], Cc(4), float(KCAP), None, Alu.is_ge), waits=[(sV, sV.v)])
    vq(lambda e: e.memset(Cc(13), 0.0), waits=[(sV, sV.v)])
    vq(lambda e: e.copy_predicated(Cc(13), stm[0:8, 0:1], Cc(0)), waits=[(sV, sV.v)])
    vq(lambda e: e.tensor_scalar(selgt[0:8, :], routeT[0:8, :], Cc(13), None,
                                 Alu.is_gt, Alu.add, accum_out=Cc(14)), waits=[(sV, sV.v)])
    vq(lambda e: e.tensor_scalar(seleq[0:8, :], routeT[0:8, :], Cc(13), None, Alu.is_equal),
       waits=[(sV, sV.v)])
    vq(lambda e: e.memset(routeT[0:8, :], 0.0), waits=[(sV, sV.v)])
    vq(lambda e: e.tensor_tensor_scan(routeT[0:8, :], seleq[0:8, :], routeT[0:8, :],
                                      0.0, Alu.add, Alu.add), waits=[(sV, sV.v)])
    vq(lambda e: e.tensor_scalar(Cc(5), Cc(14), -1.0, float(KCAP), Alu.mult, Alu.add),
       waits=[(sV, sV.v)])
    vq(lambda e: e.tensor_scalar(routeT[0:8, :], routeT[0:8, :], Cc(5), None, Alu.is_le),
       waits=[(sV, sV.v)])
    vq(lambda e: e.tensor_mul(seleq[0:8, :], seleq[0:8, :], routeT[0:8, :]), waits=[(sV, sV.v)])
    v_self32 = vq(lambda e: e.tensor_add(selgt[0:8, :], selgt[0:8, :], seleq[0:8, :]),
                  waits=[(sV, sV.v)])

    if stage == 2:
        v_o1 = dmaw("s", dS, sel_d[:, :], selgt[0:8, :], waits=[(sV, v_self32)])
        v_out = v_o1
        for c in range(NCH):
            v_out = dmaw("s", dS, xln_d.rearrange("(t p) d -> p t d", p=128)[:, 4 * c:4 * c + 4, :],
                         t1_v[:, 4 * c:4 * c + 4, :], waits=[(sV, v_self32)])
        return finalize(nc, P, [(dS, v_out)])

    v_selbf = vq(lambda e: e.tensor_copy(selbf[0:8, :], selgt[0:8, :]), waits=[(sV, v_self32)])
    v_b2 = 0
    for half in range(2):
        v_l = dmaw("y", dC, b2stg[0:8, :], b2_d[:, half * 512:(half + 1) * 512],
                   waits=[(sV, v_selbf), (dC, dC.v), (sS, v_b2)])
        v_b2 = P.q("s", lambda e, h=half: e.activation(
            b2bf[0:8, h * 512:(h + 1) * 512], b2stg[0:8, :], AF.Copy),
            waits=[(dC, v_l)], inc=sS)
    v_esel0 = P.q("g", lambda e: e.memset(esel[0:8, :], 0.0),
                  waits=[(sV, v_selbf)], inc=sG)
    v_esel = P.q("g", lambda e: e.affine_select(
        out=esel[0:8, :], in_=esel[0:8, :],
        pattern=[[-1, 8], [0, 128]], compare_op=Alu.not_equal,
        fill=1.0, base=0, channel_multiplier=1),
        waits=[(sG, v_esel0)], inc=sG)

    v_ln2_sums = 0
    for cc in range(NCH):
        if cc > 0:
            emit_w1(cc)
        v_selba = 0
        for ei in range(E):
            key, bank = (("pa", pr_a) if ei % 2 == 0 else ("pb", pr_b))
            v_mm = P.q("t", lambda e, b=bank, e_=ei, c_=cc: e.matmul(
                b[:, :], esel[0:8, e_ * 128:(e_ + 1) * 128],
                selbf[0:8, c_ * 512:(c_ + 1) * 512],
                start=True, stop=True),
                waits=[(sV, v_selbf), (sG, v_esel)] + bw(key), inc=sT)
            v_ev = P.q("v", lambda e, b=bank, e_=ei: e.tensor_copy(
                selba_v[:, e_, :], b[:, :]),
                waits=[(sT, v_mm)], inc=sV)
            bankV[key] = v_ev
            v_selba = v_ev
        for ei in range(E):
            for ht in range(2):
                v_hmask = P.q("v", lambda e, e_=ei, h_=ht: e.tensor_mul(
                    hbuf_v[:, e_, h_, :], hbuf_v[:, e_, h_, :], selba_v[:, e_, :]),
                    waits=[(sS, gelus[(ei, ht)]), (sV, v_selba)], inc=sV)
                bankV["hb%d%d" % (ei, ht)] = v_hmask
        v_h_done = sV.v
        for jt in range(TPC):
            i = cc * TPC + jt
            for h in range(2):
                key, bank = (("pa", pr_a) if h == 0 else ("pb", pr_b))
                for ei in range(E):
                    for ht in range(2):
                        P.q("t", lambda e, b=bank, e_=ei, h_=h, ht_=ht, j_=jt: e.matmul(
                            b[:, :],
                            hbuf_v[:, e_, ht_, j_ * 128:(j_ + 1) * 128],
                            w2bf_v[:, e_, ht_, h_ * 512:(h_ + 1) * 512],
                            start=(e_ == 0 and ht_ == 0), stop=False),
                            waits=[(sV, max(v_h_done, v_w2_done)), (sS, v_b2)] + bw(key),
                            inc=None)
                P.q("t", lambda e, b=bank, i_=i, h_=h: e.matmul(
                    b[:, :], selbf[0:8, i_ * 128:(i_ + 1) * 128],
                    b2bf[0:8, h_ * 512:(h_ + 1) * 512],
                    start=False, stop=True), inc=sT)
                v_mm = sT.v
                w2state["prev_mm"] = v_mm
                v_ev = P.q("v", lambda e, b=bank, i_=i, h_=h: e.tensor_add(
                    t1_v[:, i_, h_ * 512:(h_ + 1) * 512], b[:, :],
                    t1_v[:, i_, h_ * 512:(h_ + 1) * 512]),
                    waits=[(sT, v_mm)], inc=sV)
                bankV[key] = v_ev
            P.q("v", lambda e, i_=i: e.reduce_sum(mu2[:, i_:i_ + 1], t1_v[:, i_, :], axis=AX.X),
                waits=[(sV, sV.v)], inc=sV)
            P.q("v", lambda e, i_=i: e.scalar_tensor_tensor(
                junk1024[:, :], t1_v[:, i_, :], 1.0, t1_v[:, i_, :],
                Alu.mult, Alu.mult, accum_out=s22[:, i_:i_ + 1]),
                waits=[(sV, sV.v)], inc=sV)
    v_ln2_sums = sV.v

    P.q("v", lambda e: e.tensor_scalar(mu2[:, :], mu2[:, :], 1.0 / D, None, Alu.mult),
        waits=[(sV, v_ln2_sums)], inc=sV)
    P.q("v", lambda e: e.tensor_scalar(lntmp[:, :], s22[:, :], 1.0 / D, None, Alu.mult),
        waits=[(sV, sV.v)], inc=sV)
    P.q("v", lambda e: e.tensor_mul(lntmp2[:, :], mu2[:, :], mu2[:, :]),
        waits=[(sV, sV.v)], inc=sV)
    P.q("v", lambda e: e.tensor_sub(lntmp[:, :], lntmp[:, :], lntmp2[:, :]),
        waits=[(sV, sV.v)], inc=sV)
    P.q("v", lambda e: e.tensor_scalar_add(lntmp[:, :], lntmp[:, :], EPS_LN),
        waits=[(sV, sV.v)], inc=sV)
    v_rc2 = P.q("v", lambda e: e.reciprocal(lntmp2[:, :], lntmp[:, :]),
                waits=[(sV, sV.v)], inc=sV)
    v_inv2 = P.q("s", lambda e: e.activation(inv2[:, :], lntmp2[:, :], AF.Sqrt),
                 waits=[(sV, v_rc2)], inc=sS)
    v_final = 0
    for i in range(NT):
        v_final = P.q("v", lambda e, i_=i: e.tensor_scalar(
            t1_v[:, i_, :], t1_v[:, i_, :], mu2[:, i_:i_ + 1], inv2[:, i_:i_ + 1],
            Alu.subtract, Alu.mult),
            waits=[(sS, v_inv2)], inc=sV)
    y_re = y_d.rearrange("(t p) d -> p t d", p=128)
    v_out = 0
    for c in range(NCH):
        v_out = dmaw("s", dS, y_re[:, 4 * c:4 * c + 4, :], t1_v[:, 4 * c:4 * c + 4, :],
                     waits=[(sV, v_final)])
    return finalize(nc, P, [(dS, v_out)])


def finalize(nc, P, final_waits):
    def fw(eng):
        for c, val in final_waits:
            eng.wait_ge(c.h, val)
    P.ops["y"].append(fw)
    with nc.Block() as block:
        @block.sync
        def _(eng):
            for f in P.ops["y"]:
                f(eng)

        @block.tensor
        def _(eng):
            for f in P.ops["t"]:
                f(eng)

        @block.vector
        def _(eng):
            for f in P.ops["v"]:
                f(eng)

        @block.scalar
        def _(eng):
            for f in P.ops["s"]:
                f(eng)

        @block.gpsimd
        def _(eng):
            for f in P.ops["g"]:
                f(eng)
    return nc


def make_in_maps(L0_attn_x, L0_attn_Wqkv, L0_attn_bqkv, L0_attn_Wproj, L0_attn_bproj,
                 L0_attn_gamma, L0_ffn_w_gate_w, L0_ffn_w_gate_b, L0_ffn_W1, L0_ffn_b1,
                 L0_ffn_W2, L0_ffn_b2):
    x = np.ascontiguousarray(np.asarray(L0_attn_x, dtype=np.float32))
    com = {
        "wqkv": np.ascontiguousarray(np.asarray(L0_attn_Wqkv, dtype=np.float32)),
        "bqkv": np.ascontiguousarray(np.asarray(L0_attn_bqkv, dtype=np.float32)),
        "wproj": np.ascontiguousarray(np.asarray(L0_attn_Wproj, dtype=np.float32)),
        "bproj": np.ascontiguousarray(np.asarray(L0_attn_bproj, dtype=np.float32)),
        "gam": np.asarray(L0_attn_gamma, dtype=np.float32).reshape(1, 1).copy(),
        "wg": np.ascontiguousarray(np.asarray(L0_ffn_w_gate_w, dtype=np.float32)),
        "wgb": np.ascontiguousarray(np.asarray(L0_ffn_w_gate_b, dtype=np.float32)),
        "w1": np.ascontiguousarray(np.asarray(L0_ffn_W1, dtype=np.float32)),
        "b1": np.ascontiguousarray(np.asarray(L0_ffn_b1, dtype=np.float32)),
        "w2": np.ascontiguousarray(np.asarray(L0_ffn_W2, dtype=np.float32)),
        "b2": np.ascontiguousarray(np.asarray(L0_ffn_b2, dtype=np.float32)),
    }
    return [dict(com, x=np.ascontiguousarray(x[b])) for b in range(8)]


def kernel(**inputs):
    nc = build_nc(stage=3)
    in_maps = make_in_maps(**inputs)
    res = run_bass_kernel_spmd(nc, in_maps, core_ids=list(range(8)))
    return np.stack([res.results[b]["y"] for b in range(8)], axis=0)
